# version 14
# baseline (speedup 1.0000x reference)
import contextlib
import numpy as np
import ml_dtypes
import concourse.bass as bass
import concourse.mybir as mybir
from concourse.bass_utils import run_bass_kernel_spmd

F32 = mybir.dt.float32
BF16 = mybir.dt.bfloat16
F16 = mybir.dt.float16
AF = mybir.ActivationFunctionType
ALU = mybir.AluOpType
AX = mybir.AxisListType

MAGIC = float(np.float32(1.5 * 2**23))
QMAX = 127.0

B, T, C, I = 2, 16384, 1024, 2816
TOK = 4096
NT = TOK // 128
CCH = C // 128
ICH = I // 128
STAGE = "full"

ENGS = ("sync", "vector", "scalar", "tensor", "gpsimd")


class Bld:
    def __init__(self, nc):
        self.nc = nc
        self.prog = {e: [] for e in ENGS}
        self.es = contextlib.ExitStack()
        self.semh = {}
        self.semv = {}
        self.guard = {}

    def sem(self, name):
        self.semh[name] = self.es.enter_context(self.nc.semaphore(name))
        self.semv[name] = 0

    def sbuf(self, name, shape, dt, scope=None):
        scope = scope if scope is not None else self.es
        return scope.enter_context(self.nc.sbuf_tensor(name, shape, dt))

    def psum(self, name, shape, dt):
        return self.es.enter_context(self.nc.psum_tensor(name, shape, dt))

    def w(self, eng, sem, val=None):
        v = self.semv[sem] if val is None else val
        if v <= 0:
            return v
        h = self.semh[sem]
        self.prog[eng].append([lambda E: E.wait_ge(h, v), None, False])
        return v

    SELF = {"vector": "dv", "scalar": "ac", "gpsimd": "gp"}

    def _emit(self, eng, meth, a, kw, extra_incs):
        if eng in self.SELF and meth != "dma_start":
            s = self.SELF[eng]
            h = self.semh[s]
            v0 = self.semv[s]
            if v0 > 0:
                self.prog[eng].append(
                    [lambda E: E.wait_ge(h, v0), None, False])
            self.semv[s] = v0 + 1
            incs = [(h, 1)] + list(extra_incs)
            self.prog[eng].append(
                [lambda E: getattr(E, meth)(*a, **kw), incs, True])
            return self.semv[s]
        self.prog[eng].append(
            [lambda E: getattr(E, meth)(*a, **kw), list(extra_incs),
             meth != "dma_start"])
        return None

    def i(self, eng, meth, *a, **kw):
        self._emit(eng, meth, a, kw, [])

    def im(self, eng, meth, *a, sem=None, amt=1, **kw):
        if eng in self.SELF and meth != "dma_start":
            assert sem == self.SELF[eng], (eng, sem)
            return self._emit(eng, meth, a, kw, [])
        h = self.semh[sem]
        self.semv[sem] += amt
        v = self.semv[sem]
        self._emit(eng, meth, a, kw, [(h, amt)])
        return v

    def mark(self, eng, sem, amt=1):
        if eng in self.SELF:
            assert sem == self.SELF[eng], (eng, sem)
            return self.semv[sem]
        for ent in reversed(self.prog[eng]):
            if ent[2]:
                if not ent[1]:
                    h = self.semh[sem]
                    self.semv[sem] += amt
                    ent[1].append((h, amt))
                    return self.semv[sem]
                break
        return self.im(eng, "nop", sem=sem, amt=amt)

    DMA_SEMS = ("ds", "dg", "dx", "dw0", "dw1", "dm0", "dm1",
                "dc0", "dc1", "dcr", "dh0", "dh1", "dt")

    def dma(self, eng, out, in_, sem=None, **kw):
        if sem is None:
            sem = "dg" if eng == "gpsimd" else "ds"
        return self.im(eng, "dma_start", sem=sem, amt=16, out=out, in_=in_, **kw)

    def need(self, eng, key):
        tk = self.guard.get(key)
        if tk is not None:
            self.w(eng, tk[0], tk[1])

    def done(self, key, sem, val):
        self.guard[key] = (sem, val)

    def barrier(self):
        for s in self.DMA_SEMS:
            if s == "dg":
                self.w("gpsimd", s)
            else:
                self.w("sync", s)
        toks = {}
        for e in ENGS:
            if e == "gpsimd":
                self.i(e, "memset", self.barsc_g[0:1, 0:1], 0.0)
                toks[e] = ("gp", self.semv["gp"])
            elif e in self.SELF:
                toks[e] = (self.SELF[e], self.semv[self.SELF[e]])
            elif e == "tensor":
                toks[e] = ("btensor", self.mark(e, "btensor"))
            else:
                toks[e] = ("bsync", self.im(e, "nop", sem="bsync", amt=1))
        for e in ENGS:
            for e2 in ENGS:
                if e2 != e:
                    self.w(e, toks[e2][0], toks[e2][1])

    def finish(self, blk):
        for e in ENGS:
            ops = self.prog[e]

            def runner(E, ops=ops):
                for fn, incs, _ in ops:
                    ins = fn(E)
                    if incs:
                        for h, amt in incs:
                            ins = ins.then_inc(h, amt)
            
            getattr(blk, e)(runner)


def build_program(stage=None):
    stage = stage or STAGE
    nc = bass.Bass("TRN2", target_bir_lowering=False, debug=False)
    g = Bld(nc)

    x_in = nc.dram_tensor("x", [TOK, C], F32, kind="ExternalInput").ap()
    wg_in = nc.dram_tensor("w_gate", [I, C], F32, kind="ExternalInput").ap()
    wu_in = nc.dram_tensor("w_up", [I, C], F32, kind="ExternalInput").ap()
    wd_in = nc.dram_tensor("w_down", [C, I], F32, kind="ExternalInput").ap()
    wv_in = nc.dram_tensor("w_val", [C, C], F32, kind="ExternalInput").ap()
    wr_in = nc.dram_tensor("w_rec", [C, C], F32, kind="ExternalInput").ap()
    wo_in = nc.dram_tensor("w_out", [C, C], F32, kind="ExternalInput").ap()
    td_in = nc.dram_tensor("tdec", [CCH, 128], F32, kind="ExternalInput").ap()
    tf_in = nc.dram_tensor("tfir", [CCH, 128], F32, kind="ExternalInput").ap()
    clip_in = nc.dram_tensor("clips", [1, 2], F32, kind="ExternalInput").ap()
    eye_in = nc.dram_tensor("eye", [128, 128], F32, kind="ExternalInput").ap()
    oh_in = nc.dram_tensor("onehot", [4, 1], F32, kind="ExternalInput").ap()
    out_d = nc.dram_tensor("out", [TOK, C], F32, kind="ExternalOutput").ap()

    wgb = nc.dram_tensor("wgb", [ICH, 128, C], BF16, kind="Internal").ap()
    wub = nc.dram_tensor("wub", [ICH, 128, C], BF16, kind="Internal").ap()
    wdb = nc.dram_tensor("wdb", [ICH, 128, C], F16, kind="Internal").ap()
    wvb = nc.dram_tensor("wvb", [CCH, 128, C], BF16, kind="Internal").ap()
    wrb = nc.dram_tensor("wrb", [CCH, 128, C], BF16, kind="Internal").ap()
    wob = nc.dram_tensor("wob", [CCH, 128, C], BF16, kind="Internal").ap()
    hsp = nc.dram_tensor("hsp", [ICH, 128, TOK], F16, kind="Internal").ap()
    vsp = nc.dram_tensor("vsp", [CCH, 128, TOK], BF16, kind="Internal").ap()
    rsp = nc.dram_tensor("rsp", [CCH, 128, TOK], BF16, kind="Internal").ap()
    rdsp = nc.dram_tensor("rdsp", [CCH, 128, TOK], BF16, kind="Internal").ap()
    mx_i = nc.dram_tensor("mx_i", [1, 8], F32, kind="Internal").ap()
    mx_o = nc.dram_tensor("mx_o", [1, 8], F32, kind="Internal").ap()
    ag_i = nc.dram_tensor("ag_i", [8, 128], F32, kind="Internal").ap()
    ag_o = nc.dram_tensor("ag_o", [32, 128], F32, kind="Internal").ap()
    col_d = nc.dram_tensor("col_d", [1, 128], F32, kind="Internal").ap()
    sc_d = nc.dram_tensor("sc_d", [1, 4], F32, kind="Internal").ap()

    es = g.es
    with es:
        for s in Bld.DMA_SEMS + ("pe", "dv", "ac", "gp"):
            g.sem(s)
        for e in ENGS:
            g.sem("b" + e)

        xres = g.sbuf("xres", [128, NT * C], F32)
        eye32 = g.sbuf("eye32", [128, 128], F32)
        eyeb = g.sbuf("eyeb", [128, 128], BF16)
        eyeh = g.sbuf("eyeh", [128, 128], F16)
        st_sum = g.sbuf("st_sum", [128, NT], F32)
        st_sq = g.sbuf("st_sq", [128, NT], F32)
        st_max = g.sbuf("st_max", [128, NT], F32)
        st_min = g.sbuf("st_min", [128, NT], F32)
        m_a = g.sbuf("m_a", [128, NT], F32)
        r_a = g.sbuf("r_a", [128, NT], F32)
        m_b = g.sbuf("m_b", [128, NT], F32)
        r_b = g.sbuf("r_b", [128, NT], F32)
        stt_sc = g.sbuf("stt_sc", [128, NT], F32)
        s_wg = g.sbuf("s_wg", [128, ICH], F32)
        s_wu = g.sbuf("s_wu", [128, ICH], F32)
        s_wd = g.sbuf("s_wd", [128, CCH], F32)
        eff_g = g.sbuf("eff_g", [128, ICH], F32)
        eff_u = g.sbuf("eff_u", [128, ICH], F32)
        alph = g.sbuf("alph", [128, CCH], F32)
        gbt = g.sbuf("gbt", [128, CCH], F32)
        delt = g.sbuf("delt", [128, CCH], F32)
        dplus = g.sbuf("dplus", [128, CCH], F32)
        a512 = g.sbuf("a512", [128, CCH], F32)
        a4096 = g.sbuf("a4096", [128, CCH], F32)
        kap = g.sbuf("kap", [128, CCH], F32)
        kap1a = g.sbuf("kap1a", [128, CCH], F32)
        kap1b = g.sbuf("kap1b", [128, CCH], F32)
        z512 = g.sbuf("z512", [128, CCH], F32)
        at1 = g.sbuf("at1", [128, CCH], F32)
        at2 = g.sbuf("at2", [128, CCH], F32)
        at3 = g.sbuf("at3", [128, CCH], F32)
        td_t = g.sbuf("td_t", [128, CCH], F32)
        tf_t = g.sbuf("tf_t", [128, CCH], F32)
        ones8 = g.sbuf("ones8", [128, CCH], F32)
        zc = g.sbuf("zc", [128, CCH], F32)
        acd = g.sbuf("acd", [128, CCH], F32)
        Epart = g.sbuf("Epart", [128, CCH], F32)
        magic_c = g.sbuf("magic_c", [128, 1], F32)
        scb = g.sbuf("scb", [128, 4], F32)
        ones_row = g.sbuf("ones_row", [1, 128], F32)
        neg1_c = g.sbuf("neg1_c", [128, 1], F32)
        zro_c = g.sbuf("zro_c", [128, 1], F32)
        one_c = g.sbuf("one_c", [128, 1], F32)
        red_c = g.sbuf("red_c", [128, 1], F32)
        oh_c = g.sbuf("oh_c", [128, 4], F32)
        sc_t = g.sbuf("sc_t", [1, 8], F32)
        sc_t2 = g.sbuf("sc_t2", [1, 8], F32)
        row128 = g.sbuf("row128", [1, 128], F32)
        clip_s = g.sbuf("clip_s", [1, 2], F32)
        g.barsc_v = g.sbuf("barsc_v", [1, 2], F32)
        g.barsc_s = g.sbuf("barsc_s", [1, 2], F32)
        g.barsc_g = g.sbuf("barsc_g", [1, 2], F32)

        P0 = g.psum("P0", [128, 512], F32)
        P1 = g.psum("P1", [128, 512], F32)
        P2 = g.psum("P2", [128, 512], F32)
        P3 = g.psum("P3", [128, 512], F32)
        PT0 = g.psum("PT0", [128, 1024], BF16)
        PT1 = g.psum("PT1", [128, 1024], BF16)
        PS = g.psum("PS", [128, 512], F32)

        sW = es.enter_context(contextlib.ExitStack())
        td8 = g.sbuf("td8", [8, 128], F32, sW)
        tf8 = g.sbuf("tf8", [8, 128], F32, sW)
        z1loc = g.sbuf("z1loc", [128, 512], F32, sW)
        qsq = g.sbuf("qsq", [128, 1024], F32, sW)
        wq1 = g.sbuf("wq1", [128, I], F32, sW)
        wq1b = g.sbuf("wq1b", [128, I], F32, sW)
        wqo = g.sbuf("wqo", [128, I], BF16, sW)
        wqob = g.sbuf("wqob", [128, I], BF16, sW)
        wst = g.sbuf("wst", [128, I], BF16, sW)
        wstb = g.sbuf("wstb", [128, I], BF16, sW)
        winv = g.sbuf("winv", [128, 2], F32, sW)

        g.dma("sync", eye32[:], eye_in)
        g.dma("sync", clip_s[:], clip_in)
        tok_eye = g.semv["ds"]
        g.dma("sync", td8[:], td_in)
        g.dma("sync", tf8[:], tf_in)
        tok_tdtf = g.semv["ds"]
        for p in range(4):
            g.dma("sync", oh_c[:, p:p + 1],
                  oh_in[p:p + 1, 0:1].broadcast_to([128, 1]))
        tok_oh = g.semv["ds"]
        for t in range(NT):
            g.dma("sync", xres[:, C * t:C * (t + 1)],
                  x_in[128 * t:128 * (t + 1), :], sem="dx")

        g.i("vector", "memset", magic_c[:], MAGIC)
        g.i("vector", "memset", zro_c[:], 0.0)
        g.i("vector", "memset", one_c[:], 1.0)
        g.i("vector", "memset", neg1_c[:], -1.0)
        g.i("vector", "memset", ones8[:], 1.0)
        g.i("vector", "memset", ones_row[:], 1.0)
        g.i("vector", "memset", sc_t[:], 0.0)
        g.i("vector", "memset", Epart[:], 0.0)
        g.w("vector", "ds")
        g.i("vector", "tensor_copy", out=eyeb[:], in_=eye32[:])
        tok_eyes = g.im("vector", "tensor_copy", out=eyeh[:], in_=eye32[:], sem="dv")

        g.w("tensor", "ds")
        g.w("tensor", "dv", tok_eyes)
        g.i("tensor", "transpose", PS[:, 0:8], td8[:, :], eye32[0:8, 0:8])
        tok_tdT = g.im("tensor", "transpose", PS[:, 8:16], tf8[:, :],
                       eye32[0:8, 0:8], sem="pe")
        g.w("scalar", "pe", tok_tdT)
        g.i("scalar", "copy", out=td_t[:], in_=PS[:, 0:8])
        tok_tft = g.im("scalar", "copy", out=tf_t[:], in_=PS[:, 8:16], sem="ac")
        g.done("PS", "ac", tok_tft)
        g.w("vector", "ac", tok_tft)
        g.i("vector", "tensor_scalar", out=at1[:], in0=td_t[:], scalar1=0.0,
            scalar2=None, op0=ALU.max)
        g.i("vector", "tensor_scalar", out=at2[:], in0=at1[:], scalar1=-1.0,
            scalar2=-87.0, op0=ALU.mult, op1=ALU.max)
        tok_v = g.im("vector", "tensor_scalar", out=at3[:], in0=td_t[:],
                     scalar1=0.0, scalar2=-87.0, op0=ALU.min, op1=ALU.max,
                     sem="dv")
        g.w("scalar", "dv", tok_v)
        g.i("scalar", "activation", out=alph[:], in_=at2[:], func=AF.Exp)
        tok_be = g.im("scalar", "activation", out=gbt[:], in_=at3[:],
                      func=AF.Exp, sem="ac")
        g.w("vector", "ac", tok_be)
        g.i("vector", "tensor_scalar", out=at2[:], in0=tf_t[:], scalar1=0.0,
            scalar2=None, op0=ALU.max)
        g.i("vector", "tensor_scalar", out=at2[:], in0=at2[:], scalar1=-1.0,
            scalar2=-87.0, op0=ALU.mult, op1=ALU.max)
        tok_v = g.im("vector", "tensor_scalar", out=at3[:], in0=tf_t[:],
                     scalar1=0.0, scalar2=-87.0, op0=ALU.min, op1=ALU.max,
                     sem="dv")
        g.w("scalar", "dv", tok_v)
        g.i("scalar", "activation", out=delt[:], in_=at3[:], func=AF.Exp)
        tok_ga = g.im("scalar", "activation", out=at3[:], in_=at2[:],
                      func=AF.Exp, sem="ac")
        g.w("vector", "ac", tok_ga)
        g.i("vector", "tensor_mul", out=gbt[:], in0=gbt[:], in1=at3[:])
        g.i("vector", "tensor_scalar", out=dplus[:], in0=delt[:], scalar1=1e-6,
            scalar2=None, op0=ALU.add)
        tok_v = g.im("vector", "tensor_scalar", out=at2[:], in0=at1[:],
                     scalar1=-512.0, scalar2=-87.0, op0=ALU.mult, op1=ALU.max,
                     sem="dv")
        g.w("scalar", "dv", tok_v)
        tok_a5 = g.im("scalar", "activation", out=a512[:], in_=at2[:],
                      func=AF.Exp, sem="ac")
        g.w("vector", "ac", tok_a5)
        for dc in range(CCH):
            tok_zs = g.im("vector", "tensor_tensor_scan", out=z1loc[:],
                          data0=alph[:, dc:dc + 1].broadcast_to([128, 512]),
                          data1=gbt[:, dc:dc + 1].broadcast_to([128, 512]),
                          initial=0.0, op0=ALU.mult, op1=ALU.add, sem="dv")
            g.w("vector", "dv", tok_zs)
            g.i("vector", "tensor_copy", out=z512[:, dc:dc + 1],
                in_=z1loc[:, 511:512])
        g.i("vector", "tensor_copy", out=zc[:], in_=z512[:])
        g.i("vector", "tensor_copy", out=acd[:], in_=a512[:])
        for _ in range(3):
            g.i("vector", "tensor_scalar", out=at1[:], in0=acd[:], scalar1=1.0,
                scalar2=None, op0=ALU.add)
            g.i("vector", "tensor_mul", out=at2[:], in0=zc[:], in1=at1[:])
            g.i("vector", "tensor_copy", out=zc[:], in_=at2[:])
            g.i("vector", "tensor_mul", out=at3[:], in0=acd[:], in1=acd[:])
            g.i("vector", "tensor_copy", out=acd[:], in_=at3[:])
        g.i("vector", "tensor_copy", out=a4096[:], in_=acd[:])
        g.w("vector", "ds")
        g.i("vector", "tensor_scalar", out=at1[:], in0=a4096[:], scalar1=1.0,
            scalar2=None, op0=ALU.add)
        g.i("vector", "tensor_mul", out=at2[:], in0=a4096[:], in1=at1[:])
        g.i("vector", "tensor_scalar", out=at2[:], in0=at2[:], scalar1=1.0,
            scalar2=None, op0=ALU.add)
        g.i("vector", "tensor_scalar", out=at3[:], in0=ones8[:],
            scalar1=oh_c[:, 1:2], scalar2=None, op0=ALU.mult)
        g.i("vector", "scalar_tensor_tensor", out=kap1b[:], in0=at1[:],
            scalar=oh_c[:, 2:3], in1=at3[:], op0=ALU.mult, op1=ALU.add)
        g.i("vector", "scalar_tensor_tensor", out=at3[:], in0=at2[:],
            scalar=oh_c[:, 3:4], in1=kap1b[:], op0=ALU.mult, op1=ALU.add)
        g.im("vector", "tensor_mul", out=kap1a[:], in0=zc[:], in1=at3[:],
             sem="dv")

        g.w("vector", "dx")
        g.w("scalar", "dx")
        for t in range(NT):
            xt = xres[:, C * t:C * (t + 1)]
            g.i("vector", "reduce_sum", out=st_sum[:, t:t + 1], in_=xt, axis=AX.X)
            g.i("vector", "reduce_max", out=st_max[:, t:t + 1], in_=xt, axis=AX.X)
            g.i("vector", "tensor_reduce", out=st_min[:, t:t + 1], in_=xt,
                axis=AX.X, op=ALU.min)
            g.i("scalar", "activation", out=qsq[:], in_=xt, func=AF.Square,
                accum_out=st_sq[:, t:t + 1])
        tok_av = g.mark("vector", "dv")
        tok_as = g.mark("scalar", "ac")

        def prep_tail(rc, mode, ncols, nb, dst_blk, sr=None, bufs=None,
                      gk=("wqo", "wst")):
            sr = rc if sr is None else sr
            f16 = (mode == "qd")
            qvx, wstx = bufs if bufs else ((wqo, wqob)[rc % 2],
                                           (wst, wstb)[rc % 2])
            qv = qvx.bitcast(F16) if f16 else qvx
            sv_ = wstx.bitcast(F16) if f16 else wstx
            for kb in range(nb):
                off = (kb % 4) * 128
                PTx = (PT0, PT1)[(kb // 4) % 2]
                PTv = PTx.bitcast(F16) if f16 else PTx
                g.need("tensor", ("PT", (kb // 4) % 2))
                tok_tr = g.im("tensor", "transpose", PTv[:, off:off + 128],
                              qv[:, 128 * kb:128 * (kb + 1)],
                              (eyeh if f16 else eyeb)[:, :], sem="pe")
                if (kb % 4 == 3) or kb == nb - 1:
                    lo = (kb // 4) * 512
                    width = off + 128
                    g.w("scalar", "dt")
                    g.w("scalar", "pe", tok_tr)
                    tok_ev = g.im("scalar", "copy",
                                  out=sv_[:, lo:lo + width],
                                  in_=PTv[:, 0:width], sem="ac")
                    g.done(("PT", (kb // 4) % 2), "ac", tok_ev)
            g.done((gk[0], rc % 2), "pe", g.semv["pe"])
            g.w("scalar", "ac")
            if mode in ("qi", "cb"):
                g.dma("scalar", dst_blk[sr], sv_[:, 0:ncols], sem="dt")
            else:
                for kb in range(nb):
                    g.dma("scalar", dst_blk[kb][:, 128 * sr:128 * (sr + 1)],
                          sv_[:, 128 * kb:128 * (kb + 1)], sem="dt")
            g.done((gk[1], rc % 2), "dt", g.semv["dt"])

        L6 = ("dw0", "dw1", "dm0", "dm1", "dc0", "dc1")

        def prep_w1():
            srcs = (wg_in, wu_in)
            dsts = (wgb, wub)
            stl = (s_wg, s_wu)
            NCH = 2 * ICH
            wpair = {}
            atok = {}
            qtok = {}

            def issue_load(c):
                g.need("sync", ("wl1", c % 6))
                g.dma("sync", wldq[c % 6][:],
                      srcs[c % 2][128 * (c // 2):128 * (c // 2 + 1), :],
                      sem=L6[c % 6])

            def stA(c):
                g.w("vector", L6[c % 6])
                tok_rr = g.im("vector", "tensor_reduce", out=red_c[:],
                              in_=wldq[c % 6][:], axis=AX.X, op=ALU.max,
                              apply_absolute_value=True, sem="dv")
                g.w("vector", "dv", tok_rr)
                tok_st = g.im("vector", "tensor_scalar",
                              out=stl[c % 2][:, c // 2:c // 2 + 1],
                              in0=red_c[:], scalar1=1.0 / QMAX, scalar2=None,
                              op0=ALU.mult, sem="dv")
                g.w("vector", "dv", tok_st)
                return g.im("vector", "reciprocal",
                            out=winv6[:, c % 6:c % 6 + 1],
                            in_=stl[c % 2][:, c // 2:c // 2 + 1], sem="dv")

            for j in range(5):
                issue_load(j)
                atok[j] = stA(j)
            for c in range(NCH):
                if c + 5 < NCH:
                    issue_load(c + 5)
                    atok[c + 5] = stA(c + 5)
                wq = (wq1, wq1b)[c % 2]
                g.need("scalar", ("wq1", c % 2))
                g.w("scalar", "dv", atok[c])
                tok_q1 = g.im("scalar", "activation", out=wq[:, 0:C],
                              in_=wldq[c % 6][:], func=AF.Identity,
                              scale=winv6[:, c % 6:c % 6 + 1],
                              bias=magic_c[:, 0:1], sem="ac")
                g.done(("wl1", c % 6), "ac", tok_q1)
                g.w("vector", "ac", tok_q1)
                g.need("vector", ("wqo", c % 2))
                tok_q = g.im("vector", "tensor_scalar",
                             out=(wqo, wqob)[c % 2][:, 0:C], in0=wq[:, 0:C],
                             scalar1=MAGIC, scalar2=None, op0=ALU.subtract,
                             sem="dv")
                g.done(("wq1", c % 2), "dv", tok_q)
                qtok[c] = tok_q
                if c >= 1:
                    g.w("tensor", "dv", qtok[c - 1])
                    prep_tail(c - 1, "qi", C, CCH, dsts[(c - 1) % 2],
                              sr=(c - 1) // 2)
                    if (c - 1) % 2 == 1:
                        g.w("gpsimd", "dt")
                        wpair[(c - 1) // 2] = g.im(
                            "gpsimd", "memset", g.barsc_g[0:1, 0:1], 0.0,
                            sem="gp")
            g.w("tensor", "dv", qtok[NCH - 1])
            prep_tail(NCH - 1, "qi", C, CCH, dsts[1], sr=ICH - 1)
            g.w("gpsimd", "dt")
            wpair[ICH - 1] = g.im("gpsimd", "memset", g.barsc_g[0:1, 0:1],
                                  0.0, sem="gp")
            return wpair

        def prep_weight(src, nrows_ch, ncols, dst_blk, s_tile, mode):
            ldb = (wld, wld2)
            q1b = (wq1, wq1b)
            nb = ncols // 128

            def wd_load(rc):
                g.need("sync", ("wld", rc % 2))
                hc = ncols // 4
                for sg in range(4):
                    g.dma("sync", ldb[rc % 2][:, hc * sg:hc * (sg + 1)],
                          src[128 * rc:128 * (rc + 1), hc * sg:hc * (sg + 1)],
                          sem="dw%d" % (rc % 2))

            def stageA(rc):
                lb = ldb[rc % 2][:, 0:ncols]
                g.w("vector", "dw%d" % (rc % 2))
                tok_rr = g.im("vector", "tensor_reduce", out=red_c[:],
                              in_=lb, axis=AX.X, op=ALU.max,
                              apply_absolute_value=True, sem="dv")
                g.w("vector", "dv", tok_rr)
                tok_st = g.im("vector", "tensor_scalar",
                              out=s_tile[:, rc:rc + 1], in0=red_c[:],
                              scalar1=1.0 / QMAX, scalar2=None,
                              op0=ALU.mult, sem="dv")
                g.w("vector", "dv", tok_st)
                return g.im("vector", "reciprocal",
                            out=winv[:, rc % 2:rc % 2 + 1],
                            in_=s_tile[:, rc:rc + 1], sem="dv")

            atok = {}
            qtok = {}
            if mode in ("qi", "qd"):
                for rc in range(nrows_ch):
                    if rc == 0:
                        wd_load(0)
                        atok[0] = stageA(0)
                    lb = ldb[rc % 2][:, 0:ncols]
                    wq = q1b[rc % 2]
                    g.need("scalar", ("wq1", rc % 2))
                    g.w("scalar", "dv", atok[rc])
                    tok_q1 = g.im("scalar", "activation", out=wq[:, 0:ncols],
                                  in_=lb, func=AF.Identity,
                                  scale=winv[:, rc % 2:rc % 2 + 1],
                                  bias=magic_c[:, 0:1], sem="ac")
                    g.done(("wld", rc % 2), "ac", tok_q1)
                    if rc + 1 < nrows_ch:
                        wd_load(rc + 1)
                        atok[rc + 1] = stageA(rc + 1)
                    g.w("vector", "ac", tok_q1)
                    g.need("vector", ("wqo", rc % 2))
                    qvx = (wqo, wqob)[rc % 2]
                    if mode == "qi":
                        tok_q = g.im("vector", "tensor_scalar",
                                     out=qvx[:, 0:ncols], in0=wq[:, 0:ncols],
                                     scalar1=MAGIC, scalar2=None,
                                     op0=ALU.subtract, sem="dv")
                    else:
                        tok_q = g.im("vector", "tensor_scalar",
                                     out=qvx.bitcast(F16)[:, 0:ncols],
                                     in0=wq[:, 0:ncols], scalar1=MAGIC,
                                     scalar2=s_tile[:, rc:rc + 1],
                                     op0=ALU.subtract, op1=ALU.mult, sem="dv")
                    g.done(("wq1", rc % 2), "dv", tok_q)
                    qtok[rc] = tok_q
                    if rc >= 1:
                        g.w("tensor", "dv", qtok[rc - 1])
                        prep_tail(rc - 1, mode, ncols, nb, dst_blk)
                g.w("tensor", "dv", qtok[nrows_ch - 1])
                prep_tail(nrows_ch - 1, mode, ncols, nb, dst_blk)
                return
            for rc in range(nrows_ch):
                g.need("sync", ("wld", rc % 2))
                g.dma("sync", ldb[rc % 2][:, 0:ncols],
                      src[128 * rc:128 * (rc + 1), 0:ncols],
                      sem="dw%d" % (rc % 2))
                lb = ldb[rc % 2][:, 0:ncols]
                if False:
                    pass
                else:
                    g.need("scalar", ("wqo", rc % 2))
                    g.w("scalar", "dw%d" % (rc % 2))
                    tok_q = g.im("scalar", "copy",
                                 out=(wqo, wqob)[rc % 2][:, 0:ncols], in_=lb,
                                 sem="ac")
                    g.done(("wld", rc % 2), "ac", tok_q)
                    qtok[rc] = tok_q
                    if rc >= 1:
                        g.w("tensor", "ac", qtok[rc - 1])
                        prep_tail(rc - 1, mode, ncols, nb, dst_blk)
            if mode not in ("qi", "qd"):
                g.w("tensor", "ac", qtok[nrows_ch - 1])
                prep_tail(nrows_ch - 1, mode, ncols, nb, dst_blk)

        def ln_finalize(m_t, r_t, with_absmax):
            tok_mm = g.im("vector", "tensor_scalar", out=m_t[:], in0=st_sum[:],
                          scalar1=1.0 / C, scalar2=None, op0=ALU.mult, sem="dv")
            g.i("vector", "tensor_scalar", out=stt_sc[:], in0=st_sq[:],
                scalar1=1.0 / C, scalar2=None, op0=ALU.mult)
            g.w("vector", "dv", tok_mm)
            g.i("vector", "tensor_mul", out=r_t[:], in0=m_t[:], in1=m_t[:])
            g.i("vector", "tensor_sub", out=stt_sc[:], in0=stt_sc[:], in1=r_t[:])
            tok = g.im("vector", "tensor_scalar", out=r_t[:], in0=stt_sc[:],
                       scalar1=1e-5, scalar2=None, op0=ALU.add, sem="dv")
            g.w("scalar", "dv", tok)
            tok2 = g.im("scalar", "activation", out=stt_sc[:], in_=r_t[:],
                        func=AF.Sqrt, sem="ac")
            g.w("vector", "ac", tok2)
            tok3 = g.im("vector", "reciprocal", out=r_t[:], in_=stt_sc[:],
                        sem="dv")
            if with_absmax:
                g.i("vector", "tensor_sub", out=stt_sc[:], in0=st_max[:],
                    in1=m_t[:])
                g.i("vector", "tensor_sub", out=st_max[:], in0=m_t[:],
                    in1=st_min[:])
                g.i("vector", "tensor_max", out=stt_sc[:], in0=stt_sc[:],
                    in1=st_max[:])
                g.i("vector", "tensor_mul", out=stt_sc[:], in0=stt_sc[:],
                    in1=r_t[:])
                g.i("vector", "reduce_max", out=red_c[:], in_=stt_sc[:],
                    axis=AX.X)
                tok4 = g.mark("vector", "dv")
                return tok3, tok4
            return tok3, None

        def global_max_scale(tok_red, clip_row):
            g.w("tensor", "dv", tok_red)
            g.need("tensor", "PS")
            tok_t = g.im("tensor", "transpose", PS[0:1, 0:128], red_c[:, 0:1],
                         eye32[:, :], sem="pe")
            g.w("vector", "pe", tok_t)
            tok_m = g.im("vector", "reduce_max", out=sc_t[:, 0:1],
                         in_=PS[0:1, 0:128], axis=AX.X, sem="dv")
            g.done("PS", "dv", tok_m)
            g.w("gpsimd", "dv", tok_m)
            g.dma("gpsimd", mx_i, sc_t[:, 0:8])
            g.w("gpsimd", "dg")
            tok_cc = g.im("gpsimd", "collective_compute", "AllReduce", ALU.max,
                          replica_groups=[list(range(8))], ins=[mx_i],
                          outs=[mx_o], sem="gp")
            g.w("gpsimd", "gp", tok_cc)
            g.dma("gpsimd", sc_t[:, 0:8], mx_o)
            g.w("scalar", "dg", g.semv["dg"])
            tok_sg = g.im("scalar", "activation", out=sc_t2[:, 0:1],
                          in_=clip_s[0:1, clip_row:clip_row + 1],
                          func=AF.Sigmoid, sem="ac")
            g.w("vector", "ac", tok_sg)
            tok_s1 = g.im("vector", "tensor_scalar", out=sc_t[:, 1:2],
                          in0=sc_t[:, 0:1], scalar1=sc_t2[0:1, 0:1],
                          scalar2=1.0 / QMAX, op0=ALU.mult, op1=ALU.mult,
                          sem="dv")
            g.w("vector", "dv", tok_s1)
            g.i("vector", "tensor_scalar", out=sc_t[:, 3:4], in0=sc_t[:, 1:2],
                scalar1=0.5, scalar2=None, op0=ALU.mult)
            tok_s = g.im("vector", "reciprocal", out=sc_t[:, 2:3],
                         in_=sc_t[:, 1:2], sem="dv")
            g.w("tensor", "dv", tok_s)
            g.need("tensor", "PS")
            tok_b = g.im("tensor", "matmul", PS[:, 0:3], ones_row[0:1, :],
                         sc_t[0:1, 1:4], start=True, stop=True, sem="pe")
            g.w("scalar", "pe", tok_b)
            tok_c = g.im("scalar", "copy", out=scb[:, 0:3], in_=PS[:, 0:3],
                         sem="ac")
            g.done("PS", "ac", tok_c)
            return tok_c

        g.w("vector", "dv", tok_av)
        g.w("vector", "ac", tok_as)
        t_r1, t_am1 = ln_finalize(m_a, r_a, True)
        tok_sx = global_max_scale(t_am1, 0)

        sW1 = es.enter_context(contextlib.ExitStack())
        wldq = [g.sbuf("wldq%d" % j, [128, C], F32, sW1) for j in range(6)]
        winv6 = g.sbuf("winv6", [128, 6], F32, sW1)
        wpair = prep_w1()
        qa_w1 = g.semv["ac"]
        g.done(("wld", 0), "ac", qa_w1)
        g.done(("wld", 1), "ac", qa_w1)
        sW1.close()
        sW2 = es.enter_context(contextlib.ExitStack())
        wld = g.sbuf("wld", [128, I], F32, sW2)
        wld2 = g.sbuf("wld2", [128, I], F32, sW2)
        prep_weight(wd_in, CCH, I, wdb, s_wd, "qd")
        wstore = g.semv["dt"]

        g.w("vector", "ac", tok_sx)
        g.i("vector", "tensor_scalar", out=eff_g[:], in0=s_wg[:],
            scalar1=scb[:, 0:1], scalar2=None, op0=ALU.mult)
        g.im("vector", "tensor_scalar", out=eff_u[:], in0=s_wu[:],
             scalar1=scb[:, 0:1], scalar2=None, op0=ALU.mult, sem="dv")
        tok_eff = g.semv["dv"]
        sW2.close()
        sW.close()

        if stage == "dbg1":
            g.barrier()
            sDBG = es.enter_context(contextlib.ExitStack())
            dbf = g.sbuf("dbf", [128, 1024], F32, sDBG)
            dbh = g.sbuf("dbh", [128, 1024], BF16, sDBG)
            g.i("vector", "memset", dbf[:], 0.0)
            g.i("vector", "tensor_copy", out=dbf[:, 0:NT], in_=m_a[:])
            g.i("vector", "tensor_copy", out=dbf[:, 32:32 + NT], in_=r_a[:])
            g.i("vector", "tensor_copy", out=dbf[:, 64:64 + NT], in_=st_sum[:])
            g.i("vector", "tensor_copy", out=dbf[:, 96:96 + NT], in_=st_sq[:])
            g.i("vector", "tensor_copy", out=dbf[:, 128:129], in_=scb[:, 0:1])
            g.i("vector", "tensor_copy", out=dbf[:, 129:130], in_=scb[:, 1:2])
            g.i("vector", "tensor_copy", out=dbf[:, 130:131], in_=scb[:, 2:3])
            g.i("vector", "tensor_copy", out=dbf[:, 132:132 + ICH], in_=s_wg[:])
            g.i("vector", "tensor_copy", out=dbf[:, 160:160 + ICH], in_=s_wu[:])
            g.i("vector", "tensor_copy", out=dbf[:, 192:192 + CCH], in_=s_wd[:])
            tok_d = g.mark("vector", "dv")
            g.w("sync", "dv", tok_d)
            g.dma("sync", out_d[0:128, :], dbf[:])
            g.i("vector", "tensor_scalar", out=dbf[:], in0=xres[:, 0:C],
                scalar1=m_a[:, 0:1], scalar2=r_a[:, 0:1],
                op0=ALU.subtract, op1=ALU.mult)
            tok_d = g.mark("vector", "dv")
            g.w("sync", "dv", tok_d)
            g.dma("sync", out_d[128:256, :], dbf[:])
            g.w("scalar", "dv", tok_d)
            tok_q1d = g.im("scalar", "activation", out=dbf[:], in_=dbf[:],
                           func=AF.Identity, scale=scb[:, 1:2],
                           bias=magic_c[:, 0:1], sem="ac")
            g.w("vector", "ac", tok_q1d)
            g.i("vector", "tensor_scalar", out=dbf[:], in0=dbf[:],
                scalar1=MAGIC, scalar2=127.0, op0=ALU.subtract, op1=ALU.min)
            g.i("vector", "tensor_scalar", out=dbh[:], in0=dbf[:],
                scalar1=-128.0, scalar2=None, op0=ALU.max)
            g.i("vector", "tensor_copy", out=dbf[:], in_=dbh[:])
            tok_d = g.mark("vector", "dv")
            g.w("sync", "dv", tok_d)
            g.dma("sync", out_d[256:384, :], dbf[:])
            g.dma("sync", dbh[:], wgb[0])
            g.w("vector", "ds")
            g.i("vector", "tensor_copy", out=dbf[:], in_=dbh[:])
            tok_d = g.mark("vector", "dv")
            g.w("sync", "dv", tok_d)
            g.dma("sync", out_d[384:512, :], dbf[:])
            g.dma("sync", dbh.bitcast(F16)[:], wdb[0])
            g.w("vector", "ds")
            g.i("vector", "tensor_copy", out=dbf[:], in_=dbh.bitcast(F16)[:])
            tok_d = g.mark("vector", "dv")
            g.w("sync", "dv", tok_d)
            g.dma("sync", out_d[512:640, :], dbf[:])
            g.w("sync", "ds")
            blk = es.enter_context(nc.Block())
            g.finish(blk)
            return nc

        def ffn_run(m_t, r_t, final, dbg3=False, wgate=None, wstore_t=None,
                    eff_t=None, inject=None, post_b=None):
            tg = "F2" if final else "F1"
            sB = es.enter_context(contextlib.ExitStack())
            xqT = g.sbuf("xqT" + tg, [128, 2 * CCH * 1024], BF16, sB)
            wmb = g.sbuf("wmb" + tg, [128, 4 * C], BF16, sB)
            hstg = g.sbuf("hstg" + tg, [128, 2 * 1024], F16, sB)
            silu_s = g.sbuf("silu_s" + tg, [128, 512], F32, sB)
            habs = g.sbuf("habs" + tg, [128, 192], F32, sB)
            lncol = g.sbuf("lncol" + tg, [128, 2], F32, sB)
            qt1 = g.sbuf("qt1" + tg, [128, 1024], F32, sB)
            xq_tm = g.sbuf("xq_tm" + tg, [128, 4 * 1024], BF16, sB)

            xq_ready = {}
            xq_used = {}
            if wstore_t is not None:
                g.w("vector", "dt", wstore_t)
                g.w("scalar", "dt", wstore_t)
                g.w("sync", "dt", wstore_t)
            if eff_t is not None:
                g.w("scalar", "dv", eff_t)

            def B0(q, half):
                slot = q % 2
                base = slot * CCH * 1024
                if half == 0 and q >= 2:
                    g.w("vector", "pe", xq_used[q - 2])
                    g.w("scalar", "pe", xq_used[q - 2])
                for tl in range(4 * half, 4 * half + 4):
                    t = 8 * q + tl
                    xt = xres[:, C * t:C * (t + 1)]
                    g.need("vector", "lncol")
                    g.i("vector", "tensor_scalar", out=lncol[:, 0:1],
                        in0=r_t[:, t:t + 1], scalar1=scb[:, 1:2],
                        scalar2=None, op0=ALU.mult)
                    tok_ln = g.im("vector", "tensor_scalar",
                                  out=lncol[:, 1:2], in0=m_t[:, t:t + 1],
                                  scalar1=lncol[:, 0:1], scalar2=-1.0,
                                  op0=ALU.mult, op1=ALU.mult, sem="dv")
                    g.need("scalar", "qt1")
                    g.w("scalar", "dv", tok_ln)
                    tok_q1 = g.im("scalar", "activation", out=qt1[:], in_=xt,
                                  func=AF.Identity, scale=lncol[:, 0:1],
                                  bias=lncol[:, 1:2], sem="ac")
                    g.done("lncol", "ac", tok_q1)
                    g.w("vector", "ac", tok_q1)
                    g.i("vector", "tensor_scalar", out=qt1[:], in0=qt1[:],
                        scalar1=MAGIC, scalar2=None, op0=ALU.add)
                    g.i("vector", "tensor_scalar", out=qt1[:], in0=qt1[:],
                        scalar1=MAGIC, scalar2=127.0, op0=ALU.subtract,
                        op1=ALU.min)
                    g.need("vector", ("xq_tm", tl % 4))
                    tok_q = g.im("vector", "tensor_scalar",
                                 out=xq_tm[:, 1024 * (tl % 4):1024 * (tl % 4 + 1)],
                                 in0=qt1[:], scalar1=-128.0, scalar2=None,
                                 op0=ALU.max, sem="dv")
                    g.done("qt1", "dv", tok_q)
                    if tl % 4 == 3:
                        qq = tl // 4
                        g.w("tensor", "dv", tok_q)
                        for cc in range(CCH):
                            PTx = (PT0, PT1)[cc % 2]
                            g.need("tensor", ("PT", cc % 2))
                            tok_tr = None
                            for tj in range(4):
                                tok_tr = g.im(
                                    "tensor", "transpose",
                                    PTx[:, 128 * tj:128 * (tj + 1)],
                                    xq_tm[:, 1024 * tj + 128 * cc:
                                          1024 * tj + 128 * cc + 128],
                                    eyeb[:, :], sem="pe")
                            dst = xqT[:, base + 1024 * cc + 512 * qq:
                                      base + 1024 * cc + 512 * qq + 512]
                            if cc % 2 == 0:
                                g.w("vector", "pe", tok_tr)
                                tok_e = g.im("vector", "tensor_copy", out=dst,
                                             in_=PTx[:, 0:512], sem="dv")
                                g.done(("PT", 0), "dv", tok_e)
                            else:
                                g.w("scalar", "pe", tok_tr)
                                tok_e = g.im("scalar", "copy", out=dst,
                                             in_=PTx[:, 0:512], sem="ac")
                                g.done(("PT", 1), "ac", tok_e)
                        for tj in range(4):
                            g.done(("xq_tm", tj), "pe", g.semv["pe"])
                if half == 1:
                    xq_ready[q] = (g.mark("vector", "dv"),
                                   g.mark("scalar", "ac"))

            def B1(q):
                slot = q % 2
                base = slot * CCH * 1024
                tv, ta = xq_ready[q]
                g.w("tensor", "dv", tv)
                g.w("tensor", "ac", ta)
                last_mm = None
                wtok = {}

                def b1_load(ic):
                    if wgate is not None:
                        g.w("sync", "gp", wgate[ic])
                    for wi, wsrc in enumerate((wgb, wub)):
                        ws = (ic % 2) * 2 + wi
                        g.need("sync", ("wmb", ws))
                        g.dma("sync", wmb[:, C * ws:C * (ws + 1)], wsrc[ic],
                              sem="dm%d" % (ic % 2))
                    wtok[ic] = g.semv["dm%d" % (ic % 2)]

                ics = [0] if dbg3 else list(range(ICH))
                b1_load(ics[0])
                for ii, ic in enumerate(ics):
                    if inject is not None and ic % 3 == 1:
                        inject()
                    if not dbg3 and q + 1 < 4 and ic in (8, 15):
                        B0(q + 1, 0 if ic == 8 else 1)
                    if ii + 1 < len(ics):
                        b1_load(ics[ii + 1])
                    g.w("tensor", "dm%d" % (ic % 2), wtok[ic])
                    for g2 in range(2):
                        for wi in range(2):
                            ws = (ic % 2) * 2 + wi
                            bank = (P0, P1, P2, P3)[2 * g2 + wi]
                            g.need("tensor", ("PB", 2 * g2 + wi))
                            for k in range(CCH):
                                last_mm = g.im(
                                    "tensor", "matmul", bank[:],
                                    wmb[:, C * ws + 128 * k:
                                        C * ws + 128 * (k + 1)],
                                    xqT[:, base + 1024 * k + 512 * g2:
                                        base + 1024 * k + 512 * g2 + 512],
                                    start=(k == 0), stop=(k == CCH - 1),
                                    sem="pe")
                        g.done(("wmb", (ic % 2) * 2), "pe", last_mm)
                        g.done(("wmb", (ic % 2) * 2 + 1), "pe", last_mm)
                        gbank, ubank = (P0, P1) if g2 == 0 else (P2, P3)
                        g.need("scalar", "silu_s")
                        g.w("scalar", "pe", last_mm)
                        tok_si = g.im("scalar", "activation", out=silu_s[:],
                                      in_=gbank[:], func=AF.Silu,
                                      scale=eff_g[:, ic:ic + 1], sem="ac")
                        g.done(("PB", 2 * g2 + 0), "ac", tok_si)
                        hdst = hstg[:, 1024 * (ic % 2) + 512 * g2:
                                    1024 * (ic % 2) + 512 * g2 + 512]
                        g.need("vector", ("hstg", ic % 2))
                        g.w("vector", "ac", tok_si)
                        tok_h = g.im("vector", "scalar_tensor_tensor",
                                     out=hdst, in0=ubank[:],
                                     scalar=eff_u[:, ic:ic + 1], in1=silu_s[:],
                                     op0=ALU.mult, op1=ALU.mult, sem="dv")
                        g.done(("PB", 2 * g2 + 1), "dv", tok_h)
                        g.done("silu_s", "dv", tok_h)
                        g.i("vector", "tensor_reduce",
                            out=habs[:, 4 * (2 * ic + g2) + q:
                                     4 * (2 * ic + g2) + q + 1],
                            in_=hdst, axis=AX.X, op=ALU.max,
                            apply_absolute_value=True)
                    tok_hv = g.mark("vector", "dv")
                    g.w("sync", "dv", tok_hv)
                    sh = "dh%d" % (ic % 2)
                    g.dma("sync", hsp[ic][:, 1024 * q:1024 * (q + 1)],
                          hstg[:, 1024 * (ic % 2):1024 * (ic % 2) + 1024],
                          sem=sh)
                    g.done(("hstg", ic % 2), sh, g.semv[sh])
                xq_used[q] = last_mm

            if dbg3:
                B0(0, 0)
                B0(0, 1)
                B1(0)
                sDBG = es.enter_context(contextlib.ExitStack())
                dbf = g.sbuf("dbf", [128, 1024], F32, sDBG)
                g.barrier()
                g.i("vector", "tensor_copy", out=dbf[:], in_=xqT[:, 0:1024])
                tok_d = g.mark("vector", "dv")
                g.w("sync", "dv", tok_d)
                g.dma("sync", out_d[0:128, :], dbf[:])
                g.w("vector", "ds")
                g.i("vector", "tensor_copy", out=dbf[:, 0:512], in_=P0[:])
                g.i("vector", "tensor_copy", out=dbf[:, 512:1024], in_=P1[:])
                tok_d = g.mark("vector", "dv")
                g.w("sync", "dv", tok_d)
                g.dma("sync", out_d[128:256, :], dbf[:])
                g.w("vector", "ds")
                g.i("vector", "tensor_copy", out=dbf[:, 0:512], in_=silu_s[:])
                g.i("vector", "tensor_copy", out=dbf[:, 512:512 + ICH],
                    in_=eff_g[:])
                g.i("vector", "tensor_copy", out=dbf[:, 544:544 + ICH],
                    in_=eff_u[:])
                tok_d = g.mark("vector", "dv")
                g.w("sync", "dv", tok_d)
                g.dma("sync", out_d[256:384, :], dbf[:])
                g.w("vector", "ds")
                g.i("vector", "tensor_copy", out=dbf[:], in_=hstg[:, 0:1024])
                tok_d = g.mark("vector", "dv")
                g.w("sync", "dv", tok_d)
                g.dma("sync", out_d[384:512, :], dbf[:])
                g.w("sync", "ds")
                g.dma("sync", out_d[512:640, :], xres[:, 0:1024])
                g.w("vector", "ds")
                g.i("vector", "memset", dbf[:], 0.0)
                g.i("vector", "tensor_copy", out=dbf[:, 0:1], in_=scb[:, 1:2])
                g.i("vector", "tensor_copy", out=dbf[:, 1:2], in_=scb[:, 0:1])
                g.i("vector", "tensor_copy", out=dbf[:, 2:3], in_=magic_c[:])
                g.i("vector", "tensor_copy", out=dbf[:, 16:48], in_=m_a[:])
                g.i("vector", "tensor_copy", out=dbf[:, 48:80], in_=r_a[:])
                g.i("vector", "tensor_copy", out=dbf[:, 80:102], in_=s_wg[:])
                g.i("vector", "tensor_copy", out=dbf[:, 112:112 + 1024 - 112]
                    if False else dbf[:, 102:103], in_=red_c[:])
                tok_d = g.mark("vector", "dv")
                g.w("sync", "dv", tok_d)
                g.dma("sync", out_d[640:768, :], dbf[:])
                g.w("vector", "ds")
                g.i("vector", "tensor_copy", out=dbf[:], in_=xq_tm[:, 0:1024])
                tok_d = g.mark("vector", "dv")
                g.w("sync", "dv", tok_d)
                g.dma("sync", out_d[768:896, :], dbf[:])
                g.w("sync", "ds")
                return None
            B0(0, 0)
            B0(0, 1)
            for q in range(4):
                B1(q)
            g.w("vector", "dv")
            g.i("vector", "reduce_max", out=red_c[:], in_=habs[:, 0:176],
                axis=AX.X)
            tok_habs = g.mark("vector", "dv")
            sB.close()
            if post_b is not None:
                post_b()

            sC = es.enter_context(contextlib.ExitStack())
            wdw = g.sbuf("wdw" + tg, [128, ICH * C], F16, sC)
            hqb = g.sbuf("hqb" + tg, [128, 2 * ICH * 256], F16, sC)
            qc = g.sbuf("qc" + tg, [128, 4 * 256], F32, sC)
            dsc = g.sbuf("dsc" + tg, [128, 2 * 1024], BF16, sC)
            g.w("sync", "pe", xq_used[3])
            g.w("sync", "dv", tok_habs)
            g.w("sync", "dh0")
            g.w("sync", "dh1")
            if wstore_t is not None:
                g.w("sync", "dt", g.semv["dt"])
            for ic in range(ICH):
                g.dma("sync", wdw[:, C * ic:C * (ic + 1)], wdb[ic], sem="dm0")
            tok_sh = global_max_scale(tok_habs, 1)
            g.w("vector", "ac", tok_sh)
            g.w("scalar", "ac", tok_sh)
            g.w("tensor", "dm0")
            hq_used = {}
            hq_tok = {}

            def c_load(gi):
                slot = gi % 2
                hb = slot * ICH * 256
                if gi >= 2:
                    g.w("sync", "pe", hq_used[gi - 2])
                sc = "dc%d" % (gi % 2)
                g.dma("sync", hqb[:, hb:hb + 11 * 256],
                      hsp.rearrange("ic p j -> p ic j")
                      [:, 0:11, 256 * gi:256 * (gi + 1)], sem=sc)
                g.dma("sync", hqb[:, hb + 11 * 256:hb + ICH * 256],
                      hsp.rearrange("ic p j -> p ic j")
                      [:, 11:ICH, 256 * gi:256 * (gi + 1)], sem=sc)

            def c_quant(gi):
                slot = gi % 2
                hb = slot * ICH * 256
                sc = "dc%d" % (gi % 2)
                g.w("scalar", sc)
                for ic in range(ICH):
                    hsl = hqb[:, hb + 256 * ic:hb + 256 * (ic + 1)]
                    qs = 256 * (2 * (gi % 2) + ic % 2)
                    g.need("scalar", ("qc", gi % 2, ic % 2))
                    tok_a = g.im("scalar", "activation",
                                 out=qc[:, qs:qs + 256],
                                 in_=hsl, func=AF.Identity,
                                 scale=scb[:, 1:2], bias=magic_c[:, 0:1],
                                 sem="ac")
                    g.w("vector", "ac", tok_a)
                    g.i("vector", "tensor_scalar", out=qc[:, qs:qs + 256],
                        in0=qc[:, qs:qs + 256], scalar1=MAGIC, scalar2=127.0,
                        op0=ALU.subtract, op1=ALU.min)
                    tok_qq = g.im("vector", "tensor_scalar", out=hsl,
                                  in0=qc[:, qs:qs + 256], scalar1=-128.0,
                                  scalar2=None, op0=ALU.max, sem="dv")
                    g.done(("qc", gi % 2, ic % 2), "dv", tok_qq)
                hq_tok[gi] = g.semv["dv"]

            c_load(0)
            c_load(1)
            c_quant(0)
            for gi in range(16):
                slot = gi % 2
                hb = slot * ICH * 256
                if gi + 1 < 16:
                    c_quant(gi + 1)
                g.w("tensor", "dv", hq_tok[gi])
                last_mm = None
                for tc in range(2):
                    t = 2 * gi + tc
                    bA, bB = ((P0, P1), (P2, P3))[tc % 2]
                    g.need("tensor", ("PC", tc % 2))
                    for ic in range(ICH):
                        stat = hqb[:, hb + 256 * ic + 128 * tc:
                                   hb + 256 * ic + 128 * tc + 128]
                        g.im("tensor", "matmul", bA[:], stat,
                             wdw[:, C * ic:C * ic + 512],
                             start=(ic == 0), stop=(ic == ICH - 1), sem="pe")
                        last_mm = g.im("tensor", "matmul", bB[:], stat,
                                       wdw[:, C * ic + 512:C * ic + 1024],
                                       start=(ic == 0), stop=(ic == ICH - 1),
                                       sem="pe")
                    g.need("scalar", ("dsc", tc % 2))
                    g.w("scalar", "pe", last_mm)
                    g.i("scalar", "activation", out=dsc[:, 1024 * (tc % 2):
                                                       1024 * (tc % 2) + 512],
                        in_=bA[:], func=AF.Identity, scale=scb[:, 2:3])
                    tok_ev = g.im("scalar", "activation",
                                  out=dsc[:, 1024 * (tc % 2) + 512:
                                          1024 * (tc % 2) + 1024],
                                  in_=bB[:], func=AF.Identity,
                                  scale=scb[:, 2:3], sem="ac")
                    g.done(("PC", tc % 2), "ac", tok_ev)
                    g.w("vector", "ac", tok_ev)
                    xsl = xres[:, C * t:C * (t + 1)]
                    tok_add = g.im("vector", "tensor_tensor", out=xsl, in0=xsl,
                                   in1=dsc[:, 1024 * (tc % 2):
                                           1024 * (tc % 2) + 1024],
                                   op=ALU.add, sem="dv")
                    g.done(("dsc", tc % 2), "dv", tok_add)
                    if not final:
                        g.i("vector", "reduce_sum", out=st_sum[:, t:t + 1],
                            in_=xsl, axis=AX.X)
                        g.w("scalar", "dv", tok_add)
                        tok_sq2 = g.im("scalar", "activation",
                                       out=dsc[:, 1024 * (tc % 2):
                                               1024 * (tc % 2) + 1024],
                                       in_=xsl, func=AF.Square,
                                       accum_out=st_sq[:, t:t + 1], sem="ac")
                        g.done(("dsc", tc % 2), "ac", tok_sq2)
                hq_used[gi] = last_mm
                if gi + 2 < 16:
                    c_load(gi + 2)
                if final:
                    g.w("sync", "dv", tok_add)
                    for tc in range(2):
                        t = 2 * gi + tc
                        g.dma("sync", out_d[128 * t:128 * (t + 1), :],
                              xres[:, C * t:C * (t + 1)], sem="dt")
            g.barrier()
            sC.close()

        if stage == "dbg3":
            g.barrier()
            ffn_run(m_a, r_a, False, dbg3=True)
            blk = es.enter_context(nc.Block())
            g.finish(blk)
            return nc
        sVRO = es.enter_context(contextlib.ExitStack())
        vld = [g.sbuf("vld%d" % j2, [128, C], F32, sVRO) for j2 in range(2)]
        vqo = [g.sbuf("vqo%d" % j2, [128, C], BF16, sVRO) for j2 in range(2)]
        vst = [g.sbuf("vst%d" % j2, [128, C], BF16, sVRO) for j2 in range(2)]
        VCH = [(wv_in, wvb, "cb"), (wr_in, wrb, "cb"), (wo_in, wob, "cc")]
        VSEM = ("dx", "dcr")
        vtok = {}

        def vload(j):
            g.need("scalar", ("vld", j % 2))
            g.dma("scalar", vld[j % 2][:],
                  VCH[j // 8][0][128 * (j % 8):128 * (j % 8 + 1), :],
                  sem=VSEM[j % 2])

        def vro_item(it):
            if it < 24:
                g.need("scalar", ("vqo", it % 2))
                g.w("scalar", VSEM[it % 2])
                tok_c = g.im("scalar", "copy", out=vqo[it % 2][:],
                             in_=vld[it % 2][:], sem="ac")
                g.done(("vld", it % 2), "ac", tok_c)
                vtok[it] = tok_c
                if it + 2 < 24:
                    vload(it + 2)
            if it >= 1:
                j = it - 1
                g.w("tensor", "ac", vtok[j])
                prep_tail(j, VCH[j // 8][2], C, CCH, VCH[j // 8][1],
                          sr=j % 8, bufs=(vqo[j % 2], vst[j % 2]),
                          gk=("vqo", "vst"))

        it_ctr = [0]

        def vro_inject():
            if it_ctr[0] <= 24:
                vro_item(it_ctr[0])
                it_ctr[0] += 1

        g.w("scalar", "dt", wstore)
        vload(0)
        vload(1)
        def vro_close():
            assert it_ctr[0] > 24, it_ctr
            sVRO.close()

        ffn_run(m_a, r_a, False, wgate=wpair, wstore_t=wstore,
                eff_t=tok_eff, inject=vro_inject, post_b=vro_close)
        tok_2v = g.mark("vector", "dv")
        tok_2s = g.mark("scalar", "ac")
        g.w("vector", "dv", tok_2v)
        g.w("vector", "ac", tok_2s)
        ln_finalize(m_b, r_b, False)

        if stage in ("ffn1", "dbg2"):
            g.barrier()
            for t in range(NT):
                g.dma("sync", out_d[128 * t:128 * (t + 1), :],
                      xres[:, C * t:C * (t + 1)])
            if stage == "dbg2":
                g.w("sync", "ds")
                sDBG = es.enter_context(contextlib.ExitStack())
                dbf = g.sbuf("dbf", [128, 1024], F32, sDBG)
                dbh = g.sbuf("dbh", [128, 1024], BF16, sDBG)
                g.dma("sync", dbh.bitcast(F16)[:], hsp[0][:, 0:1024])
                g.w("vector", "ds")
                g.i("vector", "tensor_copy", out=dbf[:], in_=dbh.bitcast(F16)[:])
                tok_d = g.mark("vector", "dv")
                g.w("sync", "dv", tok_d)
                g.dma("sync", out_d[3840:3968, :], dbf[:])
                g.w("vector", "ds")
                g.i("vector", "memset", dbf[:], 0.0)
                g.i("vector", "tensor_copy", out=dbf[:, 0:1], in_=scb[:, 0:1])
                g.i("vector", "tensor_copy", out=dbf[:, 1:2], in_=scb[:, 1:2])
                g.i("vector", "tensor_copy", out=dbf[:, 2:3], in_=scb[:, 2:3])
                g.i("vector", "tensor_copy", out=dbf[:, 3:4], in_=red_c[:])
                g.i("vector", "tensor_copy", out=dbf[:, 16:16 + NT], in_=m_b[:])
                g.i("vector", "tensor_copy", out=dbf[:, 48:48 + NT], in_=r_b[:])
                tok_d = g.mark("vector", "dv")
                g.w("sync", "dv", tok_d)
                g.dma("sync", out_d[3968:4096, :], dbf[:])
            g.w("sync", "ds")
            blk = es.enter_context(nc.Block())
            g.finish(blk)
            return nc

        g.barrier()
        sD = es.enter_context(contextlib.ExitStack())
        ln2Tb = [g.sbuf("ln2T%d" % j2, [128, CCH * 512], BF16, sD)
                 for j2 in range(2)]
        wvTs = g.sbuf("wvTs", [128, CCH * C], BF16, sD)
        wrTs = g.sbuf("wrTs", [128, CCH * C], BF16, sD)
        lnb = g.sbuf("lnb", [128, 4 * 1024], BF16, sD)
        vgrp = g.sbuf("vgrp", [128, 2 * 512], BF16, sD)
        rgrp = g.sbuf("rgrp", [128, 2 * 512], BF16, sD)
        bvsD = g.sbuf("bvsD", [128, 512], F32, sD)
        zjunk = g.sbuf("zjunk", [128, 512], F32, sD)
        bbD = g.sbuf("bbD", [128, 520], F32, sD)
        lnD = g.sbuf("lnD", [128, 512], F32, sD)
        rdD = g.sbuf("rdD", [128, 2 * 512], BF16, sD)
        e_t8 = g.sbuf("e_t8", [8, 128], F32, sD)
        e_a8 = g.sbuf("e_a8", [8, 128], F32, sD)
        e_p0 = g.sbuf("e_p0", [8, 128], F32, sD)
        e_p1 = g.sbuf("e_p1", [8, 128], F32, sD)
        e_p2 = g.sbuf("e_p2", [8, 128], F32, sD)
        e_c2 = g.sbuf("e_c2", [8, 128], F32, sD)
        e_c3 = g.sbuf("e_c3", [8, 128], F32, sD)
        e_s1 = g.sbuf("e_s1", [8, 128], F32, sD)
        e_s2 = g.sbuf("e_s2", [8, 128], F32, sD)

        for dc in range(CCH):
            g.dma("sync", wvTs[:, C * dc:C * (dc + 1)], wvb[dc], sem="dm0")
            g.dma("sync", wrTs[:, C * dc:C * (dc + 1)], wrb[dc], sem="dm0")
        g.w("tensor", "dm0")

        def bb_batch(bgi):
            bk1c, bk1n = (kap1a, kap1b) if bgi % 2 == 0 else (kap1b, kap1a)
            for dc in range(CCH):
                g.need("vector", "bbD")
                g.i("vector", "tensor_copy", out=bbD[:, 0:1],
                    in_=bk1c[:, dc:dc + 1])
                tok_tb = g.im("vector", "tensor_tensor_scan",
                              out=bbD[:, 1:513],
                              data0=alph[:, dc:dc + 1].broadcast_to([128, 512]),
                              data1=gbt[:, dc:dc + 1].broadcast_to([128, 512]),
                              initial=bk1c[:, dc:dc + 1],
                              op0=ALU.mult, op1=ALU.add, sem="dv")
                g.w("vector", "dv", tok_tb)
                g.i("vector", "tensor_copy", out=bk1n[:, dc:dc + 1],
                    in_=bbD[:, 512:513])
                tok_kk = g.mark("vector", "dv")
                g.w("scalar", "dv", tok_kk)
                tok_ln = g.im("scalar", "activation", out=lnD[:],
                              in_=bbD[:, 0:512], func=AF.Ln,
                              bias=dplus[:, dc:dc + 1], sem="ac")
                g.done("bbD", "ac", tok_ln)
                g.need("scalar", ("rdD", dc % 2))
                tok_rdd = g.im("scalar", "activation",
                               out=rdD[:, 512 * (dc % 2):512 * (dc % 2 + 1)],
                               in_=lnD[:], func=AF.Exp,
                               scale=neg1_c[:, 0:1], sem="ac")
                g.w("scalar", "ac", tok_rdd)
                g.dma("scalar", rdsp[dc][:, 512 * bgi:512 * (bgi + 1)],
                      rdD[:, 512 * (dc % 2):512 * (dc % 2 + 1)],
                      sem="dc%d" % (dc % 2))
                g.done(("rdD", dc % 2), "dc%d" % (dc % 2),
                       g.semv["dc%d" % (dc % 2)])

        def ln_write(gi2, tl):
            t = 4 * gi2 + tl
            xt = xres[:, C * t:C * (t + 1)]
            g.need("vector", "lnb")
            return g.im("vector", "tensor_scalar",
                        out=lnb[:, 1024 * tl:1024 * (tl + 1)], in0=xt,
                        scalar1=m_b[:, t:t + 1], scalar2=r_b[:, t:t + 1],
                        op0=ALU.subtract, op1=ALU.mult, sem="dv")

        lnb4tok = {}
        for tl in range(4):
            lnb4tok[0] = ln_write(0, tl)
        for gi in range(8):
            g.w("tensor", "dv", lnb4tok[gi])
            ln2T = ln2Tb[gi % 2]
            for cc in range(CCH):
                PTx = (PT0, PT1)[cc % 2]
                g.need("tensor", ("PT", cc % 2))
                tok_tr = None
                for tj in range(4):
                    tok_tr = g.im("tensor", "transpose",
                                  PTx[:, 128 * tj:128 * tj + 128],
                                  lnb[:, 1024 * tj + 128 * cc:
                                      1024 * tj + 128 * cc + 128],
                                  eyeb[:, :], sem="pe")
                dst = ln2T[:, 512 * cc:512 * (cc + 1)]
                g.need("vector" if cc % 2 == 0 else "scalar",
                       ("ln2T", gi % 2))
                if cc % 2 == 0:
                    g.w("vector", "pe", tok_tr)
                    tok_e = g.im("vector", "tensor_copy", out=dst,
                                 in_=PTx[:, 0:512], sem="dv")
                    g.done(("PT", 0), "dv", tok_e)
                else:
                    g.w("scalar", "pe", tok_tr)
                    tok_e = g.im("scalar", "copy", out=dst,
                                 in_=PTx[:, 0:512], sem="ac")
                    g.done(("PT", 1), "ac", tok_e)
            g.done("lnb", "pe", g.semv["pe"])
            tok_lv = g.mark("vector", "dv")
            tok_ls = g.mark("scalar", "ac")
            g.w("tensor", "dv", tok_lv)
            g.w("tensor", "ac", tok_ls)
            last_r = None
            for dc in range(CCH):
                if gi + 1 < 8 and 1 <= dc <= 4:
                    lnb4tok[gi + 1] = ln_write(gi + 1, dc - 1)
                bank = (P0, P1)[dc % 2]
                g.need("tensor", ("PVa", dc % 2))
                g.need("tensor", ("PVv", dc % 2))
                last = None
                for k in range(CCH):
                    last = g.im("tensor", "matmul", bank[:],
                                wvTs[:, C * dc + 128 * k:C * dc + 128 * (k + 1)],
                                ln2T[:, 512 * k:512 * (k + 1)],
                                start=(k == 0), stop=(k == CCH - 1), sem="pe")
                g.need("scalar", ("vgrp", dc % 2))
                g.w("scalar", "pe", last)
                tok_ev = g.im("scalar", "copy",
                              out=vgrp[:, 512 * (dc % 2):512 * (dc % 2 + 1)],
                              in_=bank[:], sem="ac")
                g.done(("PVa", dc % 2), "ac", tok_ev)
                g.need("vector", "bvsD")
                g.w("vector", "pe", last)
                g.w("vector", "ac", tok_ev)
                tok_bv = g.im("vector", "tensor_scalar", out=bvsD[:],
                              in0=bank[:], scalar1=gbt[:, dc:dc + 1],
                              scalar2=None, op0=ALU.mult, sem="dv")
                g.done(("PVv", dc % 2), "dv", tok_bv)
                tok_ts1 = g.im("vector", "tensor_tensor_scan", out=zjunk[:],
                               data0=alph[:, dc:dc + 1].broadcast_to([128, 512]),
                               data1=bvsD[:], initial=Epart[:, dc:dc + 1],
                               op0=ALU.mult, op1=ALU.add, sem="dv")
                g.w("vector", "dv", tok_ts1)
                tok_sc = g.im("vector", "tensor_copy",
                              out=Epart[:, dc:dc + 1],
                              in_=zjunk[:, 511:512], sem="dv")
                g.done("bvsD", "dv", tok_sc)
                bank2 = (P2, P3)[dc % 2]
                g.need("tensor", ("PR", dc % 2))
                last_r = None
                for k in range(CCH):
                    last_r = g.im("tensor", "matmul", bank2[:],
                                  wrTs[:, C * dc + 128 * k:C * dc + 128 * (k + 1)],
                                  ln2T[:, 512 * k:512 * (k + 1)],
                                  start=(k == 0), stop=(k == CCH - 1), sem="pe")
                g.need("scalar", ("rgrp", dc % 2))
                g.w("scalar", "pe", last_r)
                tok_r = g.im("scalar", "activation",
                             out=rgrp[:, 512 * (dc % 2):512 * (dc % 2 + 1)],
                             in_=bank2[:], func=AF.Sigmoid, sem="ac")
                g.done(("PR", dc % 2), "ac", tok_r)
                g.w("sync", "ac", tok_r)
                sh = "dh%d" % (dc % 2)
                g.dma("sync", vsp[dc][:, 512 * gi:512 * (gi + 1)],
                      vgrp[:, 512 * (dc % 2):512 * (dc % 2 + 1)], sem=sh)
                g.dma("sync", rsp[dc][:, 512 * gi:512 * (gi + 1)],
                      rgrp[:, 512 * (dc % 2):512 * (dc % 2 + 1)], sem=sh)
                g.done(("vgrp", dc % 2), sh, g.semv[sh])
                g.done(("rgrp", dc % 2), sh, g.semv[sh])
            g.done(("ln2T", gi % 2), "pe", last_r)
            if gi >= 1:
                bb_batch(gi - 1)

        if stage == "dloc":
            g.barrier()
            for t in range(NT):
                g.dma("sync", out_d[128 * t:128 * (t + 1), :],
                      xres[:, C * t:C * (t + 1)])
            g.w("sync", "ds")
            blk = es.enter_context(nc.Block())
            g.finish(blk)
            return nc

        bb_batch(7)

        tok_ep = g.mark("vector", "dv")
        g.w("tensor", "dv", tok_ep)
        g.need("tensor", "PS")
        g.i("tensor", "transpose", PS[0:8, 0:128], Epart[:, :], eye32[:, :])
        tok_tr = g.im("tensor", "transpose", PS[0:8, 128:256], a4096[:, :],
                      eye32[:, :], sem="pe")
        g.w("scalar", "pe", tok_tr)
        g.i("scalar", "copy", out=e_t8[:, :], in_=PS[0:8, 0:128])
        tok_e8 = g.im("scalar", "copy", out=e_a8[:, :], in_=PS[0:8, 128:256],
                      sem="ac")
        g.done("PS", "ac", tok_e8)
        g.w("gpsimd", "ac", tok_e8)
        g.dma("gpsimd", ag_i, e_t8[:, :])
        g.w("gpsimd", "dg")
        tok_ag = g.im("gpsimd", "collective_compute", "AllGather", ALU.bypass,
                      replica_groups=[[0, 1, 2, 3], [4, 5, 6, 7]],
                      ins=[ag_i], outs=[ag_o], sem="gp")
        g.w("gpsimd", "gp", tok_ag)
        g.dma("gpsimd", e_p0[:, :], ag_o[0:8, :])
        g.dma("gpsimd", e_p1[:, :], ag_o[8:16, :])
        g.dma("gpsimd", e_p2[:, :], ag_o[16:24, :])
        g.w("vector", "dg", g.semv["dg"])
        g.i("vector", "tensor_mul", out=e_c2[:], in0=e_a8[:], in1=e_p0[:])
        g.i("vector", "tensor_add", out=e_c2[:], in0=e_c2[:], in1=e_p1[:])
        g.i("vector", "tensor_mul", out=e_c3[:], in0=e_a8[:], in1=e_c2[:])
        g.i("vector", "tensor_add", out=e_c3[:], in0=e_c3[:], in1=e_p2[:])
        g.i("vector", "tensor_scalar", out=e_s1[:], in0=e_p0[:],
            scalar1=oh_c[0:8, 1:2], scalar2=None, op0=ALU.mult)
        g.i("vector", "scalar_tensor_tensor", out=e_s2[:], in0=e_c2[:],
            scalar=oh_c[0:8, 2:3], in1=e_s1[:], op0=ALU.mult, op1=ALU.add)
        tok_sel = g.im("vector", "scalar_tensor_tensor", out=e_s1[:],
                       in0=e_c3[:], scalar=oh_c[0:8, 3:4], in1=e_s2[:],
                       op0=ALU.mult, op1=ALU.add, sem="dv")
        g.w("tensor", "dv", tok_sel)
        tok_trb = g.im("tensor", "transpose", PS[:, 0:8], e_s1[:, :],
                       eye32[0:8, 0:8], sem="pe")
        g.w("scalar", "pe", tok_trb)
        tok_kap = g.im("scalar", "copy", out=kap[:, :], in_=PS[:, 0:8],
                       sem="ac")
        g.done("PS", "ac", tok_kap)
        g.barrier()
        sD.close()

        if stage == "dag":
            for t in range(NT):
                g.dma("sync", out_d[128 * t:128 * (t + 1), :],
                      xres[:, C * t:C * (t + 1)])
            g.w("sync", "ds")
            blk = es.enter_context(nc.Block())
            g.finish(blk)
            return nc

        sE = es.enter_context(contextlib.ExitStack())
        woTs = g.sbuf("woTs", [128, CCH * C], BF16, sE)
        vld = g.sbuf("vld", [128, 2 * CCH * 512], BF16, sE)
        rld = g.sbuf("rld", [128, CCH * 512], BF16, sE)
        rdld = g.sbuf("rdld", [128, CCH * 512], BF16, sE)
        rwg = g.sbuf("rwg", [128, CCH * 512], BF16, sE)
        aab = g.sbuf("aab", [128, 2 * 520], F32, sE)
        bvsE = g.sbuf("bvsE", [128, 2 * 512], F32, sE)
        numsE = g.sbuf("numsE", [128, 512], F32, sE)
        lnsE = g.sbuf("lnsE", [128, 512], F32, sE)
        wkvE = g.sbuf("wkvE", [128, 512], F32, sE)

        for dc in range(CCH):
            g.dma("sync", woTs[:, C * dc:C * (dc + 1)], wob[dc], sem="dm0")
        for gi in range(2):
            g.dma("sync", vld[:, CCH * 512 * gi:CCH * 512 * (gi + 1)],
                  vsp.rearrange("dc p j -> p dc j")
                  [:, :, 512 * gi:512 * (gi + 1)], sem="dc%d" % (gi % 2))
        g.dma("sync", rld[:, :],
              rsp.rearrange("dc p j -> p dc j")[:, :, 0:512], sem="dcr")
        g.dma("sync", rdld[:, :],
              rdsp.rearrange("dc p j -> p dc j")[:, :, 0:512], sem="dcr")
        g.w("tensor", "dm0")
        g.w("tensor", "dc0")
        g.w("tensor", "dc1")
        g.w("tensor", "dcr")

        rec_h1 = {}
        rec_h2 = {}
        for gi in range(8):
            slot = gi % 2
            vb = slot * CCH * 512
            g.w("vector", "dc%d" % (gi % 2))
            g.w("scalar", "dc%d" % (gi % 2))
            g.w("vector", "dcr", rec_h1.get(gi) or g.semv["dcr"])
            g.w("scalar", "dcr", rec_h1.get(gi) or g.semv["dcr"])
            for dc in range(CCH):
                dpar = dc % 2
                if dc == 4:
                    v2 = rec_h2.get(gi) or g.semv["dm1"]
                    g.w("vector", "dm1", v2)
                    g.w("scalar", "dm1", v2)
                if dc == 4 and gi + 1 < 8:
                    g.w("sync", "dv")
                    g.dma("sync", rld[:, 0:4 * 512],
                          rsp.rearrange("dc p j -> p dc j")
                          [:, 0:4, 512 * (gi + 1):512 * (gi + 2)], sem="dcr")
                    g.dma("sync", rdld[:, 0:4 * 512],
                          rdsp.rearrange("dc p j -> p dc j")
                          [:, 0:4, 512 * (gi + 1):512 * (gi + 2)], sem="dcr")
                    rec_h1[gi + 1] = g.semv["dcr"]
                vsl = vld[:, vb + 512 * dc:vb + 512 * (dc + 1)]
                rsl = rld[:, 512 * dc:512 * (dc + 1)]
                g.need("scalar", ("bvsE", dpar))
                tok_bv = g.im("scalar", "activation",
                              out=bvsE[:, 512 * dpar:512 * (dpar + 1)],
                              in_=vsl, func=AF.Identity,
                              scale=gbt[:, dc:dc + 1], sem="ac")
                ab = aab[:, 520 * dpar:520 * dpar + 520]
                g.need("vector", ("aab", dpar))
                g.i("vector", "tensor_copy", out=ab[:, 0:1],
                    in_=kap[:, dc:dc + 1])
                g.w("vector", "ac", tok_bv)
                tok_ta = g.im("vector", "tensor_tensor_scan", out=ab[:, 1:513],
                              data0=alph[:, dc:dc + 1].broadcast_to([128, 512]),
                              data1=bvsE[:, 512 * dpar:512 * (dpar + 1)],
                              initial=kap[:, dc:dc + 1], op0=ALU.mult,
                              op1=ALU.add, sem="dv")
                g.done(("bvsE", dpar), "dv", tok_ta)
                g.w("vector", "dv", tok_ta)
                g.i("vector", "tensor_copy", out=kap[:, dc:dc + 1],
                    in_=ab[:, 512:513])
                g.need("vector", "numsE")
                tok_nm = g.im("vector", "scalar_tensor_tensor",
                              out=numsE[:],
                              in0=vsl, scalar=delt[:, dc:dc + 1],
                              in1=ab[:, 0:512], op0=ALU.mult, op1=ALU.add,
                              sem="dv")
                g.done(("aab", dpar), "dv", tok_nm)
                g.need("vector", "wkvE")
                g.i("vector", "tensor_mul", out=wkvE[:],
                    in0=numsE[:],
                    in1=rdld[:, 512 * dc:512 * (dc + 1)])
                g.done("numsE", "dv", g.semv["dv"])
                g.need("vector", ("rwg", dc))
                tok_rw = g.im("vector", "tensor_mul",
                              out=rwg[:, 512 * dc:512 * (dc + 1)],
                              in0=rsl, in1=wkvE[:], sem="dv")
                g.done("wkvE", "dv", tok_rw)
            tok_rwall = g.semv["dv"]
            g.done(("vld", slot), "dv", tok_rwall)
            if gi + 2 < 8:
                g.need("sync", ("vld", slot))
                g.dma("sync", vld[:, vb:vb + CCH * 512],
                      vsp.rearrange("dc p j -> p dc j")
                      [:, :, 512 * (gi + 2):512 * (gi + 3)],
                      sem="dc%d" % (gi % 2))
            if gi + 1 < 8:
                g.w("sync", "dv", tok_rwall)
                g.dma("sync", rld[:, 4 * 512:CCH * 512],
                      rsp.rearrange("dc p j -> p dc j")
                      [:, 4:CCH, 512 * (gi + 1):512 * (gi + 2)], sem="dm1")
                g.dma("sync", rdld[:, 4 * 512:CCH * 512],
                      rdsp.rearrange("dc p j -> p dc j")
                      [:, 4:CCH, 512 * (gi + 1):512 * (gi + 2)], sem="dm1")
                rec_h2[gi + 1] = g.semv["dm1"]
            g.w("tensor", "dv", tok_rwall)
            for tc in range(4):
                t = 4 * gi + tc
                bA, bB = ((P0, P1), (P2, P3))[tc % 2]
                g.need("tensor", ("PO", tc % 2))
                last = None
                for dc in range(CCH):
                    stat = rwg[:, 512 * dc + 128 * tc:512 * dc + 128 * tc + 128]
                    g.im("tensor", "matmul", bA[:], stat,
                         woTs[:, C * dc:C * dc + 512],
                         start=(dc == 0), stop=(dc == CCH - 1), sem="pe")
                    last = g.im("tensor", "matmul", bB[:], stat,
                                woTs[:, C * dc + 512:C * dc + 1024],
                                start=(dc == 0), stop=(dc == CCH - 1),
                                sem="pe")
                xsl = xres[:, C * t:C * (t + 1)]
                g.w("vector", "pe", last)
                g.i("vector", "tensor_tensor", out=xsl[:, 0:512],
                    in0=xsl[:, 0:512], in1=bA[:], op=ALU.add)
                tok_add = g.im("vector", "tensor_tensor", out=xsl[:, 512:1024],
                               in0=xsl[:, 512:1024], in1=bB[:], op=ALU.add,
                               sem="dv")
                g.done(("PO", tc % 2), "dv", tok_add)
                g.i("vector", "reduce_max", out=st_max[:, t:t + 1], in_=xsl,
                    axis=AX.X)
                g.i("vector", "tensor_reduce", out=st_min[:, t:t + 1], in_=xsl,
                    axis=AX.X, op=ALU.min)
                g.w("scalar", "dv", tok_add)
                g.i("scalar", "activation", out=lnsE[:], in_=xsl[:, 0:512],
                    func=AF.Identity, accum_out=st_sum[:, t:t + 1])
                g.i("scalar", "activation", out=lnsE[:], in_=xsl[:, 512:1024],
                    func=AF.Identity, accum_out=m_b[:, t:t + 1])
                g.i("scalar", "activation", out=lnsE[:], in_=xsl[:, 0:512],
                    func=AF.Square, accum_out=st_sq[:, t:t + 1])
                g.im("scalar", "activation", out=lnsE[:],
                     in_=xsl[:, 512:1024], func=AF.Square,
                     accum_out=stt_sc[:, t:t + 1], sem="ac")
            for dc in range(CCH):
                g.done(("rwg", dc), "pe", g.semv["pe"])

        tok_3v = g.mark("vector", "dv")
        tok_3s = g.mark("scalar", "ac")
        g.w("vector", "dv", tok_3v)
        g.w("vector", "ac", tok_3s)
        g.i("vector", "tensor_add", out=st_sum[:], in0=st_sum[:], in1=m_b[:])
        g.i("vector", "tensor_add", out=st_sq[:], in0=st_sq[:], in1=stt_sc[:])
        t_r3, t_am3 = ln_finalize(m_a, r_a, True)

        if stage == "attn":
            g.barrier()
            for t in range(NT):
                g.dma("sync", out_d[128 * t:128 * (t + 1), :],
                      xres[:, C * t:C * (t + 1)])
            g.w("sync", "ds")
            blk = es.enter_context(nc.Block())
            g.finish(blk)
            return nc

        tok_sx2 = global_max_scale(t_am3, 0)
        g.w("vector", "ac", tok_sx2)
        g.i("vector", "tensor_scalar", out=eff_g[:], in0=s_wg[:],
            scalar1=scb[:, 0:1], scalar2=None, op0=ALU.mult)
        g.im("vector", "tensor_scalar", out=eff_u[:], in0=s_wu[:],
             scalar1=scb[:, 0:1], scalar2=None, op0=ALU.mult, sem="dv")
        g.barrier()
        sE.close()
        ffn_run(m_a, r_a, True)
        g.w("sync", "ds")
        g.w("gpsimd", "dg")
        blk = es.enter_context(nc.Block())
        g.finish(blk)
        return nc


_NC_CACHE = {}


def kernel(L0_attn_x, L0_attn_time_decay, L0_attn_time_first, L0_attn_key_w,
           L0_attn_value_w, L0_attn_recep_w, L0_attn_out_w, L0_ffn_clip_x,
           L0_ffn_clip_down, L0_ffn_w_gate, L0_ffn_w_up, L0_ffn_w_down):
    x = np.ascontiguousarray(np.asarray(L0_attn_x, dtype=np.float32))
    td = np.ascontiguousarray(
        np.asarray(L0_attn_time_decay, np.float32).reshape(CCH, 128))
    tf = np.ascontiguousarray(
        np.asarray(L0_attn_time_first, np.float32).reshape(CCH, 128))
    wv = np.ascontiguousarray(np.asarray(L0_attn_value_w, np.float32))
    wr = np.ascontiguousarray(np.asarray(L0_attn_recep_w, np.float32))
    wo = np.ascontiguousarray(np.asarray(L0_attn_out_w, np.float32))
    wg = np.ascontiguousarray(np.asarray(L0_ffn_w_gate, np.float32))
    wu = np.ascontiguousarray(np.asarray(L0_ffn_w_up, np.float32))
    wd = np.ascontiguousarray(np.asarray(L0_ffn_w_down, np.float32))
    clips = np.ascontiguousarray(
        np.stack([np.asarray(L0_ffn_clip_x, np.float32).reshape(1),
                  np.asarray(L0_ffn_clip_down, np.float32).reshape(1)],
                 axis=1))
    eye = np.eye(128, dtype=np.float32)

    if STAGE not in _NC_CACHE:
        _NC_CACHE[STAGE] = build_program(STAGE)
    nc = _NC_CACHE[STAGE]

    in_maps = []
    for r in range(8):
        b, q = r // 4, r % 4
        oh = np.zeros((4, 1), np.float32)
        oh[q, 0] = 1.0
        in_maps.append({
            "x": np.ascontiguousarray(x[b, TOK * q:TOK * (q + 1), :]),
            "w_gate": wg, "w_up": wu, "w_down": wd,
            "w_val": wv, "w_rec": wr, "w_out": wo,
            "tdec": td, "tfir": tf, "clips": clips,
            "eye": eye, "onehot": oh,
        })
    res = run_bass_kernel_spmd(nc, in_maps, core_ids=list(range(8)))
    out = np.empty((B, T, C), np.float32)
    for r in range(8):
        b, q = r // 4, r % 4
        out[b, TOK * q:TOK * (q + 1), :] = res.results[r]["out"]
    return out


# version 15 (final)
# speedup vs baseline: 1.0139x; 1.0139x over previous
import contextlib
import numpy as np
import ml_dtypes
import concourse.bass as bass
import concourse.mybir as mybir
from concourse.bass_utils import run_bass_kernel_spmd

F32 = mybir.dt.float32
BF16 = mybir.dt.bfloat16
F16 = mybir.dt.float16
AF = mybir.ActivationFunctionType
ALU = mybir.AluOpType
AX = mybir.AxisListType

MAGIC = float(np.float32(1.5 * 2**23))
QMAX = 127.0

B, T, C, I = 2, 16384, 1024, 2816
TOK = 4096
NT = TOK // 128
CCH = C // 128
ICH = I // 128
STAGE = "full"

ENGS = ("sync", "vector", "scalar", "tensor", "gpsimd")


class Bld:
    def __init__(self, nc):
        self.nc = nc
        self.prog = {e: [] for e in ENGS}
        self.es = contextlib.ExitStack()
        self.semh = {}
        self.semv = {}
        self.guard = {}

    def sem(self, name):
        self.semh[name] = self.es.enter_context(self.nc.semaphore(name))
        self.semv[name] = 0

    def sbuf(self, name, shape, dt, scope=None):
        scope = scope if scope is not None else self.es
        return scope.enter_context(self.nc.sbuf_tensor(name, shape, dt))

    def psum(self, name, shape, dt):
        return self.es.enter_context(self.nc.psum_tensor(name, shape, dt))

    def w(self, eng, sem, val=None):
        v = self.semv[sem] if val is None else val
        if v <= 0:
            return v
        h = self.semh[sem]
        self.prog[eng].append([lambda E: E.wait_ge(h, v), None, False])
        return v

    SELF = {"vector": "dv", "scalar": "ac", "gpsimd": "gp"}

    def _emit(self, eng, meth, a, kw, extra_incs):
        if eng in self.SELF and meth != "dma_start":
            s = self.SELF[eng]
            h = self.semh[s]
            v0 = self.semv[s]
            if v0 > 0:
                self.prog[eng].append(
                    [lambda E: E.wait_ge(h, v0), None, False])
            self.semv[s] = v0 + 1
            incs = [(h, 1)] + list(extra_incs)
            self.prog[eng].append(
                [lambda E: getattr(E, meth)(*a, **kw), incs, True])
            return self.semv[s]
        self.prog[eng].append(
            [lambda E: getattr(E, meth)(*a, **kw), list(extra_incs),
             meth != "dma_start"])
        return None

    def i(self, eng, meth, *a, **kw):
        self._emit(eng, meth, a, kw, [])

    def im(self, eng, meth, *a, sem=None, amt=1, **kw):
        if eng in self.SELF and meth != "dma_start":
            assert sem == self.SELF[eng], (eng, sem)
            return self._emit(eng, meth, a, kw, [])
        h = self.semh[sem]
        self.semv[sem] += amt
        v = self.semv[sem]
        self._emit(eng, meth, a, kw, [(h, amt)])
        return v

    def mark(self, eng, sem, amt=1):
        if eng in self.SELF:
            assert sem == self.SELF[eng], (eng, sem)
            return self.semv[sem]
        for ent in reversed(self.prog[eng]):
            if ent[2]:
                if not ent[1]:
                    h = self.semh[sem]
                    self.semv[sem] += amt
                    ent[1].append((h, amt))
                    return self.semv[sem]
                break
        return self.im(eng, "nop", sem=sem, amt=amt)

    DMA_SEMS = ("ds", "dg", "dx", "dw0", "dw1", "dm0", "dm1",
                "dc0", "dc1", "dcr", "dh0", "dh1", "dt")

    def dma(self, eng, out, in_, sem=None, **kw):
        if sem is None:
            sem = "dg" if eng == "gpsimd" else "ds"
        return self.im(eng, "dma_start", sem=sem, amt=16, out=out, in_=in_, **kw)

    def need(self, eng, key):
        tk = self.guard.get(key)
        if tk is not None:
            self.w(eng, tk[0], tk[1])

    def done(self, key, sem, val):
        self.guard[key] = (sem, val)

    def barrier(self):
        for s in self.DMA_SEMS:
            if s == "dg":
                self.w("gpsimd", s)
            else:
                self.w("sync", s)
        toks = {}
        for e in ENGS:
            if e == "gpsimd":
                self.i(e, "memset", self.barsc_g[0:1, 0:1], 0.0)
                toks[e] = ("gp", self.semv["gp"])
            elif e in self.SELF:
                toks[e] = (self.SELF[e], self.semv[self.SELF[e]])
            elif e == "tensor":
                toks[e] = ("btensor", self.mark(e, "btensor"))
            else:
                toks[e] = ("bsync", self.im(e, "nop", sem="bsync", amt=1))
        for e in ENGS:
            for e2 in ENGS:
                if e2 != e:
                    self.w(e, toks[e2][0], toks[e2][1])

    def finish(self, blk):
        for e in ENGS:
            ops = self.prog[e]

            def runner(E, ops=ops):
                for fn, incs, _ in ops:
                    ins = fn(E)
                    if incs:
                        for h, amt in incs:
                            ins = ins.then_inc(h, amt)
            
            getattr(blk, e)(runner)


def build_program(stage=None):
    stage = stage or STAGE
    nc = bass.Bass("TRN2", target_bir_lowering=False, debug=False)
    g = Bld(nc)

    x_in = nc.dram_tensor("x", [TOK, C], F32, kind="ExternalInput").ap()
    wg_in = nc.dram_tensor("w_gate", [I, C], F32, kind="ExternalInput").ap()
    wu_in = nc.dram_tensor("w_up", [I, C], F32, kind="ExternalInput").ap()
    wd_in = nc.dram_tensor("w_down", [C, I], F32, kind="ExternalInput").ap()
    wv_in = nc.dram_tensor("w_val", [C, C], F32, kind="ExternalInput").ap()
    wr_in = nc.dram_tensor("w_rec", [C, C], F32, kind="ExternalInput").ap()
    wo_in = nc.dram_tensor("w_out", [C, C], F32, kind="ExternalInput").ap()
    td_in = nc.dram_tensor("tdec", [CCH, 128], F32, kind="ExternalInput").ap()
    tf_in = nc.dram_tensor("tfir", [CCH, 128], F32, kind="ExternalInput").ap()
    clip_in = nc.dram_tensor("clips", [1, 2], F32, kind="ExternalInput").ap()
    eye_in = nc.dram_tensor("eye", [128, 128], F32, kind="ExternalInput").ap()
    oh_in = nc.dram_tensor("onehot", [4, 1], F32, kind="ExternalInput").ap()
    out_d = nc.dram_tensor("out", [TOK, C], F32, kind="ExternalOutput").ap()

    wgb = nc.dram_tensor("wgb", [ICH, 128, C], BF16, kind="Internal").ap()
    wub = nc.dram_tensor("wub", [ICH, 128, C], BF16, kind="Internal").ap()
    wdb = nc.dram_tensor("wdb", [ICH, 128, C], F16, kind="Internal").ap()
    wvb = nc.dram_tensor("wvb", [CCH, 128, C], BF16, kind="Internal").ap()
    wrb = nc.dram_tensor("wrb", [CCH, 128, C], BF16, kind="Internal").ap()
    wob = nc.dram_tensor("wob", [CCH, 128, C], BF16, kind="Internal").ap()
    hsp = nc.dram_tensor("hsp", [ICH, 128, TOK], F16, kind="Internal").ap()
    vsp = nc.dram_tensor("vsp", [CCH, 128, TOK], BF16, kind="Internal").ap()
    rsp = nc.dram_tensor("rsp", [CCH, 128, TOK], BF16, kind="Internal").ap()
    rdsp = nc.dram_tensor("rdsp", [CCH, 128, TOK], BF16, kind="Internal").ap()
    mx_i = nc.dram_tensor("mx_i", [1, 8], F32, kind="Internal").ap()
    mx_o = nc.dram_tensor("mx_o", [1, 8], F32, kind="Internal").ap()
    ag_i = nc.dram_tensor("ag_i", [8, 128], F32, kind="Internal").ap()
    ag_o = nc.dram_tensor("ag_o", [32, 128], F32, kind="Internal").ap()
    col_d = nc.dram_tensor("col_d", [1, 128], F32, kind="Internal").ap()
    sc_d = nc.dram_tensor("sc_d", [1, 4], F32, kind="Internal").ap()

    es = g.es
    with es:
        for s in Bld.DMA_SEMS + ("pe", "dv", "ac", "gp"):
            g.sem(s)
        for e in ENGS:
            g.sem("b" + e)

        xres = g.sbuf("xres", [128, NT * C], F32)
        eye32 = g.sbuf("eye32", [128, 128], F32)
        eyeb = g.sbuf("eyeb", [128, 128], BF16)
        eyeh = g.sbuf("eyeh", [128, 128], F16)
        st_sum = g.sbuf("st_sum", [128, NT], F32)
        st_sq = g.sbuf("st_sq", [128, NT], F32)
        st_max = g.sbuf("st_max", [128, NT], F32)
        st_min = g.sbuf("st_min", [128, NT], F32)
        m_a = g.sbuf("m_a", [128, NT], F32)
        r_a = g.sbuf("r_a", [128, NT], F32)
        m_b = g.sbuf("m_b", [128, NT], F32)
        r_b = g.sbuf("r_b", [128, NT], F32)
        stt_sc = g.sbuf("stt_sc", [128, NT], F32)
        s_wg = g.sbuf("s_wg", [128, ICH], F32)
        s_wu = g.sbuf("s_wu", [128, ICH], F32)
        s_wd = g.sbuf("s_wd", [128, CCH], F32)
        eff_g = g.sbuf("eff_g", [128, ICH], F32)
        eff_u = g.sbuf("eff_u", [128, ICH], F32)
        alph = g.sbuf("alph", [128, CCH], F32)
        gbt = g.sbuf("gbt", [128, CCH], F32)
        delt = g.sbuf("delt", [128, CCH], F32)
        dplus = g.sbuf("dplus", [128, CCH], F32)
        a512 = g.sbuf("a512", [128, CCH], F32)
        a4096 = g.sbuf("a4096", [128, CCH], F32)
        kap = g.sbuf("kap", [128, CCH], F32)
        kap1a = g.sbuf("kap1a", [128, CCH], F32)
        kap1b = g.sbuf("kap1b", [128, CCH], F32)
        z512 = g.sbuf("z512", [128, CCH], F32)
        at1 = g.sbuf("at1", [128, CCH], F32)
        at2 = g.sbuf("at2", [128, CCH], F32)
        at3 = g.sbuf("at3", [128, CCH], F32)
        td_t = g.sbuf("td_t", [128, CCH], F32)
        tf_t = g.sbuf("tf_t", [128, CCH], F32)
        ones8 = g.sbuf("ones8", [128, CCH], F32)
        zc = g.sbuf("zc", [128, CCH], F32)
        acd = g.sbuf("acd", [128, CCH], F32)
        Epart = g.sbuf("Epart", [128, CCH], F32)
        magic_c = g.sbuf("magic_c", [128, 1], F32)
        scb = g.sbuf("scb", [128, 4], F32)
        ones_row = g.sbuf("ones_row", [1, 128], F32)
        neg1_c = g.sbuf("neg1_c", [128, 1], F32)
        zro_c = g.sbuf("zro_c", [128, 1], F32)
        one_c = g.sbuf("one_c", [128, 1], F32)
        red_c = g.sbuf("red_c", [128, 1], F32)
        oh_c = g.sbuf("oh_c", [128, 4], F32)
        sc_t = g.sbuf("sc_t", [1, 8], F32)
        sc_t2 = g.sbuf("sc_t2", [1, 8], F32)
        row128 = g.sbuf("row128", [1, 128], F32)
        clip_s = g.sbuf("clip_s", [1, 2], F32)
        g.barsc_v = g.sbuf("barsc_v", [1, 2], F32)
        g.barsc_s = g.sbuf("barsc_s", [1, 2], F32)
        g.barsc_g = g.sbuf("barsc_g", [1, 2], F32)

        P0 = g.psum("P0", [128, 512], F32)
        P1 = g.psum("P1", [128, 512], F32)
        P2 = g.psum("P2", [128, 512], F32)
        P3 = g.psum("P3", [128, 512], F32)
        PT0 = g.psum("PT0", [128, 1024], BF16)
        PT1 = g.psum("PT1", [128, 1024], BF16)
        PT2 = g.psum("PT2", [128, 1024], BF16)
        PS = g.psum("PS", [128, 512], F32)

        sW = es.enter_context(contextlib.ExitStack())
        td8 = g.sbuf("td8", [8, 128], F32, sW)
        tf8 = g.sbuf("tf8", [8, 128], F32, sW)
        z1loc = g.sbuf("z1loc", [128, 512], F32, sW)
        qsq = g.sbuf("qsq", [128, 1024], F32, sW)
        wq1 = g.sbuf("wq1", [128, I], F32, sW)
        wq1b = g.sbuf("wq1b", [128, I], F32, sW)
        wqo = g.sbuf("wqo", [128, I], BF16, sW)
        wqob = g.sbuf("wqob", [128, I], BF16, sW)
        wst = g.sbuf("wst", [128, I], BF16, sW)
        wstb = g.sbuf("wstb", [128, I], BF16, sW)
        winv = g.sbuf("winv", [128, 2], F32, sW)

        g.dma("sync", eye32[:], eye_in)
        g.dma("sync", clip_s[:], clip_in)
        tok_eye = g.semv["ds"]
        g.dma("sync", td8[:], td_in)
        g.dma("sync", tf8[:], tf_in)
        tok_tdtf = g.semv["ds"]
        for p in range(4):
            g.dma("sync", oh_c[:, p:p + 1],
                  oh_in[p:p + 1, 0:1].broadcast_to([128, 1]))
        tok_oh = g.semv["ds"]
        for t in range(NT):
            g.dma("sync", xres[:, C * t:C * (t + 1)],
                  x_in[128 * t:128 * (t + 1), :], sem="dx")

        g.i("vector", "memset", magic_c[:], MAGIC)
        g.i("vector", "memset", zro_c[:], 0.0)
        g.i("vector", "memset", one_c[:], 1.0)
        g.i("vector", "memset", neg1_c[:], -1.0)
        g.i("vector", "memset", ones8[:], 1.0)
        g.i("vector", "memset", ones_row[:], 1.0)
        g.i("vector", "memset", sc_t[:], 0.0)
        g.i("vector", "memset", Epart[:], 0.0)
        g.w("vector", "ds")
        g.i("vector", "tensor_copy", out=eyeb[:], in_=eye32[:])
        tok_eyes = g.im("vector", "tensor_copy", out=eyeh[:], in_=eye32[:], sem="dv")

        g.w("tensor", "ds")
        g.w("tensor", "dv", tok_eyes)
        g.i("tensor", "transpose", PS[:, 0:8], td8[:, :], eye32[0:8, 0:8])
        tok_tdT = g.im("tensor", "transpose", PS[:, 8:16], tf8[:, :],
                       eye32[0:8, 0:8], sem="pe")
        g.w("scalar", "pe", tok_tdT)
        g.i("scalar", "copy", out=td_t[:], in_=PS[:, 0:8])
        tok_tft = g.im("scalar", "copy", out=tf_t[:], in_=PS[:, 8:16], sem="ac")
        g.done("PS", "ac", tok_tft)
        g.w("vector", "ac", tok_tft)
        g.i("vector", "tensor_scalar", out=at1[:], in0=td_t[:], scalar1=0.0,
            scalar2=None, op0=ALU.max)
        g.i("vector", "tensor_scalar", out=at2[:], in0=at1[:], scalar1=-1.0,
            scalar2=-87.0, op0=ALU.mult, op1=ALU.max)
        tok_v = g.im("vector", "tensor_scalar", out=at3[:], in0=td_t[:],
                     scalar1=0.0, scalar2=-87.0, op0=ALU.min, op1=ALU.max,
                     sem="dv")
        g.w("scalar", "dv", tok_v)
        g.i("scalar", "activation", out=alph[:], in_=at2[:], func=AF.Exp)
        tok_be = g.im("scalar", "activation", out=gbt[:], in_=at3[:],
                      func=AF.Exp, sem="ac")
        g.w("vector", "ac", tok_be)
        g.i("vector", "tensor_scalar", out=at2[:], in0=tf_t[:], scalar1=0.0,
            scalar2=None, op0=ALU.max)
        g.i("vector", "tensor_scalar", out=at2[:], in0=at2[:], scalar1=-1.0,
            scalar2=-87.0, op0=ALU.mult, op1=ALU.max)
        tok_v = g.im("vector", "tensor_scalar", out=at3[:], in0=tf_t[:],
                     scalar1=0.0, scalar2=-87.0, op0=ALU.min, op1=ALU.max,
                     sem="dv")
        g.w("scalar", "dv", tok_v)
        g.i("scalar", "activation", out=delt[:], in_=at3[:], func=AF.Exp)
        tok_ga = g.im("scalar", "activation", out=at3[:], in_=at2[:],
                      func=AF.Exp, sem="ac")
        g.w("vector", "ac", tok_ga)
        g.i("vector", "tensor_mul", out=gbt[:], in0=gbt[:], in1=at3[:])
        g.i("vector", "tensor_scalar", out=dplus[:], in0=delt[:], scalar1=1e-6,
            scalar2=None, op0=ALU.add)
        tok_v = g.im("vector", "tensor_scalar", out=at2[:], in0=at1[:],
                     scalar1=-512.0, scalar2=-87.0, op0=ALU.mult, op1=ALU.max,
                     sem="dv")
        g.w("scalar", "dv", tok_v)
        tok_a5 = g.im("scalar", "activation", out=a512[:], in_=at2[:],
                      func=AF.Exp, sem="ac")
        g.w("vector", "ac", tok_a5)
        for dc in range(CCH):
            tok_zs = g.im("vector", "tensor_tensor_scan", out=z1loc[:],
                          data0=alph[:, dc:dc + 1].broadcast_to([128, 512]),
                          data1=gbt[:, dc:dc + 1].broadcast_to([128, 512]),
                          initial=0.0, op0=ALU.mult, op1=ALU.add, sem="dv")
            g.w("vector", "dv", tok_zs)
            g.i("vector", "tensor_copy", out=z512[:, dc:dc + 1],
                in_=z1loc[:, 511:512])
        g.i("vector", "tensor_copy", out=zc[:], in_=z512[:])
        g.i("vector", "tensor_copy", out=acd[:], in_=a512[:])
        for _ in range(3):
            g.i("vector", "tensor_scalar", out=at1[:], in0=acd[:], scalar1=1.0,
                scalar2=None, op0=ALU.add)
            g.i("vector", "tensor_mul", out=at2[:], in0=zc[:], in1=at1[:])
            g.i("vector", "tensor_copy", out=zc[:], in_=at2[:])
            g.i("vector", "tensor_mul", out=at3[:], in0=acd[:], in1=acd[:])
            g.i("vector", "tensor_copy", out=acd[:], in_=at3[:])
        g.i("vector", "tensor_copy", out=a4096[:], in_=acd[:])
        g.w("vector", "ds")
        g.i("vector", "tensor_scalar", out=at1[:], in0=a4096[:], scalar1=1.0,
            scalar2=None, op0=ALU.add)
        g.i("vector", "tensor_mul", out=at2[:], in0=a4096[:], in1=at1[:])
        g.i("vector", "tensor_scalar", out=at2[:], in0=at2[:], scalar1=1.0,
            scalar2=None, op0=ALU.add)
        g.i("vector", "tensor_scalar", out=at3[:], in0=ones8[:],
            scalar1=oh_c[:, 1:2], scalar2=None, op0=ALU.mult)
        g.i("vector", "scalar_tensor_tensor", out=kap1b[:], in0=at1[:],
            scalar=oh_c[:, 2:3], in1=at3[:], op0=ALU.mult, op1=ALU.add)
        g.i("vector", "scalar_tensor_tensor", out=at3[:], in0=at2[:],
            scalar=oh_c[:, 3:4], in1=kap1b[:], op0=ALU.mult, op1=ALU.add)
        g.im("vector", "tensor_mul", out=kap1a[:], in0=zc[:], in1=at3[:],
             sem="dv")

        g.w("vector", "dx")
        g.w("scalar", "dx")
        for t in range(NT):
            xt = xres[:, C * t:C * (t + 1)]
            g.i("vector", "reduce_sum", out=st_sum[:, t:t + 1], in_=xt, axis=AX.X)
            g.i("vector", "reduce_max", out=st_max[:, t:t + 1], in_=xt, axis=AX.X)
            g.i("vector", "tensor_reduce", out=st_min[:, t:t + 1], in_=xt,
                axis=AX.X, op=ALU.min)
            g.i("scalar", "activation", out=qsq[:], in_=xt, func=AF.Square,
                accum_out=st_sq[:, t:t + 1])
        tok_av = g.mark("vector", "dv")
        tok_as = g.mark("scalar", "ac")

        def prep_tail(rc, mode, ncols, nb, dst_blk, sr=None, bufs=None,
                      gk=("wqo", "wst")):
            sr = rc if sr is None else sr
            f16 = (mode == "qd")
            qvx, wstx = bufs if bufs else ((wqo, wqob)[rc % 2],
                                           (wst, wstb)[rc % 2])
            qv = qvx.bitcast(F16) if f16 else qvx
            sv_ = wstx.bitcast(F16) if f16 else wstx
            for kb in range(nb):
                off = (kb % 4) * 128
                PTx = (PT0, PT1)[(kb // 4) % 2]
                PTv = PTx.bitcast(F16) if f16 else PTx
                g.need("tensor", ("PT", (kb // 4) % 2))
                tok_tr = g.im("tensor", "transpose", PTv[:, off:off + 128],
                              qv[:, 128 * kb:128 * (kb + 1)],
                              (eyeh if f16 else eyeb)[:, :], sem="pe")
                if (kb % 4 == 3) or kb == nb - 1:
                    lo = (kb // 4) * 512
                    width = off + 128
                    g.w("scalar", "dt")
                    g.w("scalar", "pe", tok_tr)
                    tok_ev = g.im("scalar", "copy",
                                  out=sv_[:, lo:lo + width],
                                  in_=PTv[:, 0:width], sem="ac")
                    g.done(("PT", (kb // 4) % 2), "ac", tok_ev)
            g.done((gk[0], rc % 2), "pe", g.semv["pe"])
            g.w("scalar", "ac")
            if mode in ("qi", "cb"):
                g.dma("scalar", dst_blk[sr], sv_[:, 0:ncols], sem="dt")
            else:
                for kb in range(nb):
                    g.dma("scalar", dst_blk[kb][:, 128 * sr:128 * (sr + 1)],
                          sv_[:, 128 * kb:128 * (kb + 1)], sem="dt")
            g.done((gk[1], rc % 2), "dt", g.semv["dt"])

        L6 = ("dw0", "dw1", "dm0", "dm1", "dc0", "dc1")

        def prep_w1():
            srcs = (wg_in, wu_in)
            dsts = (wgb, wub)
            stl = (s_wg, s_wu)
            NCH = 2 * ICH
            wpair = {}
            atok = {}
            qtok = {}

            def issue_load(c):
                g.need("sync", ("wl1", c % 6))
                g.dma("sync", wldq[c % 6][:],
                      srcs[c % 2][128 * (c // 2):128 * (c // 2 + 1), :],
                      sem=L6[c % 6])

            def stA(c):
                g.w("vector", L6[c % 6])
                tok_rr = g.im("vector", "tensor_reduce", out=red_c[:],
                              in_=wldq[c % 6][:], axis=AX.X, op=ALU.max,
                              apply_absolute_value=True, sem="dv")
                g.w("vector", "dv", tok_rr)
                tok_st = g.im("vector", "tensor_scalar",
                              out=stl[c % 2][:, c // 2:c // 2 + 1],
                              in0=red_c[:], scalar1=1.0 / QMAX, scalar2=None,
                              op0=ALU.mult, sem="dv")
                g.w("vector", "dv", tok_st)
                return g.im("vector", "reciprocal",
                            out=winv6[:, c % 6:c % 6 + 1],
                            in_=stl[c % 2][:, c // 2:c // 2 + 1], sem="dv")

            for j in range(5):
                issue_load(j)
                atok[j] = stA(j)
            for c in range(NCH):
                if c + 5 < NCH:
                    issue_load(c + 5)
                    atok[c + 5] = stA(c + 5)
                wq = (wq1, wq1b)[c % 2]
                g.need("scalar", ("wq1", c % 2))
                g.w("scalar", "dv", atok[c])
                tok_q1 = g.im("scalar", "activation", out=wq[:, 0:C],
                              in_=wldq[c % 6][:], func=AF.Identity,
                              scale=winv6[:, c % 6:c % 6 + 1],
                              bias=magic_c[:, 0:1], sem="ac")
                g.done(("wl1", c % 6), "ac", tok_q1)
                g.w("vector", "ac", tok_q1)
                g.need("vector", ("wqo", c % 2))
                tok_q = g.im("vector", "tensor_scalar",
                             out=(wqo, wqob)[c % 2][:, 0:C], in0=wq[:, 0:C],
                             scalar1=MAGIC, scalar2=None, op0=ALU.subtract,
                             sem="dv")
                g.done(("wq1", c % 2), "dv", tok_q)
                qtok[c] = tok_q
                if c >= 1:
                    g.w("tensor", "dv", qtok[c - 1])
                    prep_tail(c - 1, "qi", C, CCH, dsts[(c - 1) % 2],
                              sr=(c - 1) // 2)
                    if (c - 1) % 2 == 1:
                        g.w("gpsimd", "dt")
                        wpair[(c - 1) // 2] = g.im(
                            "gpsimd", "memset", g.barsc_g[0:1, 0:1], 0.0,
                            sem="gp")
            g.w("tensor", "dv", qtok[NCH - 1])
            prep_tail(NCH - 1, "qi", C, CCH, dsts[1], sr=ICH - 1)
            g.w("gpsimd", "dt")
            wpair[ICH - 1] = g.im("gpsimd", "memset", g.barsc_g[0:1, 0:1],
                                  0.0, sem="gp")
            return wpair

        def prep_weight(src, nrows_ch, ncols, dst_blk, s_tile, mode):
            ldb = (wld, wld2)
            q1b = (wq1, wq1b)
            nb = ncols // 128

            def wd_load(rc):
                g.need("sync", ("wld", rc % 2))
                hc = ncols // 4
                for sg in range(4):
                    g.dma("sync", ldb[rc % 2][:, hc * sg:hc * (sg + 1)],
                          src[128 * rc:128 * (rc + 1), hc * sg:hc * (sg + 1)],
                          sem="dw%d" % (rc % 2))

            def stageA(rc):
                lb = ldb[rc % 2][:, 0:ncols]
                g.w("vector", "dw%d" % (rc % 2))
                tok_rr = g.im("vector", "tensor_reduce", out=red_c[:],
                              in_=lb, axis=AX.X, op=ALU.max,
                              apply_absolute_value=True, sem="dv")
                g.w("vector", "dv", tok_rr)
                tok_st = g.im("vector", "tensor_scalar",
                              out=s_tile[:, rc:rc + 1], in0=red_c[:],
                              scalar1=1.0 / QMAX, scalar2=None,
                              op0=ALU.mult, sem="dv")
                g.w("vector", "dv", tok_st)
                return g.im("vector", "reciprocal",
                            out=winv[:, rc % 2:rc % 2 + 1],
                            in_=s_tile[:, rc:rc + 1], sem="dv")

            atok = {}
            qtok = {}
            if mode in ("qi", "qd"):
                for rc in range(nrows_ch):
                    if rc == 0:
                        wd_load(0)
                        atok[0] = stageA(0)
                    lb = ldb[rc % 2][:, 0:ncols]
                    wq = q1b[rc % 2]
                    g.need("scalar", ("wq1", rc % 2))
                    g.w("scalar", "dv", atok[rc])
                    tok_q1 = g.im("scalar", "activation", out=wq[:, 0:ncols],
                                  in_=lb, func=AF.Identity,
                                  scale=winv[:, rc % 2:rc % 2 + 1],
                                  bias=magic_c[:, 0:1], sem="ac")
                    g.done(("wld", rc % 2), "ac", tok_q1)
                    if rc + 1 < nrows_ch:
                        wd_load(rc + 1)
                        atok[rc + 1] = stageA(rc + 1)
                    g.w("vector", "ac", tok_q1)
                    g.need("vector", ("wqo", rc % 2))
                    qvx = (wqo, wqob)[rc % 2]
                    if mode == "qi":
                        tok_q = g.im("vector", "tensor_scalar",
                                     out=qvx[:, 0:ncols], in0=wq[:, 0:ncols],
                                     scalar1=MAGIC, scalar2=None,
                                     op0=ALU.subtract, sem="dv")
                    else:
                        tok_q = g.im("vector", "tensor_scalar",
                                     out=qvx.bitcast(F16)[:, 0:ncols],
                                     in0=wq[:, 0:ncols], scalar1=MAGIC,
                                     scalar2=s_tile[:, rc:rc + 1],
                                     op0=ALU.subtract, op1=ALU.mult, sem="dv")
                    g.done(("wq1", rc % 2), "dv", tok_q)
                    qtok[rc] = tok_q
                    if rc >= 1:
                        g.w("tensor", "dv", qtok[rc - 1])
                        prep_tail(rc - 1, mode, ncols, nb, dst_blk)
                g.w("tensor", "dv", qtok[nrows_ch - 1])
                prep_tail(nrows_ch - 1, mode, ncols, nb, dst_blk)
                return
            for rc in range(nrows_ch):
                g.need("sync", ("wld", rc % 2))
                g.dma("sync", ldb[rc % 2][:, 0:ncols],
                      src[128 * rc:128 * (rc + 1), 0:ncols],
                      sem="dw%d" % (rc % 2))
                lb = ldb[rc % 2][:, 0:ncols]
                if False:
                    pass
                else:
                    g.need("scalar", ("wqo", rc % 2))
                    g.w("scalar", "dw%d" % (rc % 2))
                    tok_q = g.im("scalar", "copy",
                                 out=(wqo, wqob)[rc % 2][:, 0:ncols], in_=lb,
                                 sem="ac")
                    g.done(("wld", rc % 2), "ac", tok_q)
                    qtok[rc] = tok_q
                    if rc >= 1:
                        g.w("tensor", "ac", qtok[rc - 1])
                        prep_tail(rc - 1, mode, ncols, nb, dst_blk)
            if mode not in ("qi", "qd"):
                g.w("tensor", "ac", qtok[nrows_ch - 1])
                prep_tail(nrows_ch - 1, mode, ncols, nb, dst_blk)

        def ln_finalize(m_t, r_t, with_absmax):
            tok_mm = g.im("vector", "tensor_scalar", out=m_t[:], in0=st_sum[:],
                          scalar1=1.0 / C, scalar2=None, op0=ALU.mult, sem="dv")
            g.i("vector", "tensor_scalar", out=stt_sc[:], in0=st_sq[:],
                scalar1=1.0 / C, scalar2=None, op0=ALU.mult)
            g.w("vector", "dv", tok_mm)
            g.i("vector", "tensor_mul", out=r_t[:], in0=m_t[:], in1=m_t[:])
            g.i("vector", "tensor_sub", out=stt_sc[:], in0=stt_sc[:], in1=r_t[:])
            tok = g.im("vector", "tensor_scalar", out=r_t[:], in0=stt_sc[:],
                       scalar1=1e-5, scalar2=None, op0=ALU.add, sem="dv")
            g.w("scalar", "dv", tok)
            tok2 = g.im("scalar", "activation", out=stt_sc[:], in_=r_t[:],
                        func=AF.Sqrt, sem="ac")
            g.w("vector", "ac", tok2)
            tok3 = g.im("vector", "reciprocal", out=r_t[:], in_=stt_sc[:],
                        sem="dv")
            if with_absmax:
                g.i("vector", "tensor_sub", out=stt_sc[:], in0=st_max[:],
                    in1=m_t[:])
                g.i("vector", "tensor_sub", out=st_max[:], in0=m_t[:],
                    in1=st_min[:])
                g.i("vector", "tensor_max", out=stt_sc[:], in0=stt_sc[:],
                    in1=st_max[:])
                g.i("vector", "tensor_mul", out=stt_sc[:], in0=stt_sc[:],
                    in1=r_t[:])
                g.i("vector", "reduce_max", out=red_c[:], in_=stt_sc[:],
                    axis=AX.X)
                tok4 = g.mark("vector", "dv")
                return tok3, tok4
            return tok3, None

        def global_max_scale(tok_red, clip_row):
            g.w("tensor", "dv", tok_red)
            g.need("tensor", "PS")
            tok_t = g.im("tensor", "transpose", PS[0:1, 0:128], red_c[:, 0:1],
                         eye32[:, :], sem="pe")
            g.w("vector", "pe", tok_t)
            tok_m = g.im("vector", "reduce_max", out=sc_t[:, 0:1],
                         in_=PS[0:1, 0:128], axis=AX.X, sem="dv")
            g.done("PS", "dv", tok_m)
            g.w("gpsimd", "dv", tok_m)
            g.dma("gpsimd", mx_i, sc_t[:, 0:8])
            g.w("gpsimd", "dg")
            tok_cc = g.im("gpsimd", "collective_compute", "AllReduce", ALU.max,
                          replica_groups=[list(range(8))], ins=[mx_i],
                          outs=[mx_o], sem="gp")
            g.w("gpsimd", "gp", tok_cc)
            g.dma("gpsimd", sc_t[:, 0:8], mx_o)
            g.w("scalar", "dg", g.semv["dg"])
            tok_sg = g.im("scalar", "activation", out=sc_t2[:, 0:1],
                          in_=clip_s[0:1, clip_row:clip_row + 1],
                          func=AF.Sigmoid, sem="ac")
            g.w("vector", "ac", tok_sg)
            tok_s1 = g.im("vector", "tensor_scalar", out=sc_t[:, 1:2],
                          in0=sc_t[:, 0:1], scalar1=sc_t2[0:1, 0:1],
                          scalar2=1.0 / QMAX, op0=ALU.mult, op1=ALU.mult,
                          sem="dv")
            g.w("vector", "dv", tok_s1)
            g.i("vector", "tensor_scalar", out=sc_t[:, 3:4], in0=sc_t[:, 1:2],
                scalar1=0.5, scalar2=None, op0=ALU.mult)
            tok_s = g.im("vector", "reciprocal", out=sc_t[:, 2:3],
                         in_=sc_t[:, 1:2], sem="dv")
            g.w("tensor", "dv", tok_s)
            g.need("tensor", "PS")
            tok_b = g.im("tensor", "matmul", PS[:, 0:3], ones_row[0:1, :],
                         sc_t[0:1, 1:4], start=True, stop=True, sem="pe")
            g.w("scalar", "pe", tok_b)
            tok_c = g.im("scalar", "copy", out=scb[:, 0:3], in_=PS[:, 0:3],
                         sem="ac")
            g.done("PS", "ac", tok_c)
            return tok_c

        g.w("vector", "dv", tok_av)
        g.w("vector", "ac", tok_as)
        t_r1, t_am1 = ln_finalize(m_a, r_a, True)
        tok_sx = global_max_scale(t_am1, 0)

        sW1 = es.enter_context(contextlib.ExitStack())
        wldq = [g.sbuf("wldq%d" % j, [128, C], F32, sW1) for j in range(6)]
        winv6 = g.sbuf("winv6", [128, 6], F32, sW1)
        wpair = prep_w1()
        qa_w1 = g.semv["ac"]
        g.done(("wld", 0), "ac", qa_w1)
        g.done(("wld", 1), "ac", qa_w1)
        sW1.close()
        sW2 = es.enter_context(contextlib.ExitStack())
        wld = g.sbuf("wld", [128, I], F32, sW2)
        wld2 = g.sbuf("wld2", [128, I], F32, sW2)
        prep_weight(wd_in, CCH, I, wdb, s_wd, "qd")
        wstore = g.semv["dt"]

        g.w("vector", "ac", tok_sx)
        g.i("vector", "tensor_scalar", out=eff_g[:], in0=s_wg[:],
            scalar1=scb[:, 0:1], scalar2=None, op0=ALU.mult)
        g.im("vector", "tensor_scalar", out=eff_u[:], in0=s_wu[:],
             scalar1=scb[:, 0:1], scalar2=None, op0=ALU.mult, sem="dv")
        tok_eff = g.semv["dv"]
        sW2.close()
        sW.close()

        if stage == "dbg1":
            g.barrier()
            sDBG = es.enter_context(contextlib.ExitStack())
            dbf = g.sbuf("dbf", [128, 1024], F32, sDBG)
            dbh = g.sbuf("dbh", [128, 1024], BF16, sDBG)
            g.i("vector", "memset", dbf[:], 0.0)
            g.i("vector", "tensor_copy", out=dbf[:, 0:NT], in_=m_a[:])
            g.i("vector", "tensor_copy", out=dbf[:, 32:32 + NT], in_=r_a[:])
            g.i("vector", "tensor_copy", out=dbf[:, 64:64 + NT], in_=st_sum[:])
            g.i("vector", "tensor_copy", out=dbf[:, 96:96 + NT], in_=st_sq[:])
            g.i("vector", "tensor_copy", out=dbf[:, 128:129], in_=scb[:, 0:1])
            g.i("vector", "tensor_copy", out=dbf[:, 129:130], in_=scb[:, 1:2])
            g.i("vector", "tensor_copy", out=dbf[:, 130:131], in_=scb[:, 2:3])
            g.i("vector", "tensor_copy", out=dbf[:, 132:132 + ICH], in_=s_wg[:])
            g.i("vector", "tensor_copy", out=dbf[:, 160:160 + ICH], in_=s_wu[:])
            g.i("vector", "tensor_copy", out=dbf[:, 192:192 + CCH], in_=s_wd[:])
            tok_d = g.mark("vector", "dv")
            g.w("sync", "dv", tok_d)
            g.dma("sync", out_d[0:128, :], dbf[:])
            g.i("vector", "tensor_scalar", out=dbf[:], in0=xres[:, 0:C],
                scalar1=m_a[:, 0:1], scalar2=r_a[:, 0:1],
                op0=ALU.subtract, op1=ALU.mult)
            tok_d = g.mark("vector", "dv")
            g.w("sync", "dv", tok_d)
            g.dma("sync", out_d[128:256, :], dbf[:])
            g.w("scalar", "dv", tok_d)
            tok_q1d = g.im("scalar", "activation", out=dbf[:], in_=dbf[:],
                           func=AF.Identity, scale=scb[:, 1:2],
                           bias=magic_c[:, 0:1], sem="ac")
            g.w("vector", "ac", tok_q1d)
            g.i("vector", "tensor_scalar", out=dbf[:], in0=dbf[:],
                scalar1=MAGIC, scalar2=127.0, op0=ALU.subtract, op1=ALU.min)
            g.i("vector", "tensor_scalar", out=dbh[:], in0=dbf[:],
                scalar1=-128.0, scalar2=None, op0=ALU.max)
            g.i("vector", "tensor_copy", out=dbf[:], in_=dbh[:])
            tok_d = g.mark("vector", "dv")
            g.w("sync", "dv", tok_d)
            g.dma("sync", out_d[256:384, :], dbf[:])
            g.dma("sync", dbh[:], wgb[0])
            g.w("vector", "ds")
            g.i("vector", "tensor_copy", out=dbf[:], in_=dbh[:])
            tok_d = g.mark("vector", "dv")
            g.w("sync", "dv", tok_d)
            g.dma("sync", out_d[384:512, :], dbf[:])
            g.dma("sync", dbh.bitcast(F16)[:], wdb[0])
            g.w("vector", "ds")
            g.i("vector", "tensor_copy", out=dbf[:], in_=dbh.bitcast(F16)[:])
            tok_d = g.mark("vector", "dv")
            g.w("sync", "dv", tok_d)
            g.dma("sync", out_d[512:640, :], dbf[:])
            g.w("sync", "ds")
            blk = es.enter_context(nc.Block())
            g.finish(blk)
            return nc

        def ffn_run(m_t, r_t, final, dbg3=False, wgate=None, wstore_t=None,
                    eff_t=None, inject=None, post_b=None):
            tg = "F2" if final else "F1"
            sB = es.enter_context(contextlib.ExitStack())
            xqT = g.sbuf("xqT" + tg, [128, 2 * CCH * 1024], BF16, sB)
            wmb = g.sbuf("wmb" + tg, [128, 4 * C], BF16, sB)
            hstg = g.sbuf("hstg" + tg, [128, 2 * 1024], F16, sB)
            silu_s = g.sbuf("silu_s" + tg, [128, 512], F32, sB)
            habs = g.sbuf("habs" + tg, [128, 192], F32, sB)
            lncol = g.sbuf("lncol" + tg, [128, 2], F32, sB)
            qt1 = g.sbuf("qt1" + tg, [128, 1024], F32, sB)
            xq_tm = g.sbuf("xq_tm" + tg, [128, 4 * 1024], BF16, sB)

            xq_ready = {}
            xq_used = {}
            if wstore_t is not None:
                g.w("vector", "dt", wstore_t)
                g.w("scalar", "dt", wstore_t)
                g.w("sync", "dt", wstore_t)
            if eff_t is not None:
                g.w("scalar", "dv", eff_t)

            def B0(q, half):
                slot = q % 2
                base = slot * CCH * 1024
                if half == 0 and q >= 2:
                    g.w("vector", "pe", xq_used[q - 2])
                    g.w("scalar", "pe", xq_used[q - 2])
                for tl in range(4 * half, 4 * half + 4):
                    t = 8 * q + tl
                    xt = xres[:, C * t:C * (t + 1)]
                    g.need("vector", "lncol")
                    g.i("vector", "tensor_scalar", out=lncol[:, 0:1],
                        in0=r_t[:, t:t + 1], scalar1=scb[:, 1:2],
                        scalar2=None, op0=ALU.mult)
                    tok_ln = g.im("vector", "tensor_scalar",
                                  out=lncol[:, 1:2], in0=m_t[:, t:t + 1],
                                  scalar1=lncol[:, 0:1], scalar2=-1.0,
                                  op0=ALU.mult, op1=ALU.mult, sem="dv")
                    g.need("scalar", "qt1")
                    g.w("scalar", "dv", tok_ln)
                    tok_q1 = g.im("scalar", "activation", out=qt1[:], in_=xt,
                                  func=AF.Identity, scale=lncol[:, 0:1],
                                  bias=lncol[:, 1:2], sem="ac")
                    g.done("lncol", "ac", tok_q1)
                    g.w("vector", "ac", tok_q1)
                    g.i("vector", "tensor_scalar", out=qt1[:], in0=qt1[:],
                        scalar1=MAGIC, scalar2=None, op0=ALU.add)
                    g.i("vector", "tensor_scalar", out=qt1[:], in0=qt1[:],
                        scalar1=MAGIC, scalar2=127.0, op0=ALU.subtract,
                        op1=ALU.min)
                    g.need("vector", ("xq_tm", tl % 4))
                    tok_q = g.im("vector", "tensor_scalar",
                                 out=xq_tm[:, 1024 * (tl % 4):1024 * (tl % 4 + 1)],
                                 in0=qt1[:], scalar1=-128.0, scalar2=None,
                                 op0=ALU.max, sem="dv")
                    g.done("qt1", "dv", tok_q)
                    if tl % 4 == 3:
                        qq = tl // 4
                        g.w("tensor", "dv", tok_q)
                        for cc in range(CCH):
                            PTx = (PT0, PT1)[cc % 2]
                            g.need("tensor", ("PT", cc % 2))
                            tok_tr = None
                            for tj in range(4):
                                tok_tr = g.im(
                                    "tensor", "transpose",
                                    PTx[:, 128 * tj:128 * (tj + 1)],
                                    xq_tm[:, 1024 * tj + 128 * cc:
                                          1024 * tj + 128 * cc + 128],
                                    eyeb[:, :], sem="pe")
                            dst = xqT[:, base + 1024 * cc + 512 * qq:
                                      base + 1024 * cc + 512 * qq + 512]
                            if cc % 2 == 0:
                                g.w("vector", "pe", tok_tr)
                                tok_e = g.im("vector", "tensor_copy", out=dst,
                                             in_=PTx[:, 0:512], sem="dv")
                                g.done(("PT", 0), "dv", tok_e)
                            else:
                                g.w("scalar", "pe", tok_tr)
                                tok_e = g.im("scalar", "copy", out=dst,
                                             in_=PTx[:, 0:512], sem="ac")
                                g.done(("PT", 1), "ac", tok_e)
                        for tj in range(4):
                            g.done(("xq_tm", tj), "pe", g.semv["pe"])
                if half == 1:
                    xq_ready[q] = (g.mark("vector", "dv"),
                                   g.mark("scalar", "ac"))

            def B1(q):
                slot = q % 2
                base = slot * CCH * 1024
                tv, ta = xq_ready[q]
                g.w("tensor", "dv", tv)
                g.w("tensor", "ac", ta)
                last_mm = None
                wtok = {}

                def b1_load(ic):
                    if wgate is not None:
                        g.w("sync", "gp", wgate[ic])
                    for wi, wsrc in enumerate((wgb, wub)):
                        ws = (ic % 2) * 2 + wi
                        g.need("sync", ("wmb", ws))
                        g.dma("sync", wmb[:, C * ws:C * (ws + 1)], wsrc[ic],
                              sem="dm%d" % (ic % 2))
                    wtok[ic] = g.semv["dm%d" % (ic % 2)]

                ics = [0] if dbg3 else list(range(ICH))
                b1_load(ics[0])
                for ii, ic in enumerate(ics):
                    if inject is not None and ic % 3 == 1:
                        inject()
                    if not dbg3 and q + 1 < 4 and ic in (8, 15):
                        B0(q + 1, 0 if ic == 8 else 1)
                    if ii + 1 < len(ics):
                        b1_load(ics[ii + 1])
                    g.w("tensor", "dm%d" % (ic % 2), wtok[ic])
                    for g2 in range(2):
                        for wi in range(2):
                            ws = (ic % 2) * 2 + wi
                            bank = (P0, P1, P2, P3)[2 * g2 + wi]
                            g.need("tensor", ("PB", 2 * g2 + wi))
                            for k in range(CCH):
                                last_mm = g.im(
                                    "tensor", "matmul", bank[:],
                                    wmb[:, C * ws + 128 * k:
                                        C * ws + 128 * (k + 1)],
                                    xqT[:, base + 1024 * k + 512 * g2:
                                        base + 1024 * k + 512 * g2 + 512],
                                    start=(k == 0), stop=(k == CCH - 1),
                                    sem="pe")
                        g.done(("wmb", (ic % 2) * 2), "pe", last_mm)
                        g.done(("wmb", (ic % 2) * 2 + 1), "pe", last_mm)
                        gbank, ubank = (P0, P1) if g2 == 0 else (P2, P3)
                        g.need("scalar", "silu_s")
                        g.w("scalar", "pe", last_mm)
                        tok_si = g.im("scalar", "activation", out=silu_s[:],
                                      in_=gbank[:], func=AF.Silu,
                                      scale=eff_g[:, ic:ic + 1], sem="ac")
                        g.done(("PB", 2 * g2 + 0), "ac", tok_si)
                        hdst = hstg[:, 1024 * (ic % 2) + 512 * g2:
                                    1024 * (ic % 2) + 512 * g2 + 512]
                        g.need("vector", ("hstg", ic % 2))
                        g.w("vector", "ac", tok_si)
                        tok_h = g.im("vector", "scalar_tensor_tensor",
                                     out=hdst, in0=ubank[:],
                                     scalar=eff_u[:, ic:ic + 1], in1=silu_s[:],
                                     op0=ALU.mult, op1=ALU.mult, sem="dv")
                        g.done(("PB", 2 * g2 + 1), "dv", tok_h)
                        g.done("silu_s", "dv", tok_h)
                        g.i("vector", "tensor_reduce",
                            out=habs[:, 4 * (2 * ic + g2) + q:
                                     4 * (2 * ic + g2) + q + 1],
                            in_=hdst, axis=AX.X, op=ALU.max,
                            apply_absolute_value=True)
                    tok_hv = g.mark("vector", "dv")
                    g.w("sync", "dv", tok_hv)
                    sh = "dh%d" % (ic % 2)
                    g.dma("sync", hsp[ic][:, 1024 * q:1024 * (q + 1)],
                          hstg[:, 1024 * (ic % 2):1024 * (ic % 2) + 1024],
                          sem=sh)
                    g.done(("hstg", ic % 2), sh, g.semv[sh])
                xq_used[q] = last_mm

            if dbg3:
                B0(0, 0)
                B0(0, 1)
                B1(0)
                sDBG = es.enter_context(contextlib.ExitStack())
                dbf = g.sbuf("dbf", [128, 1024], F32, sDBG)
                g.barrier()
                g.i("vector", "tensor_copy", out=dbf[:], in_=xqT[:, 0:1024])
                tok_d = g.mark("vector", "dv")
                g.w("sync", "dv", tok_d)
                g.dma("sync", out_d[0:128, :], dbf[:])
                g.w("vector", "ds")
                g.i("vector", "tensor_copy", out=dbf[:, 0:512], in_=P0[:])
                g.i("vector", "tensor_copy", out=dbf[:, 512:1024], in_=P1[:])
                tok_d = g.mark("vector", "dv")
                g.w("sync", "dv", tok_d)
                g.dma("sync", out_d[128:256, :], dbf[:])
                g.w("vector", "ds")
                g.i("vector", "tensor_copy", out=dbf[:, 0:512], in_=silu_s[:])
                g.i("vector", "tensor_copy", out=dbf[:, 512:512 + ICH],
                    in_=eff_g[:])
                g.i("vector", "tensor_copy", out=dbf[:, 544:544 + ICH],
                    in_=eff_u[:])
                tok_d = g.mark("vector", "dv")
                g.w("sync", "dv", tok_d)
                g.dma("sync", out_d[256:384, :], dbf[:])
                g.w("vector", "ds")
                g.i("vector", "tensor_copy", out=dbf[:], in_=hstg[:, 0:1024])
                tok_d = g.mark("vector", "dv")
                g.w("sync", "dv", tok_d)
                g.dma("sync", out_d[384:512, :], dbf[:])
                g.w("sync", "ds")
                g.dma("sync", out_d[512:640, :], xres[:, 0:1024])
                g.w("vector", "ds")
                g.i("vector", "memset", dbf[:], 0.0)
                g.i("vector", "tensor_copy", out=dbf[:, 0:1], in_=scb[:, 1:2])
                g.i("vector", "tensor_copy", out=dbf[:, 1:2], in_=scb[:, 0:1])
                g.i("vector", "tensor_copy", out=dbf[:, 2:3], in_=magic_c[:])
                g.i("vector", "tensor_copy", out=dbf[:, 16:48], in_=m_a[:])
                g.i("vector", "tensor_copy", out=dbf[:, 48:80], in_=r_a[:])
                g.i("vector", "tensor_copy", out=dbf[:, 80:102], in_=s_wg[:])
                g.i("vector", "tensor_copy", out=dbf[:, 112:112 + 1024 - 112]
                    if False else dbf[:, 102:103], in_=red_c[:])
                tok_d = g.mark("vector", "dv")
                g.w("sync", "dv", tok_d)
                g.dma("sync", out_d[640:768, :], dbf[:])
                g.w("vector", "ds")
                g.i("vector", "tensor_copy", out=dbf[:], in_=xq_tm[:, 0:1024])
                tok_d = g.mark("vector", "dv")
                g.w("sync", "dv", tok_d)
                g.dma("sync", out_d[768:896, :], dbf[:])
                g.w("sync", "ds")
                return None
            B0(0, 0)
            B0(0, 1)
            for q in range(4):
                B1(q)
            g.w("vector", "dv")
            g.i("vector", "reduce_max", out=red_c[:], in_=habs[:, 0:176],
                axis=AX.X)
            tok_habs = g.mark("vector", "dv")
            sB.close()
            if post_b is not None:
                post_b()

            sC = es.enter_context(contextlib.ExitStack())
            wdw = g.sbuf("wdw" + tg, [128, ICH * C], F16, sC)
            hqb = g.sbuf("hqb" + tg, [128, 2 * ICH * 256], F16, sC)
            qc = g.sbuf("qc" + tg, [128, 4 * 256], F32, sC)
            dsc = g.sbuf("dsc" + tg, [128, 2 * 1024], BF16, sC)
            g.w("sync", "pe", xq_used[3])
            g.w("sync", "dv", tok_habs)
            g.w("sync", "dh0")
            g.w("sync", "dh1")
            if wstore_t is not None:
                g.w("sync", "dt", g.semv["dt"])
            for ic in range(ICH):
                g.dma("sync", wdw[:, C * ic:C * (ic + 1)], wdb[ic], sem="dm0")
            tok_sh = global_max_scale(tok_habs, 1)
            g.w("vector", "ac", tok_sh)
            g.w("scalar", "ac", tok_sh)
            g.w("tensor", "dm0")
            hq_used = {}
            hq_tok = {}

            def c_load(gi):
                slot = gi % 2
                hb = slot * ICH * 256
                if gi >= 2:
                    g.w("sync", "pe", hq_used[gi - 2])
                sc = "dc%d" % (gi % 2)
                g.dma("sync", hqb[:, hb:hb + 11 * 256],
                      hsp.rearrange("ic p j -> p ic j")
                      [:, 0:11, 256 * gi:256 * (gi + 1)], sem=sc)
                g.dma("sync", hqb[:, hb + 11 * 256:hb + ICH * 256],
                      hsp.rearrange("ic p j -> p ic j")
                      [:, 11:ICH, 256 * gi:256 * (gi + 1)], sem=sc)

            def c_quant(gi):
                slot = gi % 2
                hb = slot * ICH * 256
                sc = "dc%d" % (gi % 2)
                g.w("scalar", sc)
                for ic in range(ICH):
                    hsl = hqb[:, hb + 256 * ic:hb + 256 * (ic + 1)]
                    qs = 256 * (2 * (gi % 2) + ic % 2)
                    g.need("scalar", ("qc", gi % 2, ic % 2))
                    tok_a = g.im("scalar", "activation",
                                 out=qc[:, qs:qs + 256],
                                 in_=hsl, func=AF.Identity,
                                 scale=scb[:, 1:2], bias=magic_c[:, 0:1],
                                 sem="ac")
                    g.w("vector", "ac", tok_a)
                    g.i("vector", "tensor_scalar", out=qc[:, qs:qs + 256],
                        in0=qc[:, qs:qs + 256], scalar1=MAGIC, scalar2=127.0,
                        op0=ALU.subtract, op1=ALU.min)
                    tok_qq = g.im("vector", "tensor_scalar", out=hsl,
                                  in0=qc[:, qs:qs + 256], scalar1=-128.0,
                                  scalar2=None, op0=ALU.max, sem="dv")
                    g.done(("qc", gi % 2, ic % 2), "dv", tok_qq)
                hq_tok[gi] = g.semv["dv"]

            c_load(0)
            c_load(1)
            c_quant(0)
            for gi in range(16):
                slot = gi % 2
                hb = slot * ICH * 256
                if gi + 1 < 16:
                    c_quant(gi + 1)
                g.w("tensor", "dv", hq_tok[gi])
                last_mm = None
                for tc in range(2):
                    t = 2 * gi + tc
                    bA, bB = ((P0, P1), (P2, P3))[tc % 2]
                    g.need("tensor", ("PC", tc % 2))
                    for ic in range(ICH):
                        stat = hqb[:, hb + 256 * ic + 128 * tc:
                                   hb + 256 * ic + 128 * tc + 128]
                        g.im("tensor", "matmul", bA[:], stat,
                             wdw[:, C * ic:C * ic + 512],
                             start=(ic == 0), stop=(ic == ICH - 1), sem="pe")
                        last_mm = g.im("tensor", "matmul", bB[:], stat,
                                       wdw[:, C * ic + 512:C * ic + 1024],
                                       start=(ic == 0), stop=(ic == ICH - 1),
                                       sem="pe")
                    g.need("scalar", ("dsc", tc % 2))
                    g.w("scalar", "pe", last_mm)
                    g.i("scalar", "activation", out=dsc[:, 1024 * (tc % 2):
                                                       1024 * (tc % 2) + 512],
                        in_=bA[:], func=AF.Identity, scale=scb[:, 2:3])
                    tok_ev = g.im("scalar", "activation",
                                  out=dsc[:, 1024 * (tc % 2) + 512:
                                          1024 * (tc % 2) + 1024],
                                  in_=bB[:], func=AF.Identity,
                                  scale=scb[:, 2:3], sem="ac")
                    g.done(("PC", tc % 2), "ac", tok_ev)
                    g.w("vector", "ac", tok_ev)
                    xsl = xres[:, C * t:C * (t + 1)]
                    tok_add = g.im("vector", "tensor_tensor", out=xsl, in0=xsl,
                                   in1=dsc[:, 1024 * (tc % 2):
                                           1024 * (tc % 2) + 1024],
                                   op=ALU.add, sem="dv")
                    g.done(("dsc", tc % 2), "dv", tok_add)
                    if not final:
                        g.i("vector", "reduce_sum", out=st_sum[:, t:t + 1],
                            in_=xsl, axis=AX.X)
                        g.w("scalar", "dv", tok_add)
                        tok_sq2 = g.im("scalar", "activation",
                                       out=dsc[:, 1024 * (tc % 2):
                                               1024 * (tc % 2) + 1024],
                                       in_=xsl, func=AF.Square,
                                       accum_out=st_sq[:, t:t + 1], sem="ac")
                        g.done(("dsc", tc % 2), "ac", tok_sq2)
                hq_used[gi] = last_mm
                if gi + 2 < 16:
                    c_load(gi + 2)
                if final:
                    g.w("sync", "dv", tok_add)
                    for tc in range(2):
                        t = 2 * gi + tc
                        g.dma("sync", out_d[128 * t:128 * (t + 1), :],
                              xres[:, C * t:C * (t + 1)], sem="dt")
            g.barrier()
            sC.close()

        if stage == "dbg3":
            g.barrier()
            ffn_run(m_a, r_a, False, dbg3=True)
            blk = es.enter_context(nc.Block())
            g.finish(blk)
            return nc
        sVRO = es.enter_context(contextlib.ExitStack())
        vld = [g.sbuf("vld%d" % j2, [128, C], F32, sVRO) for j2 in range(2)]
        vqo = [g.sbuf("vqo%d" % j2, [128, C], BF16, sVRO) for j2 in range(2)]
        vst = [g.sbuf("vst%d" % j2, [128, C], BF16, sVRO) for j2 in range(2)]
        VCH = [(wv_in, wvb, "cb"), (wr_in, wrb, "cb"), (wo_in, wob, "cc")]
        VSEM = ("dx", "dcr")
        vtok = {}

        def vload(j):
            g.need("scalar", ("vld", j % 2))
            g.dma("scalar", vld[j % 2][:],
                  VCH[j // 8][0][128 * (j % 8):128 * (j % 8 + 1), :],
                  sem=VSEM[j % 2])

        def vro_item(it):
            if it < 24:
                g.need("scalar", ("vqo", it % 2))
                g.w("scalar", VSEM[it % 2])
                tok_c = g.im("scalar", "copy", out=vqo[it % 2][:],
                             in_=vld[it % 2][:], sem="ac")
                g.done(("vld", it % 2), "ac", tok_c)
                vtok[it] = tok_c
                if it + 2 < 24:
                    vload(it + 2)
            if it >= 1:
                j = it - 1
                g.w("tensor", "ac", vtok[j])
                prep_tail(j, VCH[j // 8][2], C, CCH, VCH[j // 8][1],
                          sr=j % 8, bufs=(vqo[j % 2], vst[j % 2]),
                          gk=("vqo", "vst"))

        it_ctr = [0]

        def vro_inject():
            if it_ctr[0] <= 24:
                vro_item(it_ctr[0])
                it_ctr[0] += 1

        g.w("scalar", "dt", wstore)
        vload(0)
        vload(1)
        def vro_close():
            assert it_ctr[0] > 24, it_ctr
            sVRO.close()

        ffn_run(m_a, r_a, False, wgate=wpair, wstore_t=wstore,
                eff_t=tok_eff, inject=vro_inject, post_b=vro_close)
        tok_2v = g.mark("vector", "dv")
        tok_2s = g.mark("scalar", "ac")
        g.w("vector", "dv", tok_2v)
        g.w("vector", "ac", tok_2s)
        ln_finalize(m_b, r_b, False)

        if stage in ("ffn1", "dbg2"):
            g.barrier()
            for t in range(NT):
                g.dma("sync", out_d[128 * t:128 * (t + 1), :],
                      xres[:, C * t:C * (t + 1)])
            if stage == "dbg2":
                g.w("sync", "ds")
                sDBG = es.enter_context(contextlib.ExitStack())
                dbf = g.sbuf("dbf", [128, 1024], F32, sDBG)
                dbh = g.sbuf("dbh", [128, 1024], BF16, sDBG)
                g.dma("sync", dbh.bitcast(F16)[:], hsp[0][:, 0:1024])
                g.w("vector", "ds")
                g.i("vector", "tensor_copy", out=dbf[:], in_=dbh.bitcast(F16)[:])
                tok_d = g.mark("vector", "dv")
                g.w("sync", "dv", tok_d)
                g.dma("sync", out_d[3840:3968, :], dbf[:])
                g.w("vector", "ds")
                g.i("vector", "memset", dbf[:], 0.0)
                g.i("vector", "tensor_copy", out=dbf[:, 0:1], in_=scb[:, 0:1])
                g.i("vector", "tensor_copy", out=dbf[:, 1:2], in_=scb[:, 1:2])
                g.i("vector", "tensor_copy", out=dbf[:, 2:3], in_=scb[:, 2:3])
                g.i("vector", "tensor_copy", out=dbf[:, 3:4], in_=red_c[:])
                g.i("vector", "tensor_copy", out=dbf[:, 16:16 + NT], in_=m_b[:])
                g.i("vector", "tensor_copy", out=dbf[:, 48:48 + NT], in_=r_b[:])
                tok_d = g.mark("vector", "dv")
                g.w("sync", "dv", tok_d)
                g.dma("sync", out_d[3968:4096, :], dbf[:])
            g.w("sync", "ds")
            blk = es.enter_context(nc.Block())
            g.finish(blk)
            return nc

        g.barrier()
        sD = es.enter_context(contextlib.ExitStack())
        ln2Tb = [g.sbuf("ln2T%d" % j2, [128, CCH * 512], BF16, sD)
                 for j2 in range(2)]
        wvTs = g.sbuf("wvTs", [128, CCH * C], BF16, sD)
        wrTs = g.sbuf("wrTs", [128, CCH * C], BF16, sD)
        lnb = g.sbuf("lnb", [128, 4 * 1024], BF16, sD)
        vgrp = g.sbuf("vgrp", [128, 2 * 512], BF16, sD)
        rgrp = g.sbuf("rgrp", [128, 2 * 512], BF16, sD)
        bvsD = g.sbuf("bvsD", [128, 512], F32, sD)
        zjunk = g.sbuf("zjunk", [128, 512], F32, sD)
        bbD = g.sbuf("bbD", [128, 520], F32, sD)
        lnD = g.sbuf("lnD", [128, 512], F32, sD)
        rdD = g.sbuf("rdD", [128, 2 * 512], BF16, sD)
        e_t8 = g.sbuf("e_t8", [8, 128], F32, sD)
        e_a8 = g.sbuf("e_a8", [8, 128], F32, sD)
        e_p0 = g.sbuf("e_p0", [8, 128], F32, sD)
        e_p1 = g.sbuf("e_p1", [8, 128], F32, sD)
        e_p2 = g.sbuf("e_p2", [8, 128], F32, sD)
        e_c2 = g.sbuf("e_c2", [8, 128], F32, sD)
        e_c3 = g.sbuf("e_c3", [8, 128], F32, sD)
        e_s1 = g.sbuf("e_s1", [8, 128], F32, sD)
        e_s2 = g.sbuf("e_s2", [8, 128], F32, sD)

        for dc in range(CCH):
            g.dma("sync", wvTs[:, C * dc:C * (dc + 1)], wvb[dc], sem="dm0")
            g.dma("sync", wrTs[:, C * dc:C * (dc + 1)], wrb[dc], sem="dm0")
        g.w("tensor", "dm0")

        def bb_batch(bgi):
            bk1c, bk1n = (kap1a, kap1b) if bgi % 2 == 0 else (kap1b, kap1a)
            for dc in range(CCH):
                g.need("vector", "bbD")
                g.i("vector", "tensor_copy", out=bbD[:, 0:1],
                    in_=bk1c[:, dc:dc + 1])
                tok_tb = g.im("vector", "tensor_tensor_scan",
                              out=bbD[:, 1:513],
                              data0=alph[:, dc:dc + 1].broadcast_to([128, 512]),
                              data1=gbt[:, dc:dc + 1].broadcast_to([128, 512]),
                              initial=bk1c[:, dc:dc + 1],
                              op0=ALU.mult, op1=ALU.add, sem="dv")
                g.w("vector", "dv", tok_tb)
                g.i("vector", "tensor_copy", out=bk1n[:, dc:dc + 1],
                    in_=bbD[:, 512:513])
                tok_kk = g.mark("vector", "dv")
                g.w("scalar", "dv", tok_kk)
                tok_ln = g.im("scalar", "activation", out=lnD[:],
                              in_=bbD[:, 0:512], func=AF.Ln,
                              bias=dplus[:, dc:dc + 1], sem="ac")
                g.done("bbD", "ac", tok_ln)
                g.need("scalar", ("rdD", dc % 2))
                tok_rdd = g.im("scalar", "activation",
                               out=rdD[:, 512 * (dc % 2):512 * (dc % 2 + 1)],
                               in_=lnD[:], func=AF.Exp,
                               scale=neg1_c[:, 0:1], sem="ac")
                g.w("scalar", "ac", tok_rdd)
                g.dma("scalar", rdsp[dc][:, 512 * bgi:512 * (bgi + 1)],
                      rdD[:, 512 * (dc % 2):512 * (dc % 2 + 1)],
                      sem="dc%d" % (dc % 2))
                g.done(("rdD", dc % 2), "dc%d" % (dc % 2),
                       g.semv["dc%d" % (dc % 2)])

        def ln_write(gi2, tl):
            t = 4 * gi2 + tl
            xt = xres[:, C * t:C * (t + 1)]
            g.need("vector", "lnb")
            return g.im("vector", "tensor_scalar",
                        out=lnb[:, 1024 * tl:1024 * (tl + 1)], in0=xt,
                        scalar1=m_b[:, t:t + 1], scalar2=r_b[:, t:t + 1],
                        op0=ALU.subtract, op1=ALU.mult, sem="dv")

        lnb4tok = {}
        ev_tok = {}

        def ln2t_build(gi2, ccs):
            ln2Tx = ln2Tb[gi2 % 2]
            for cc in ccs:
                PTx = (PT0, PT1, PT2)[cc % 3]
                g.need("tensor", ("PT", cc % 3))
                tok_tr = None
                for tj in range(4):
                    tok_tr = g.im("tensor", "transpose",
                                  PTx[:, 128 * tj:128 * tj + 128],
                                  lnb[:, 1024 * tj + 128 * cc:
                                      1024 * tj + 128 * cc + 128],
                                  eyeb[:, :], sem="pe")
                dst = ln2Tx[:, 512 * cc:512 * (cc + 1)]
                g.need("vector" if cc % 2 == 0 else "scalar",
                       ("ln2T", gi2 % 2))
                if cc % 2 == 0:
                    g.w("vector", "pe", tok_tr)
                    tok_e = g.im("vector", "tensor_copy", out=dst,
                                 in_=PTx[:, 0:512], sem="dv")
                    g.done(("PT", cc % 3), "dv", tok_e)
                else:
                    g.w("scalar", "pe", tok_tr)
                    tok_e = g.im("scalar", "copy", out=dst,
                                 in_=PTx[:, 0:512], sem="ac")
                    g.done(("PT", cc % 3), "ac", tok_e)
            if ccs[-1] == CCH - 1:
                g.done("lnb", "pe", g.semv["pe"])
                ev_tok[gi2] = (g.mark("vector", "dv"), g.mark("scalar", "ac"))

        for tl in range(4):
            lnb4tok[0] = ln_write(0, tl)
        g.w("tensor", "dv", lnb4tok[0])
        ln2t_build(0, list(range(CCH)))
        for gi in range(8):
            tok_lv, tok_ls = ev_tok[gi]
            g.w("tensor", "dv", tok_lv)
            g.w("tensor", "ac", tok_ls)
            ln2T = ln2Tb[gi % 2]
            last_r = None
            for dc in range(CCH):
                if gi + 1 < 8 and 1 <= dc <= 4:
                    lnb4tok[gi + 1] = ln_write(gi + 1, dc - 1)
                    if dc == 4:
                        g.w("tensor", "dv", lnb4tok[gi + 1])
                if gi + 1 < 8 and 5 <= dc <= 7:
                    ln2t_build(gi + 1,
                               [[0, 1, 2], [3, 4, 5], [6, 7]][dc - 5])
                bank = (P0, P1)[dc % 2]
                g.need("tensor", ("PVa", dc % 2))
                g.need("tensor", ("PVv", dc % 2))
                last = None
                for k in range(CCH):
                    last = g.im("tensor", "matmul", bank[:],
                                wvTs[:, C * dc + 128 * k:C * dc + 128 * (k + 1)],
                                ln2T[:, 512 * k:512 * (k + 1)],
                                start=(k == 0), stop=(k == CCH - 1), sem="pe")
                g.need("scalar", ("vgrp", dc % 2))
                g.w("scalar", "pe", last)
                tok_ev = g.im("scalar", "copy",
                              out=vgrp[:, 512 * (dc % 2):512 * (dc % 2 + 1)],
                              in_=bank[:], sem="ac")
                g.done(("PVa", dc % 2), "ac", tok_ev)
                g.need("vector", "bvsD")
                g.w("vector", "pe", last)
                g.w("vector", "ac", tok_ev)
                tok_bv = g.im("vector", "tensor_scalar", out=bvsD[:],
                              in0=bank[:], scalar1=gbt[:, dc:dc + 1],
                              scalar2=None, op0=ALU.mult, sem="dv")
                g.done(("PVv", dc % 2), "dv", tok_bv)
                tok_ts1 = g.im("vector", "tensor_tensor_scan", out=zjunk[:],
                               data0=alph[:, dc:dc + 1].broadcast_to([128, 512]),
                               data1=bvsD[:], initial=Epart[:, dc:dc + 1],
                               op0=ALU.mult, op1=ALU.add, sem="dv")
                g.w("vector", "dv", tok_ts1)
                tok_sc = g.im("vector", "tensor_copy",
                              out=Epart[:, dc:dc + 1],
                              in_=zjunk[:, 511:512], sem="dv")
                g.done("bvsD", "dv", tok_sc)
                bank2 = (P2, P3)[dc % 2]
                g.need("tensor", ("PR", dc % 2))
                last_r = None
                for k in range(CCH):
                    last_r = g.im("tensor", "matmul", bank2[:],
                                  wrTs[:, C * dc + 128 * k:C * dc + 128 * (k + 1)],
                                  ln2T[:, 512 * k:512 * (k + 1)],
                                  start=(k == 0), stop=(k == CCH - 1), sem="pe")
                g.need("scalar", ("rgrp", dc % 2))
                g.w("scalar", "pe", last_r)
                tok_r = g.im("scalar", "activation",
                             out=rgrp[:, 512 * (dc % 2):512 * (dc % 2 + 1)],
                             in_=bank2[:], func=AF.Sigmoid, sem="ac")
                g.done(("PR", dc % 2), "ac", tok_r)
                g.w("sync", "ac", tok_r)
                sh = "dh%d" % (dc % 2)
                g.dma("sync", vsp[dc][:, 512 * gi:512 * (gi + 1)],
                      vgrp[:, 512 * (dc % 2):512 * (dc % 2 + 1)], sem=sh)
                g.dma("sync", rsp[dc][:, 512 * gi:512 * (gi + 1)],
                      rgrp[:, 512 * (dc % 2):512 * (dc % 2 + 1)], sem=sh)
                g.done(("vgrp", dc % 2), sh, g.semv[sh])
                g.done(("rgrp", dc % 2), sh, g.semv[sh])
            g.done(("ln2T", gi % 2), "pe", last_r)
            if gi >= 1:
                bb_batch(gi - 1)

        if stage == "dloc":
            g.barrier()
            for t in range(NT):
                g.dma("sync", out_d[128 * t:128 * (t + 1), :],
                      xres[:, C * t:C * (t + 1)])
            g.w("sync", "ds")
            blk = es.enter_context(nc.Block())
            g.finish(blk)
            return nc

        bb_batch(7)

        tok_ep = g.mark("vector", "dv")
        g.w("tensor", "dv", tok_ep)
        g.need("tensor", "PS")
        g.i("tensor", "transpose", PS[0:8, 0:128], Epart[:, :], eye32[:, :])
        tok_tr = g.im("tensor", "transpose", PS[0:8, 128:256], a4096[:, :],
                      eye32[:, :], sem="pe")
        g.w("scalar", "pe", tok_tr)
        g.i("scalar", "copy", out=e_t8[:, :], in_=PS[0:8, 0:128])
        tok_e8 = g.im("scalar", "copy", out=e_a8[:, :], in_=PS[0:8, 128:256],
                      sem="ac")
        g.done("PS", "ac", tok_e8)
        g.w("gpsimd", "ac", tok_e8)
        g.dma("gpsimd", ag_i, e_t8[:, :])
        g.w("gpsimd", "dg")
        tok_ag = g.im("gpsimd", "collective_compute", "AllGather", ALU.bypass,
                      replica_groups=[[0, 1, 2, 3], [4, 5, 6, 7]],
                      ins=[ag_i], outs=[ag_o], sem="gp")
        g.w("gpsimd", "gp", tok_ag)
        g.dma("gpsimd", e_p0[:, :], ag_o[0:8, :])
        g.dma("gpsimd", e_p1[:, :], ag_o[8:16, :])
        g.dma("gpsimd", e_p2[:, :], ag_o[16:24, :])
        g.w("vector", "dg", g.semv["dg"])
        g.i("vector", "tensor_mul", out=e_c2[:], in0=e_a8[:], in1=e_p0[:])
        g.i("vector", "tensor_add", out=e_c2[:], in0=e_c2[:], in1=e_p1[:])
        g.i("vector", "tensor_mul", out=e_c3[:], in0=e_a8[:], in1=e_c2[:])
        g.i("vector", "tensor_add", out=e_c3[:], in0=e_c3[:], in1=e_p2[:])
        g.i("vector", "tensor_scalar", out=e_s1[:], in0=e_p0[:],
            scalar1=oh_c[0:8, 1:2], scalar2=None, op0=ALU.mult)
        g.i("vector", "scalar_tensor_tensor", out=e_s2[:], in0=e_c2[:],
            scalar=oh_c[0:8, 2:3], in1=e_s1[:], op0=ALU.mult, op1=ALU.add)
        tok_sel = g.im("vector", "scalar_tensor_tensor", out=e_s1[:],
                       in0=e_c3[:], scalar=oh_c[0:8, 3:4], in1=e_s2[:],
                       op0=ALU.mult, op1=ALU.add, sem="dv")
        g.w("tensor", "dv", tok_sel)
        tok_trb = g.im("tensor", "transpose", PS[:, 0:8], e_s1[:, :],
                       eye32[0:8, 0:8], sem="pe")
        g.w("scalar", "pe", tok_trb)
        tok_kap = g.im("scalar", "copy", out=kap[:, :], in_=PS[:, 0:8],
                       sem="ac")
        g.done("PS", "ac", tok_kap)
        g.barrier()
        sD.close()

        if stage == "dag":
            for t in range(NT):
                g.dma("sync", out_d[128 * t:128 * (t + 1), :],
                      xres[:, C * t:C * (t + 1)])
            g.w("sync", "ds")
            blk = es.enter_context(nc.Block())
            g.finish(blk)
            return nc

        sE = es.enter_context(contextlib.ExitStack())
        woTs = g.sbuf("woTs", [128, CCH * C], BF16, sE)
        vld = g.sbuf("vld", [128, 2 * CCH * 512], BF16, sE)
        rld = g.sbuf("rld", [128, CCH * 512], BF16, sE)
        rdld = g.sbuf("rdld", [128, CCH * 512], BF16, sE)
        rwg = g.sbuf("rwg", [128, CCH * 512], BF16, sE)
        aab = g.sbuf("aab", [128, 2 * 520], F32, sE)
        bvsE = g.sbuf("bvsE", [128, 2 * 512], F32, sE)
        numsE = g.sbuf("numsE", [128, 512], F32, sE)
        lnsE = g.sbuf("lnsE", [128, 512], F32, sE)
        wkvE = g.sbuf("wkvE", [128, 512], F32, sE)

        for dc in range(CCH):
            g.dma("sync", woTs[:, C * dc:C * (dc + 1)], wob[dc], sem="dm0")
        for gi in range(2):
            g.dma("sync", vld[:, CCH * 512 * gi:CCH * 512 * (gi + 1)],
                  vsp.rearrange("dc p j -> p dc j")
                  [:, :, 512 * gi:512 * (gi + 1)], sem="dc%d" % (gi % 2))
        g.dma("sync", rld[:, :],
              rsp.rearrange("dc p j -> p dc j")[:, :, 0:512], sem="dcr")
        g.dma("sync", rdld[:, :],
              rdsp.rearrange("dc p j -> p dc j")[:, :, 0:512], sem="dcr")
        g.w("tensor", "dm0")
        g.w("tensor", "dc0")
        g.w("tensor", "dc1")
        g.w("tensor", "dcr")

        rec_h1 = {}
        rec_h2 = {}
        for gi in range(8):
            slot = gi % 2
            vb = slot * CCH * 512
            g.w("vector", "dc%d" % (gi % 2))
            g.w("scalar", "dc%d" % (gi % 2))
            g.w("vector", "dcr", rec_h1.get(gi) or g.semv["dcr"])
            g.w("scalar", "dcr", rec_h1.get(gi) or g.semv["dcr"])
            for dc in range(CCH):
                dpar = dc % 2
                if dc == 4:
                    v2 = rec_h2.get(gi) or g.semv["dm1"]
                    g.w("vector", "dm1", v2)
                    g.w("scalar", "dm1", v2)
                if dc == 4 and gi + 1 < 8:
                    g.w("sync", "dv")
                    g.dma("sync", rld[:, 0:4 * 512],
                          rsp.rearrange("dc p j -> p dc j")
                          [:, 0:4, 512 * (gi + 1):512 * (gi + 2)], sem="dcr")
                    g.dma("sync", rdld[:, 0:4 * 512],
                          rdsp.rearrange("dc p j -> p dc j")
                          [:, 0:4, 512 * (gi + 1):512 * (gi + 2)], sem="dcr")
                    rec_h1[gi + 1] = g.semv["dcr"]
                vsl = vld[:, vb + 512 * dc:vb + 512 * (dc + 1)]
                rsl = rld[:, 512 * dc:512 * (dc + 1)]
                g.need("scalar", ("bvsE", dpar))
                tok_bv = g.im("scalar", "activation",
                              out=bvsE[:, 512 * dpar:512 * (dpar + 1)],
                              in_=vsl, func=AF.Identity,
                              scale=gbt[:, dc:dc + 1], sem="ac")
                ab = aab[:, 520 * dpar:520 * dpar + 520]
                g.need("vector", ("aab", dpar))
                g.i("vector", "tensor_copy", out=ab[:, 0:1],
                    in_=kap[:, dc:dc + 1])
                g.w("vector", "ac", tok_bv)
                tok_ta = g.im("vector", "tensor_tensor_scan", out=ab[:, 1:513],
                              data0=alph[:, dc:dc + 1].broadcast_to([128, 512]),
                              data1=bvsE[:, 512 * dpar:512 * (dpar + 1)],
                              initial=kap[:, dc:dc + 1], op0=ALU.mult,
                              op1=ALU.add, sem="dv")
                g.done(("bvsE", dpar), "dv", tok_ta)
                g.w("vector", "dv", tok_ta)
                g.i("vector", "tensor_copy", out=kap[:, dc:dc + 1],
                    in_=ab[:, 512:513])
                g.need("vector", "numsE")
                tok_nm = g.im("vector", "scalar_tensor_tensor",
                              out=numsE[:],
                              in0=vsl, scalar=delt[:, dc:dc + 1],
                              in1=ab[:, 0:512], op0=ALU.mult, op1=ALU.add,
                              sem="dv")
                g.done(("aab", dpar), "dv", tok_nm)
                g.need("vector", "wkvE")
                g.i("vector", "tensor_mul", out=wkvE[:],
                    in0=numsE[:],
                    in1=rdld[:, 512 * dc:512 * (dc + 1)])
                g.done("numsE", "dv", g.semv["dv"])
                g.need("vector", ("rwg", dc))
                tok_rw = g.im("vector", "tensor_mul",
                              out=rwg[:, 512 * dc:512 * (dc + 1)],
                              in0=rsl, in1=wkvE[:], sem="dv")
                g.done("wkvE", "dv", tok_rw)
            tok_rwall = g.semv["dv"]
            g.done(("vld", slot), "dv", tok_rwall)
            if gi + 2 < 8:
                g.need("sync", ("vld", slot))
                g.dma("sync", vld[:, vb:vb + CCH * 512],
                      vsp.rearrange("dc p j -> p dc j")
                      [:, :, 512 * (gi + 2):512 * (gi + 3)],
                      sem="dc%d" % (gi % 2))
            if gi + 1 < 8:
                g.w("sync", "dv", tok_rwall)
                g.dma("sync", rld[:, 4 * 512:CCH * 512],
                      rsp.rearrange("dc p j -> p dc j")
                      [:, 4:CCH, 512 * (gi + 1):512 * (gi + 2)], sem="dm1")
                g.dma("sync", rdld[:, 4 * 512:CCH * 512],
                      rdsp.rearrange("dc p j -> p dc j")
                      [:, 4:CCH, 512 * (gi + 1):512 * (gi + 2)], sem="dm1")
                rec_h2[gi + 1] = g.semv["dm1"]
            g.w("tensor", "dv", tok_rwall)
            for tc in range(4):
                t = 4 * gi + tc
                bA, bB = ((P0, P1), (P2, P3))[tc % 2]
                g.need("tensor", ("PO", tc % 2))
                last = None
                for dc in range(CCH):
                    stat = rwg[:, 512 * dc + 128 * tc:512 * dc + 128 * tc + 128]
                    g.im("tensor", "matmul", bA[:], stat,
                         woTs[:, C * dc:C * dc + 512],
                         start=(dc == 0), stop=(dc == CCH - 1), sem="pe")
                    last = g.im("tensor", "matmul", bB[:], stat,
                                woTs[:, C * dc + 512:C * dc + 1024],
                                start=(dc == 0), stop=(dc == CCH - 1),
                                sem="pe")
                xsl = xres[:, C * t:C * (t + 1)]
                g.w("vector", "pe", last)
                g.i("vector", "tensor_tensor", out=xsl[:, 0:512],
                    in0=xsl[:, 0:512], in1=bA[:], op=ALU.add)
                tok_add = g.im("vector", "tensor_tensor", out=xsl[:, 512:1024],
                               in0=xsl[:, 512:1024], in1=bB[:], op=ALU.add,
                               sem="dv")
                g.done(("PO", tc % 2), "dv", tok_add)
                g.i("vector", "reduce_max", out=st_max[:, t:t + 1], in_=xsl,
                    axis=AX.X)
                g.i("vector", "tensor_reduce", out=st_min[:, t:t + 1], in_=xsl,
                    axis=AX.X, op=ALU.min)
                g.w("scalar", "dv", tok_add)
                g.i("scalar", "activation", out=lnsE[:], in_=xsl[:, 0:512],
                    func=AF.Identity, accum_out=st_sum[:, t:t + 1])
                g.i("scalar", "activation", out=lnsE[:], in_=xsl[:, 512:1024],
                    func=AF.Identity, accum_out=m_b[:, t:t + 1])
                g.i("scalar", "activation", out=lnsE[:], in_=xsl[:, 0:512],
                    func=AF.Square, accum_out=st_sq[:, t:t + 1])
                g.im("scalar", "activation", out=lnsE[:],
                     in_=xsl[:, 512:1024], func=AF.Square,
                     accum_out=stt_sc[:, t:t + 1], sem="ac")
            for dc in range(CCH):
                g.done(("rwg", dc), "pe", g.semv["pe"])

        tok_3v = g.mark("vector", "dv")
        tok_3s = g.mark("scalar", "ac")
        g.w("vector", "dv", tok_3v)
        g.w("vector", "ac", tok_3s)
        g.i("vector", "tensor_add", out=st_sum[:], in0=st_sum[:], in1=m_b[:])
        g.i("vector", "tensor_add", out=st_sq[:], in0=st_sq[:], in1=stt_sc[:])
        t_r3, t_am3 = ln_finalize(m_a, r_a, True)

        if stage == "attn":
            g.barrier()
            for t in range(NT):
                g.dma("sync", out_d[128 * t:128 * (t + 1), :],
                      xres[:, C * t:C * (t + 1)])
            g.w("sync", "ds")
            blk = es.enter_context(nc.Block())
            g.finish(blk)
            return nc

        tok_sx2 = global_max_scale(t_am3, 0)
        g.w("vector", "ac", tok_sx2)
        g.i("vector", "tensor_scalar", out=eff_g[:], in0=s_wg[:],
            scalar1=scb[:, 0:1], scalar2=None, op0=ALU.mult)
        g.im("vector", "tensor_scalar", out=eff_u[:], in0=s_wu[:],
             scalar1=scb[:, 0:1], scalar2=None, op0=ALU.mult, sem="dv")
        g.barrier()
        sE.close()
        ffn_run(m_a, r_a, True)
        g.w("sync", "ds")
        g.w("gpsimd", "dg")
        blk = es.enter_context(nc.Block())
        g.finish(blk)
        return nc


_NC_CACHE = {}


def kernel(L0_attn_x, L0_attn_time_decay, L0_attn_time_first, L0_attn_key_w,
           L0_attn_value_w, L0_attn_recep_w, L0_attn_out_w, L0_ffn_clip_x,
           L0_ffn_clip_down, L0_ffn_w_gate, L0_ffn_w_up, L0_ffn_w_down):
    x = np.ascontiguousarray(np.asarray(L0_attn_x, dtype=np.float32))
    td = np.ascontiguousarray(
        np.asarray(L0_attn_time_decay, np.float32).reshape(CCH, 128))
    tf = np.ascontiguousarray(
        np.asarray(L0_attn_time_first, np.float32).reshape(CCH, 128))
    wv = np.ascontiguousarray(np.asarray(L0_attn_value_w, np.float32))
    wr = np.ascontiguousarray(np.asarray(L0_attn_recep_w, np.float32))
    wo = np.ascontiguousarray(np.asarray(L0_attn_out_w, np.float32))
    wg = np.ascontiguousarray(np.asarray(L0_ffn_w_gate, np.float32))
    wu = np.ascontiguousarray(np.asarray(L0_ffn_w_up, np.float32))
    wd = np.ascontiguousarray(np.asarray(L0_ffn_w_down, np.float32))
    clips = np.ascontiguousarray(
        np.stack([np.asarray(L0_ffn_clip_x, np.float32).reshape(1),
                  np.asarray(L0_ffn_clip_down, np.float32).reshape(1)],
                 axis=1))
    eye = np.eye(128, dtype=np.float32)

    if STAGE not in _NC_CACHE:
        _NC_CACHE[STAGE] = build_program(STAGE)
    nc = _NC_CACHE[STAGE]

    in_maps = []
    for r in range(8):
        b, q = r // 4, r % 4
        oh = np.zeros((4, 1), np.float32)
        oh[q, 0] = 1.0
        in_maps.append({
            "x": np.ascontiguousarray(x[b, TOK * q:TOK * (q + 1), :]),
            "w_gate": wg, "w_up": wu, "w_down": wd,
            "w_val": wv, "w_rec": wr, "w_out": wo,
            "tdec": td, "tfir": tf, "clips": clips,
            "eye": eye, "onehot": oh,
        })
    res = run_bass_kernel_spmd(nc, in_maps, core_ids=list(range(8)))
    out = np.empty((B, T, C), np.float32)
    for r in range(8):
        b, q = r // 4, r % 4
        out[b, TOK * q:TOK * (q + 1), :] = res.results[r]["out"]
    return out


# version 16
# speedup vs baseline: 1.0212x; 1.0072x over previous
import contextlib
import numpy as np
import ml_dtypes
import concourse.bass as bass
import concourse.mybir as mybir
from concourse.bass_utils import run_bass_kernel_spmd

F32 = mybir.dt.float32
BF16 = mybir.dt.bfloat16
F16 = mybir.dt.float16
AF = mybir.ActivationFunctionType
ALU = mybir.AluOpType
AX = mybir.AxisListType

MAGIC = float(np.float32(1.5 * 2**23))
QMAX = 127.0

B, T, C, I = 2, 16384, 1024, 2816
TOK = 4096
NT = TOK // 128
CCH = C // 128
ICH = I // 128
STAGE = "full"

ENGS = ("sync", "vector", "scalar", "tensor", "gpsimd")


class Bld:
    def __init__(self, nc):
        self.nc = nc
        self.prog = {e: [] for e in ENGS}
        self.es = contextlib.ExitStack()
        self.semh = {}
        self.semv = {}
        self.guard = {}

    def sem(self, name):
        self.semh[name] = self.es.enter_context(self.nc.semaphore(name))
        self.semv[name] = 0

    def sbuf(self, name, shape, dt, scope=None):
        scope = scope if scope is not None else self.es
        return scope.enter_context(self.nc.sbuf_tensor(name, shape, dt))

    def psum(self, name, shape, dt):
        return self.es.enter_context(self.nc.psum_tensor(name, shape, dt))

    def w(self, eng, sem, val=None):
        v = self.semv[sem] if val is None else val
        if v <= 0:
            return v
        h = self.semh[sem]
        self.prog[eng].append([lambda E: E.wait_ge(h, v), None, False])
        return v

    SELF = {"vector": "dv", "scalar": "ac", "gpsimd": "gp"}

    def _emit(self, eng, meth, a, kw, extra_incs):
        if eng in self.SELF and meth != "dma_start":
            s = self.SELF[eng]
            h = self.semh[s]
            v0 = self.semv[s]
            if v0 > 0:
                self.prog[eng].append(
                    [lambda E: E.wait_ge(h, v0), None, False])
            self.semv[s] = v0 + 1
            incs = [(h, 1)] + list(extra_incs)
            self.prog[eng].append(
                [lambda E: getattr(E, meth)(*a, **kw), incs, True])
            return self.semv[s]
        self.prog[eng].append(
            [lambda E: getattr(E, meth)(*a, **kw), list(extra_incs),
             meth != "dma_start"])
        return None

    def i(self, eng, meth, *a, **kw):
        self._emit(eng, meth, a, kw, [])

    def im(self, eng, meth, *a, sem=None, amt=1, **kw):
        if eng in self.SELF and meth != "dma_start":
            assert sem == self.SELF[eng], (eng, sem)
            return self._emit(eng, meth, a, kw, [])
        h = self.semh[sem]
        self.semv[sem] += amt
        v = self.semv[sem]
        self._emit(eng, meth, a, kw, [(h, amt)])
        return v

    def mark(self, eng, sem, amt=1):
        if eng in self.SELF:
            assert sem == self.SELF[eng], (eng, sem)
            return self.semv[sem]
        for ent in reversed(self.prog[eng]):
            if ent[2]:
                if not ent[1]:
                    h = self.semh[sem]
                    self.semv[sem] += amt
                    ent[1].append((h, amt))
                    return self.semv[sem]
                break
        return self.im(eng, "nop", sem=sem, amt=amt)

    DMA_SEMS = ("ds", "dg", "dx", "dw0", "dw1", "dm0", "dm1",
                "dc0", "dc1", "dcr", "dh0", "dh1", "dt")

    def dma(self, eng, out, in_, sem=None, **kw):
        if sem is None:
            sem = "dg" if eng == "gpsimd" else "ds"
        return self.im(eng, "dma_start", sem=sem, amt=16, out=out, in_=in_, **kw)

    def need(self, eng, key):
        tk = self.guard.get(key)
        if tk is not None:
            self.w(eng, tk[0], tk[1])

    def done(self, key, sem, val):
        self.guard[key] = (sem, val)

    def barrier(self):
        for s in self.DMA_SEMS:
            if s == "dg":
                self.w("gpsimd", s)
            else:
                self.w("sync", s)
        toks = {}
        for e in ENGS:
            if e == "gpsimd":
                self.i(e, "memset", self.barsc_g[0:1, 0:1], 0.0)
                toks[e] = ("gp", self.semv["gp"])
            elif e in self.SELF:
                toks[e] = (self.SELF[e], self.semv[self.SELF[e]])
            elif e == "tensor":
                toks[e] = ("btensor", self.mark(e, "btensor"))
            else:
                toks[e] = ("bsync", self.im(e, "nop", sem="bsync", amt=1))
        for e in ENGS:
            for e2 in ENGS:
                if e2 != e:
                    self.w(e, toks[e2][0], toks[e2][1])

    def finish(self, blk):
        for e in ENGS:
            ops = self.prog[e]

            def runner(E, ops=ops):
                for fn, incs, _ in ops:
                    ins = fn(E)
                    if incs:
                        for h, amt in incs:
                            ins = ins.then_inc(h, amt)
            
            getattr(blk, e)(runner)


def build_program(stage=None):
    stage = stage or STAGE
    nc = bass.Bass("TRN2", target_bir_lowering=False, debug=False)
    g = Bld(nc)

    x_in = nc.dram_tensor("x", [TOK, C], F32, kind="ExternalInput").ap()
    wg_in = nc.dram_tensor("w_gate", [I, C], F32, kind="ExternalInput").ap()
    wu_in = nc.dram_tensor("w_up", [I, C], F32, kind="ExternalInput").ap()
    wd_in = nc.dram_tensor("w_down", [C, I], F32, kind="ExternalInput").ap()
    wv_in = nc.dram_tensor("w_val", [C, C], F32, kind="ExternalInput").ap()
    wr_in = nc.dram_tensor("w_rec", [C, C], F32, kind="ExternalInput").ap()
    wo_in = nc.dram_tensor("w_out", [C, C], F32, kind="ExternalInput").ap()
    td_in = nc.dram_tensor("tdec", [CCH, 128], F32, kind="ExternalInput").ap()
    tf_in = nc.dram_tensor("tfir", [CCH, 128], F32, kind="ExternalInput").ap()
    clip_in = nc.dram_tensor("clips", [1, 2], F32, kind="ExternalInput").ap()
    eye_in = nc.dram_tensor("eye", [128, 128], F32, kind="ExternalInput").ap()
    oh_in = nc.dram_tensor("onehot", [4, 1], F32, kind="ExternalInput").ap()
    out_d = nc.dram_tensor("out", [TOK, C], F32, kind="ExternalOutput").ap()

    wgb = nc.dram_tensor("wgb", [ICH, 128, C], BF16, kind="Internal").ap()
    wub = nc.dram_tensor("wub", [ICH, 128, C], BF16, kind="Internal").ap()
    wdb = nc.dram_tensor("wdb", [ICH, 128, C], F16, kind="Internal").ap()
    wvb = nc.dram_tensor("wvb", [CCH, 128, C], BF16, kind="Internal").ap()
    wrb = nc.dram_tensor("wrb", [CCH, 128, C], BF16, kind="Internal").ap()
    wob = nc.dram_tensor("wob", [CCH, 128, C], BF16, kind="Internal").ap()
    hsp = nc.dram_tensor("hsp", [ICH, 128, TOK], F16, kind="Internal").ap()
    vsp = nc.dram_tensor("vsp", [CCH, 128, TOK], BF16, kind="Internal").ap()
    rsp = nc.dram_tensor("rsp", [CCH, 128, TOK], BF16, kind="Internal").ap()
    rdsp = nc.dram_tensor("rdsp", [CCH, 128, TOK], BF16, kind="Internal").ap()
    mx_i = nc.dram_tensor("mx_i", [1, 8], F32, kind="Internal").ap()
    mx_o = nc.dram_tensor("mx_o", [1, 8], F32, kind="Internal").ap()
    ag_i = nc.dram_tensor("ag_i", [8, 128], F32, kind="Internal").ap()
    ag_o = nc.dram_tensor("ag_o", [32, 128], F32, kind="Internal").ap()
    col_d = nc.dram_tensor("col_d", [1, 128], F32, kind="Internal").ap()
    sc_d = nc.dram_tensor("sc_d", [1, 4], F32, kind="Internal").ap()

    es = g.es
    with es:
        for s in Bld.DMA_SEMS + ("pe", "dv", "ac", "gp"):
            g.sem(s)
        for e in ENGS:
            g.sem("b" + e)

        xres = g.sbuf("xres", [128, NT * C], F32)
        eye32 = g.sbuf("eye32", [128, 128], F32)
        eyeb = g.sbuf("eyeb", [128, 128], BF16)
        eyeh = g.sbuf("eyeh", [128, 128], F16)
        st_sum = g.sbuf("st_sum", [128, NT], F32)
        st_sq = g.sbuf("st_sq", [128, NT], F32)
        st_max = g.sbuf("st_max", [128, NT], F32)
        st_min = g.sbuf("st_min", [128, NT], F32)
        m_a = g.sbuf("m_a", [128, NT], F32)
        r_a = g.sbuf("r_a", [128, NT], F32)
        m_b = g.sbuf("m_b", [128, NT], F32)
        r_b = g.sbuf("r_b", [128, NT], F32)
        stt_sc = g.sbuf("stt_sc", [128, NT], F32)
        s_wg = g.sbuf("s_wg", [128, ICH], F32)
        s_wu = g.sbuf("s_wu", [128, ICH], F32)
        s_wd = g.sbuf("s_wd", [128, CCH], F32)
        eff_g = g.sbuf("eff_g", [128, ICH], F32)
        eff_u = g.sbuf("eff_u", [128, ICH], F32)
        alph = g.sbuf("alph", [128, CCH], F32)
        gbt = g.sbuf("gbt", [128, CCH], F32)
        delt = g.sbuf("delt", [128, CCH], F32)
        dplus = g.sbuf("dplus", [128, CCH], F32)
        a512 = g.sbuf("a512", [128, CCH], F32)
        a4096 = g.sbuf("a4096", [128, CCH], F32)
        kap = g.sbuf("kap", [128, CCH], F32)
        kap1a = g.sbuf("kap1a", [128, CCH], F32)
        kap1b = g.sbuf("kap1b", [128, CCH], F32)
        z512 = g.sbuf("z512", [128, CCH], F32)
        at1 = g.sbuf("at1", [128, CCH], F32)
        at2 = g.sbuf("at2", [128, CCH], F32)
        at3 = g.sbuf("at3", [128, CCH], F32)
        td_t = g.sbuf("td_t", [128, CCH], F32)
        tf_t = g.sbuf("tf_t", [128, CCH], F32)
        ones8 = g.sbuf("ones8", [128, CCH], F32)
        zc = g.sbuf("zc", [128, CCH], F32)
        acd = g.sbuf("acd", [128, CCH], F32)
        Epart = g.sbuf("Epart", [128, CCH], F32)
        magic_c = g.sbuf("magic_c", [128, 1], F32)
        scb = g.sbuf("scb", [128, 4], F32)
        ones_row = g.sbuf("ones_row", [1, 128], F32)
        neg1_c = g.sbuf("neg1_c", [128, 1], F32)
        zro_c = g.sbuf("zro_c", [128, 1], F32)
        one_c = g.sbuf("one_c", [128, 1], F32)
        red_c = g.sbuf("red_c", [128, 1], F32)
        oh_c = g.sbuf("oh_c", [128, 4], F32)
        sc_t = g.sbuf("sc_t", [1, 8], F32)
        sc_t2 = g.sbuf("sc_t2", [1, 8], F32)
        row128 = g.sbuf("row128", [1, 128], F32)
        clip_s = g.sbuf("clip_s", [1, 2], F32)
        g.barsc_v = g.sbuf("barsc_v", [1, 2], F32)
        g.barsc_s = g.sbuf("barsc_s", [1, 2], F32)
        g.barsc_g = g.sbuf("barsc_g", [1, 2], F32)

        P0 = g.psum("P0", [128, 512], F32)
        P1 = g.psum("P1", [128, 512], F32)
        P2 = g.psum("P2", [128, 512], F32)
        P3 = g.psum("P3", [128, 512], F32)
        PT0 = g.psum("PT0", [128, 1024], BF16)
        PT1 = g.psum("PT1", [128, 1024], BF16)
        PT2 = g.psum("PT2", [128, 1024], BF16)
        PS = g.psum("PS", [128, 512], F32)

        sW = es.enter_context(contextlib.ExitStack())
        td8 = g.sbuf("td8", [8, 128], F32, sW)
        tf8 = g.sbuf("tf8", [8, 128], F32, sW)
        z1loc = g.sbuf("z1loc", [128, 512], F32, sW)
        qsq = g.sbuf("qsq", [128, 1024], F32, sW)
        wq1 = g.sbuf("wq1", [128, I], F32, sW)
        wq1b = g.sbuf("wq1b", [128, I], F32, sW)
        wqo = g.sbuf("wqo", [128, I], BF16, sW)
        wqob = g.sbuf("wqob", [128, I], BF16, sW)
        wst = g.sbuf("wst", [128, I], BF16, sW)
        wstb = g.sbuf("wstb", [128, I], BF16, sW)
        winv = g.sbuf("winv", [128, 2], F32, sW)

        g.dma("sync", eye32[:], eye_in)
        g.dma("sync", clip_s[:], clip_in)
        tok_eye = g.semv["ds"]
        g.dma("sync", td8[:], td_in)
        g.dma("sync", tf8[:], tf_in)
        tok_tdtf = g.semv["ds"]
        for p in range(4):
            g.dma("sync", oh_c[:, p:p + 1],
                  oh_in[p:p + 1, 0:1].broadcast_to([128, 1]))
        tok_oh = g.semv["ds"]
        xobs = {}
        for t in range(NT):
            if t % 4 == 0 and t > 0:
                g.w("sync", "dx")
            g.dma("sync", xres[:, C * t:C * (t + 1)],
                  x_in[128 * t:128 * (t + 1), :], sem="dx")
            if t % 4 == 3:
                g.w("gpsimd", "dx")
                xobs[t // 4] = g.im("gpsimd", "memset",
                                    g.barsc_g[0:1, 0:1], 0.0, sem="gp")

        g.i("vector", "memset", magic_c[:], MAGIC)
        g.i("vector", "memset", zro_c[:], 0.0)
        g.i("vector", "memset", one_c[:], 1.0)
        g.i("vector", "memset", neg1_c[:], -1.0)
        g.i("vector", "memset", ones8[:], 1.0)
        g.i("vector", "memset", ones_row[:], 1.0)
        g.i("vector", "memset", sc_t[:], 0.0)
        g.i("vector", "memset", Epart[:], 0.0)
        g.w("vector", "ds")
        g.i("vector", "tensor_copy", out=eyeb[:], in_=eye32[:])
        tok_eyes = g.im("vector", "tensor_copy", out=eyeh[:], in_=eye32[:], sem="dv")

        g.w("tensor", "ds")
        g.w("tensor", "dv", tok_eyes)
        g.i("tensor", "transpose", PS[:, 0:8], td8[:, :], eye32[0:8, 0:8])
        tok_tdT = g.im("tensor", "transpose", PS[:, 8:16], tf8[:, :],
                       eye32[0:8, 0:8], sem="pe")
        g.w("scalar", "pe", tok_tdT)
        g.i("scalar", "copy", out=td_t[:], in_=PS[:, 0:8])
        tok_tft = g.im("scalar", "copy", out=tf_t[:], in_=PS[:, 8:16], sem="ac")
        g.done("PS", "ac", tok_tft)
        g.w("vector", "ac", tok_tft)
        g.i("vector", "tensor_scalar", out=at1[:], in0=td_t[:], scalar1=0.0,
            scalar2=None, op0=ALU.max)
        g.i("vector", "tensor_scalar", out=at2[:], in0=at1[:], scalar1=-1.0,
            scalar2=-87.0, op0=ALU.mult, op1=ALU.max)
        tok_v = g.im("vector", "tensor_scalar", out=at3[:], in0=td_t[:],
                     scalar1=0.0, scalar2=-87.0, op0=ALU.min, op1=ALU.max,
                     sem="dv")
        g.w("scalar", "dv", tok_v)
        g.i("scalar", "activation", out=alph[:], in_=at2[:], func=AF.Exp)
        tok_be = g.im("scalar", "activation", out=gbt[:], in_=at3[:],
                      func=AF.Exp, sem="ac")
        g.w("vector", "ac", tok_be)
        g.i("vector", "tensor_scalar", out=at2[:], in0=tf_t[:], scalar1=0.0,
            scalar2=None, op0=ALU.max)
        g.i("vector", "tensor_scalar", out=at2[:], in0=at2[:], scalar1=-1.0,
            scalar2=-87.0, op0=ALU.mult, op1=ALU.max)
        tok_v = g.im("vector", "tensor_scalar", out=at3[:], in0=tf_t[:],
                     scalar1=0.0, scalar2=-87.0, op0=ALU.min, op1=ALU.max,
                     sem="dv")
        g.w("scalar", "dv", tok_v)
        g.i("scalar", "activation", out=delt[:], in_=at3[:], func=AF.Exp)
        tok_ga = g.im("scalar", "activation", out=at3[:], in_=at2[:],
                      func=AF.Exp, sem="ac")
        g.w("vector", "ac", tok_ga)
        g.i("vector", "tensor_mul", out=gbt[:], in0=gbt[:], in1=at3[:])
        g.i("vector", "tensor_scalar", out=dplus[:], in0=delt[:], scalar1=1e-6,
            scalar2=None, op0=ALU.add)
        tok_v = g.im("vector", "tensor_scalar", out=at2[:], in0=at1[:],
                     scalar1=-512.0, scalar2=-87.0, op0=ALU.mult, op1=ALU.max,
                     sem="dv")
        g.w("scalar", "dv", tok_v)
        tok_a5 = g.im("scalar", "activation", out=a512[:], in_=at2[:],
                      func=AF.Exp, sem="ac")
        g.w("vector", "ac", tok_a5)
        for dc in range(CCH):
            tok_zs = g.im("vector", "tensor_tensor_scan", out=z1loc[:],
                          data0=alph[:, dc:dc + 1].broadcast_to([128, 512]),
                          data1=gbt[:, dc:dc + 1].broadcast_to([128, 512]),
                          initial=0.0, op0=ALU.mult, op1=ALU.add, sem="dv")
            g.w("vector", "dv", tok_zs)
            g.i("vector", "tensor_copy", out=z512[:, dc:dc + 1],
                in_=z1loc[:, 511:512])
        g.i("vector", "tensor_copy", out=zc[:], in_=z512[:])
        g.i("vector", "tensor_copy", out=acd[:], in_=a512[:])
        for _ in range(3):
            g.i("vector", "tensor_scalar", out=at1[:], in0=acd[:], scalar1=1.0,
                scalar2=None, op0=ALU.add)
            g.i("vector", "tensor_mul", out=at2[:], in0=zc[:], in1=at1[:])
            g.i("vector", "tensor_copy", out=zc[:], in_=at2[:])
            g.i("vector", "tensor_mul", out=at3[:], in0=acd[:], in1=acd[:])
            g.i("vector", "tensor_copy", out=acd[:], in_=at3[:])
        g.i("vector", "tensor_copy", out=a4096[:], in_=acd[:])
        g.w("vector", "ds")
        g.i("vector", "tensor_scalar", out=at1[:], in0=a4096[:], scalar1=1.0,
            scalar2=None, op0=ALU.add)
        g.i("vector", "tensor_mul", out=at2[:], in0=a4096[:], in1=at1[:])
        g.i("vector", "tensor_scalar", out=at2[:], in0=at2[:], scalar1=1.0,
            scalar2=None, op0=ALU.add)
        g.i("vector", "tensor_scalar", out=at3[:], in0=ones8[:],
            scalar1=oh_c[:, 1:2], scalar2=None, op0=ALU.mult)
        g.i("vector", "scalar_tensor_tensor", out=kap1b[:], in0=at1[:],
            scalar=oh_c[:, 2:3], in1=at3[:], op0=ALU.mult, op1=ALU.add)
        g.i("vector", "scalar_tensor_tensor", out=at3[:], in0=at2[:],
            scalar=oh_c[:, 3:4], in1=kap1b[:], op0=ALU.mult, op1=ALU.add)
        g.im("vector", "tensor_mul", out=kap1a[:], in0=zc[:], in1=at3[:],
             sem="dv")

        def stats_tile(t):
            xt = xres[:, C * t:C * (t + 1)]
            if t % 4 == 0:
                g.w("vector", "gp", xobs[t // 4])
                g.w("scalar", "gp", xobs[t // 4])
            g.i("vector", "reduce_sum", out=st_sum[:, t:t + 1], in_=xt,
                axis=AX.X)
            g.i("vector", "reduce_max", out=st_max[:, t:t + 1], in_=xt,
                axis=AX.X)
            g.i("vector", "tensor_reduce", out=st_min[:, t:t + 1], in_=xt,
                axis=AX.X, op=ALU.min)
            g.i("scalar", "activation", out=qsq[:], in_=xt, func=AF.Square,
                accum_out=st_sq[:, t:t + 1])

        def prep_tail(rc, mode, ncols, nb, dst_blk, sr=None, bufs=None,
                      gk=("wqo", "wst")):
            sr = rc if sr is None else sr
            f16 = (mode == "qd")
            qvx, wstx = bufs if bufs else ((wqo, wqob)[rc % 2],
                                           (wst, wstb)[rc % 2])
            qv = qvx.bitcast(F16) if f16 else qvx
            sv_ = wstx.bitcast(F16) if f16 else wstx
            for kb in range(nb):
                off = (kb % 4) * 128
                PTx = (PT0, PT1)[(kb // 4) % 2]
                PTv = PTx.bitcast(F16) if f16 else PTx
                g.need("tensor", ("PT", (kb // 4) % 2))
                tok_tr = g.im("tensor", "transpose", PTv[:, off:off + 128],
                              qv[:, 128 * kb:128 * (kb + 1)],
                              (eyeh if f16 else eyeb)[:, :], sem="pe")
                if (kb % 4 == 3) or kb == nb - 1:
                    lo = (kb // 4) * 512
                    width = off + 128
                    g.w("scalar", "dt")
                    g.w("scalar", "pe", tok_tr)
                    tok_ev = g.im("scalar", "copy",
                                  out=sv_[:, lo:lo + width],
                                  in_=PTv[:, 0:width], sem="ac")
                    g.done(("PT", (kb // 4) % 2), "ac", tok_ev)
            g.done((gk[0], rc % 2), "pe", g.semv["pe"])
            g.w("scalar", "ac")
            if mode in ("qi", "cb"):
                g.dma("scalar", dst_blk[sr], sv_[:, 0:ncols], sem="dt")
            else:
                for kb in range(nb):
                    g.dma("scalar", dst_blk[kb][:, 128 * sr:128 * (sr + 1)],
                          sv_[:, 128 * kb:128 * (kb + 1)], sem="dt")
            g.done((gk[1], rc % 2), "dt", g.semv["dt"])

        L6 = ("dw0", "dw1", "dm0", "dm1", "dc0", "dc1")

        def prep_w1():
            srcs = (wg_in, wu_in)
            dsts = (wgb, wub)
            stl = (s_wg, s_wu)
            NCH = 2 * ICH
            wpair = {}
            atok = {}
            qtok = {}

            def issue_load(c):
                g.need("scalar", ("wl1", c % 6))
                g.dma("scalar", wldq[c % 6][:],
                      srcs[c % 2][128 * (c // 2):128 * (c // 2 + 1), :],
                      sem=L6[c % 6])

            def stA(c):
                g.w("vector", L6[c % 6])
                tok_rr = g.im("vector", "tensor_reduce", out=red_c[:],
                              in_=wldq[c % 6][:], axis=AX.X, op=ALU.max,
                              apply_absolute_value=True, sem="dv")
                g.w("vector", "dv", tok_rr)
                tok_st = g.im("vector", "tensor_scalar",
                              out=stl[c % 2][:, c // 2:c // 2 + 1],
                              in0=red_c[:], scalar1=1.0 / QMAX, scalar2=None,
                              op0=ALU.mult, sem="dv")
                g.w("vector", "dv", tok_st)
                return g.im("vector", "reciprocal",
                            out=winv6[:, c % 6:c % 6 + 1],
                            in_=stl[c % 2][:, c // 2:c // 2 + 1], sem="dv")

            for j in range(5):
                issue_load(j)
                atok[j] = stA(j)
            for c in range(NCH):
                if c < NT:
                    stats_tile(c)
                if c + 5 < NCH:
                    issue_load(c + 5)
                    atok[c + 5] = stA(c + 5)
                wq = (wq1, wq1b)[c % 2]
                g.need("scalar", ("wq1", c % 2))
                g.w("scalar", "dv", atok[c])
                tok_q1 = g.im("scalar", "activation", out=wq[:, 0:C],
                              in_=wldq[c % 6][:], func=AF.Identity,
                              scale=winv6[:, c % 6:c % 6 + 1],
                              bias=magic_c[:, 0:1], sem="ac")
                g.done(("wl1", c % 6), "ac", tok_q1)
                g.w("vector", "ac", tok_q1)
                g.need("vector", ("wqo", c % 2))
                tok_q = g.im("vector", "tensor_scalar",
                             out=(wqo, wqob)[c % 2][:, 0:C], in0=wq[:, 0:C],
                             scalar1=MAGIC, scalar2=None, op0=ALU.subtract,
                             sem="dv")
                g.done(("wq1", c % 2), "dv", tok_q)
                qtok[c] = tok_q
                if c >= 1:
                    g.w("tensor", "dv", qtok[c - 1])
                    prep_tail(c - 1, "qi", C, CCH, dsts[(c - 1) % 2],
                              sr=(c - 1) // 2)
                    if (c - 1) % 2 == 1:
                        g.w("gpsimd", "dt")
                        wpair[(c - 1) // 2] = g.im(
                            "gpsimd", "memset", g.barsc_g[0:1, 0:1], 0.0,
                            sem="gp")
            g.w("tensor", "dv", qtok[NCH - 1])
            prep_tail(NCH - 1, "qi", C, CCH, dsts[1], sr=ICH - 1)
            g.w("gpsimd", "dt")
            wpair[ICH - 1] = g.im("gpsimd", "memset", g.barsc_g[0:1, 0:1],
                                  0.0, sem="gp")
            return wpair

        def prep_weight(src, nrows_ch, ncols, dst_blk, s_tile, mode):
            ldb = (wld, wld2)
            q1b = (wq1, wq1b)
            nb = ncols // 128

            def wd_load(rc):
                g.need("sync", ("wld", rc % 2))
                hc = ncols // 4
                for sg in range(4):
                    g.dma("sync", ldb[rc % 2][:, hc * sg:hc * (sg + 1)],
                          src[128 * rc:128 * (rc + 1), hc * sg:hc * (sg + 1)],
                          sem="dw%d" % (rc % 2))

            def stageA(rc):
                lb = ldb[rc % 2][:, 0:ncols]
                g.w("vector", "dw%d" % (rc % 2))
                tok_rr = g.im("vector", "tensor_reduce", out=red_c[:],
                              in_=lb, axis=AX.X, op=ALU.max,
                              apply_absolute_value=True, sem="dv")
                g.w("vector", "dv", tok_rr)
                tok_st = g.im("vector", "tensor_scalar",
                              out=s_tile[:, rc:rc + 1], in0=red_c[:],
                              scalar1=1.0 / QMAX, scalar2=None,
                              op0=ALU.mult, sem="dv")
                g.w("vector", "dv", tok_st)
                return g.im("vector", "reciprocal",
                            out=winv[:, rc % 2:rc % 2 + 1],
                            in_=s_tile[:, rc:rc + 1], sem="dv")

            atok = {}
            qtok = {}
            if mode in ("qi", "qd"):
                for rc in range(nrows_ch):
                    if rc == 0:
                        wd_load(0)
                        atok[0] = stageA(0)
                    lb = ldb[rc % 2][:, 0:ncols]
                    wq = q1b[rc % 2]
                    g.need("scalar", ("wq1", rc % 2))
                    g.w("scalar", "dv", atok[rc])
                    tok_q1 = g.im("scalar", "activation", out=wq[:, 0:ncols],
                                  in_=lb, func=AF.Identity,
                                  scale=winv[:, rc % 2:rc % 2 + 1],
                                  bias=magic_c[:, 0:1], sem="ac")
                    g.done(("wld", rc % 2), "ac", tok_q1)
                    if rc + 1 < nrows_ch:
                        wd_load(rc + 1)
                        atok[rc + 1] = stageA(rc + 1)
                    g.w("vector", "ac", tok_q1)
                    g.need("vector", ("wqo", rc % 2))
                    qvx = (wqo, wqob)[rc % 2]
                    if mode == "qi":
                        tok_q = g.im("vector", "tensor_scalar",
                                     out=qvx[:, 0:ncols], in0=wq[:, 0:ncols],
                                     scalar1=MAGIC, scalar2=None,
                                     op0=ALU.subtract, sem="dv")
                    else:
                        tok_q = g.im("vector", "tensor_scalar",
                                     out=qvx.bitcast(F16)[:, 0:ncols],
                                     in0=wq[:, 0:ncols], scalar1=MAGIC,
                                     scalar2=s_tile[:, rc:rc + 1],
                                     op0=ALU.subtract, op1=ALU.mult, sem="dv")
                    g.done(("wq1", rc % 2), "dv", tok_q)
                    qtok[rc] = tok_q
                    if rc >= 1:
                        g.w("tensor", "dv", qtok[rc - 1])
                        prep_tail(rc - 1, mode, ncols, nb, dst_blk)
                g.w("tensor", "dv", qtok[nrows_ch - 1])
                prep_tail(nrows_ch - 1, mode, ncols, nb, dst_blk)
                return
            for rc in range(nrows_ch):
                g.need("sync", ("wld", rc % 2))
                g.dma("sync", ldb[rc % 2][:, 0:ncols],
                      src[128 * rc:128 * (rc + 1), 0:ncols],
                      sem="dw%d" % (rc % 2))
                lb = ldb[rc % 2][:, 0:ncols]
                if False:
                    pass
                else:
                    g.need("scalar", ("wqo", rc % 2))
                    g.w("scalar", "dw%d" % (rc % 2))
                    tok_q = g.im("scalar", "copy",
                                 out=(wqo, wqob)[rc % 2][:, 0:ncols], in_=lb,
                                 sem="ac")
                    g.done(("wld", rc % 2), "ac", tok_q)
                    qtok[rc] = tok_q
                    if rc >= 1:
                        g.w("tensor", "ac", qtok[rc - 1])
                        prep_tail(rc - 1, mode, ncols, nb, dst_blk)
            if mode not in ("qi", "qd"):
                g.w("tensor", "ac", qtok[nrows_ch - 1])
                prep_tail(nrows_ch - 1, mode, ncols, nb, dst_blk)

        def ln_finalize(m_t, r_t, with_absmax):
            tok_mm = g.im("vector", "tensor_scalar", out=m_t[:], in0=st_sum[:],
                          scalar1=1.0 / C, scalar2=None, op0=ALU.mult, sem="dv")
            g.i("vector", "tensor_scalar", out=stt_sc[:], in0=st_sq[:],
                scalar1=1.0 / C, scalar2=None, op0=ALU.mult)
            g.w("vector", "dv", tok_mm)
            g.i("vector", "tensor_mul", out=r_t[:], in0=m_t[:], in1=m_t[:])
            g.i("vector", "tensor_sub", out=stt_sc[:], in0=stt_sc[:], in1=r_t[:])
            tok = g.im("vector", "tensor_scalar", out=r_t[:], in0=stt_sc[:],
                       scalar1=1e-5, scalar2=None, op0=ALU.add, sem="dv")
            g.w("scalar", "dv", tok)
            tok2 = g.im("scalar", "activation", out=stt_sc[:], in_=r_t[:],
                        func=AF.Sqrt, sem="ac")
            g.w("vector", "ac", tok2)
            tok3 = g.im("vector", "reciprocal", out=r_t[:], in_=stt_sc[:],
                        sem="dv")
            if with_absmax:
                g.i("vector", "tensor_sub", out=stt_sc[:], in0=st_max[:],
                    in1=m_t[:])
                g.i("vector", "tensor_sub", out=st_max[:], in0=m_t[:],
                    in1=st_min[:])
                g.i("vector", "tensor_max", out=stt_sc[:], in0=stt_sc[:],
                    in1=st_max[:])
                g.i("vector", "tensor_mul", out=stt_sc[:], in0=stt_sc[:],
                    in1=r_t[:])
                g.i("vector", "reduce_max", out=red_c[:], in_=stt_sc[:],
                    axis=AX.X)
                tok4 = g.mark("vector", "dv")
                return tok3, tok4
            return tok3, None

        def global_max_scale(tok_red, clip_row):
            g.w("tensor", "dv", tok_red)
            g.need("tensor", "PS")
            tok_t = g.im("tensor", "transpose", PS[0:1, 0:128], red_c[:, 0:1],
                         eye32[:, :], sem="pe")
            g.w("vector", "pe", tok_t)
            tok_m = g.im("vector", "reduce_max", out=sc_t[:, 0:1],
                         in_=PS[0:1, 0:128], axis=AX.X, sem="dv")
            g.done("PS", "dv", tok_m)
            g.w("gpsimd", "dv", tok_m)
            g.dma("gpsimd", mx_i, sc_t[:, 0:8])
            g.w("gpsimd", "dg")
            tok_cc = g.im("gpsimd", "collective_compute", "AllReduce", ALU.max,
                          replica_groups=[list(range(8))], ins=[mx_i],
                          outs=[mx_o], sem="gp")
            g.w("gpsimd", "gp", tok_cc)
            g.dma("gpsimd", sc_t[:, 0:8], mx_o)
            g.w("scalar", "dg", g.semv["dg"])
            tok_sg = g.im("scalar", "activation", out=sc_t2[:, 0:1],
                          in_=clip_s[0:1, clip_row:clip_row + 1],
                          func=AF.Sigmoid, sem="ac")
            g.w("vector", "ac", tok_sg)
            tok_s1 = g.im("vector", "tensor_scalar", out=sc_t[:, 1:2],
                          in0=sc_t[:, 0:1], scalar1=sc_t2[0:1, 0:1],
                          scalar2=1.0 / QMAX, op0=ALU.mult, op1=ALU.mult,
                          sem="dv")
            g.w("vector", "dv", tok_s1)
            g.i("vector", "tensor_scalar", out=sc_t[:, 3:4], in0=sc_t[:, 1:2],
                scalar1=0.5, scalar2=None, op0=ALU.mult)
            tok_s = g.im("vector", "reciprocal", out=sc_t[:, 2:3],
                         in_=sc_t[:, 1:2], sem="dv")
            g.w("tensor", "dv", tok_s)
            g.need("tensor", "PS")
            tok_b = g.im("tensor", "matmul", PS[:, 0:3], ones_row[0:1, :],
                         sc_t[0:1, 1:4], start=True, stop=True, sem="pe")
            g.w("scalar", "pe", tok_b)
            tok_c = g.im("scalar", "copy", out=scb[:, 0:3], in_=PS[:, 0:3],
                         sem="ac")
            g.done("PS", "ac", tok_c)
            return tok_c

        sW1 = es.enter_context(contextlib.ExitStack())
        wldq = [g.sbuf("wldq%d" % j, [128, C], F32, sW1) for j in range(6)]
        winv6 = g.sbuf("winv6", [128, 6], F32, sW1)
        wpair = prep_w1()
        tok_av = g.mark("vector", "dv")
        tok_as = g.mark("scalar", "ac")
        g.w("vector", "dv", tok_av)
        g.w("vector", "ac", tok_as)
        t_r1, t_am1 = ln_finalize(m_a, r_a, True)
        tok_sx = global_max_scale(t_am1, 0)
        qa_w1 = g.semv["ac"]
        g.done(("wld", 0), "ac", qa_w1)
        g.done(("wld", 1), "ac", qa_w1)
        sW1.close()
        sW2 = es.enter_context(contextlib.ExitStack())
        wld = g.sbuf("wld", [128, I], F32, sW2)
        wld2 = g.sbuf("wld2", [128, I], F32, sW2)
        prep_weight(wd_in, CCH, I, wdb, s_wd, "qd")
        wstore = g.semv["dt"]

        g.w("vector", "ac", tok_sx)
        g.i("vector", "tensor_scalar", out=eff_g[:], in0=s_wg[:],
            scalar1=scb[:, 0:1], scalar2=None, op0=ALU.mult)
        g.im("vector", "tensor_scalar", out=eff_u[:], in0=s_wu[:],
             scalar1=scb[:, 0:1], scalar2=None, op0=ALU.mult, sem="dv")
        tok_eff = g.semv["dv"]
        sW2.close()
        sW.close()

        if stage == "dbg1":
            g.barrier()
            sDBG = es.enter_context(contextlib.ExitStack())
            dbf = g.sbuf("dbf", [128, 1024], F32, sDBG)
            dbh = g.sbuf("dbh", [128, 1024], BF16, sDBG)
            g.i("vector", "memset", dbf[:], 0.0)
            g.i("vector", "tensor_copy", out=dbf[:, 0:NT], in_=m_a[:])
            g.i("vector", "tensor_copy", out=dbf[:, 32:32 + NT], in_=r_a[:])
            g.i("vector", "tensor_copy", out=dbf[:, 64:64 + NT], in_=st_sum[:])
            g.i("vector", "tensor_copy", out=dbf[:, 96:96 + NT], in_=st_sq[:])
            g.i("vector", "tensor_copy", out=dbf[:, 128:129], in_=scb[:, 0:1])
            g.i("vector", "tensor_copy", out=dbf[:, 129:130], in_=scb[:, 1:2])
            g.i("vector", "tensor_copy", out=dbf[:, 130:131], in_=scb[:, 2:3])
            g.i("vector", "tensor_copy", out=dbf[:, 132:132 + ICH], in_=s_wg[:])
            g.i("vector", "tensor_copy", out=dbf[:, 160:160 + ICH], in_=s_wu[:])
            g.i("vector", "tensor_copy", out=dbf[:, 192:192 + CCH], in_=s_wd[:])
            tok_d = g.mark("vector", "dv")
            g.w("sync", "dv", tok_d)
            g.dma("sync", out_d[0:128, :], dbf[:])
            g.i("vector", "tensor_scalar", out=dbf[:], in0=xres[:, 0:C],
                scalar1=m_a[:, 0:1], scalar2=r_a[:, 0:1],
                op0=ALU.subtract, op1=ALU.mult)
            tok_d = g.mark("vector", "dv")
            g.w("sync", "dv", tok_d)
            g.dma("sync", out_d[128:256, :], dbf[:])
            g.w("scalar", "dv", tok_d)
            tok_q1d = g.im("scalar", "activation", out=dbf[:], in_=dbf[:],
                           func=AF.Identity, scale=scb[:, 1:2],
                           bias=magic_c[:, 0:1], sem="ac")
            g.w("vector", "ac", tok_q1d)
            g.i("vector", "tensor_scalar", out=dbf[:], in0=dbf[:],
                scalar1=MAGIC, scalar2=127.0, op0=ALU.subtract, op1=ALU.min)
            g.i("vector", "tensor_scalar", out=dbh[:], in0=dbf[:],
                scalar1=-128.0, scalar2=None, op0=ALU.max)
            g.i("vector", "tensor_copy", out=dbf[:], in_=dbh[:])
            tok_d = g.mark("vector", "dv")
            g.w("sync", "dv", tok_d)
            g.dma("sync", out_d[256:384, :], dbf[:])
            g.dma("sync", dbh[:], wgb[0])
            g.w("vector", "ds")
            g.i("vector", "tensor_copy", out=dbf[:], in_=dbh[:])
            tok_d = g.mark("vector", "dv")
            g.w("sync", "dv", tok_d)
            g.dma("sync", out_d[384:512, :], dbf[:])
            g.dma("sync", dbh.bitcast(F16)[:], wdb[0])
            g.w("vector", "ds")
            g.i("vector", "tensor_copy", out=dbf[:], in_=dbh.bitcast(F16)[:])
            tok_d = g.mark("vector", "dv")
            g.w("sync", "dv", tok_d)
            g.dma("sync", out_d[512:640, :], dbf[:])
            g.w("sync", "ds")
            blk = es.enter_context(nc.Block())
            g.finish(blk)
            return nc

        def ffn_run(m_t, r_t, final, dbg3=False, wgate=None, wstore_t=None,
                    eff_t=None, inject=None, post_b=None):
            tg = "F2" if final else "F1"
            sB = es.enter_context(contextlib.ExitStack())
            xqT = g.sbuf("xqT" + tg, [128, 2 * CCH * 1024], BF16, sB)
            wmb = g.sbuf("wmb" + tg, [128, 4 * C], BF16, sB)
            hstg = g.sbuf("hstg" + tg, [128, 2 * 1024], F16, sB)
            silu_s = g.sbuf("silu_s" + tg, [128, 512], F32, sB)
            habs = g.sbuf("habs" + tg, [128, 192], F32, sB)
            lncol = g.sbuf("lncol" + tg, [128, 2], F32, sB)
            qt1 = g.sbuf("qt1" + tg, [128, 1024], F32, sB)
            xq_tm = g.sbuf("xq_tm" + tg, [128, 4 * 1024], BF16, sB)

            xq_ready = {}
            xq_used = {}
            if wstore_t is not None:
                g.w("vector", "dt", wstore_t)
                g.w("scalar", "dt", wstore_t)
                g.w("sync", "dt", wstore_t)
            if eff_t is not None:
                g.w("scalar", "dv", eff_t)

            def B0(q, half):
                slot = q % 2
                base = slot * CCH * 1024
                if half == 0 and q >= 2:
                    g.w("vector", "pe", xq_used[q - 2])
                    g.w("scalar", "pe", xq_used[q - 2])
                for tl in range(4 * half, 4 * half + 4):
                    t = 8 * q + tl
                    xt = xres[:, C * t:C * (t + 1)]
                    g.need("vector", "lncol")
                    g.i("vector", "tensor_scalar", out=lncol[:, 0:1],
                        in0=r_t[:, t:t + 1], scalar1=scb[:, 1:2],
                        scalar2=None, op0=ALU.mult)
                    tok_ln = g.im("vector", "tensor_scalar",
                                  out=lncol[:, 1:2], in0=m_t[:, t:t + 1],
                                  scalar1=lncol[:, 0:1], scalar2=-1.0,
                                  op0=ALU.mult, op1=ALU.mult, sem="dv")
                    g.need("scalar", "qt1")
                    g.w("scalar", "dv", tok_ln)
                    tok_q1 = g.im("scalar", "activation", out=qt1[:], in_=xt,
                                  func=AF.Identity, scale=lncol[:, 0:1],
                                  bias=lncol[:, 1:2], sem="ac")
                    g.done("lncol", "ac", tok_q1)
                    g.w("vector", "ac", tok_q1)
                    g.i("vector", "tensor_scalar", out=qt1[:], in0=qt1[:],
                        scalar1=MAGIC, scalar2=None, op0=ALU.add)
                    g.i("vector", "tensor_scalar", out=qt1[:], in0=qt1[:],
                        scalar1=MAGIC, scalar2=127.0, op0=ALU.subtract,
                        op1=ALU.min)
                    g.need("vector", ("xq_tm", tl % 4))
                    tok_q = g.im("vector", "tensor_scalar",
                                 out=xq_tm[:, 1024 * (tl % 4):1024 * (tl % 4 + 1)],
                                 in0=qt1[:], scalar1=-128.0, scalar2=None,
                                 op0=ALU.max, sem="dv")
                    g.done("qt1", "dv", tok_q)
                    if tl % 4 == 3:
                        qq = tl // 4
                        g.w("tensor", "dv", tok_q)
                        for cc in range(CCH):
                            PTx = (PT0, PT1)[cc % 2]
                            g.need("tensor", ("PT", cc % 2))
                            tok_tr = None
                            for tj in range(4):
                                tok_tr = g.im(
                                    "tensor", "transpose",
                                    PTx[:, 128 * tj:128 * (tj + 1)],
                                    xq_tm[:, 1024 * tj + 128 * cc:
                                          1024 * tj + 128 * cc + 128],
                                    eyeb[:, :], sem="pe")
                            dst = xqT[:, base + 1024 * cc + 512 * qq:
                                      base + 1024 * cc + 512 * qq + 512]
                            if cc % 2 == 0:
                                g.w("vector", "pe", tok_tr)
                                tok_e = g.im("vector", "tensor_copy", out=dst,
                                             in_=PTx[:, 0:512], sem="dv")
                                g.done(("PT", 0), "dv", tok_e)
                            else:
                                g.w("scalar", "pe", tok_tr)
                                tok_e = g.im("scalar", "copy", out=dst,
                                             in_=PTx[:, 0:512], sem="ac")
                                g.done(("PT", 1), "ac", tok_e)
                        for tj in range(4):
                            g.done(("xq_tm", tj), "pe", g.semv["pe"])
                if half == 1:
                    xq_ready[q] = (g.mark("vector", "dv"),
                                   g.mark("scalar", "ac"))

            def B1(q):
                slot = q % 2
                base = slot * CCH * 1024
                tv, ta = xq_ready[q]
                g.w("tensor", "dv", tv)
                g.w("tensor", "ac", ta)
                last_mm = None
                wtok = {}

                def b1_load(ic):
                    if wgate is not None:
                        g.w("sync", "gp", wgate[ic])
                    for wi, wsrc in enumerate((wgb, wub)):
                        ws = (ic % 2) * 2 + wi
                        g.need("sync", ("wmb", ws))
                        g.dma("sync", wmb[:, C * ws:C * (ws + 1)], wsrc[ic],
                              sem="dm%d" % (ic % 2))
                    wtok[ic] = g.semv["dm%d" % (ic % 2)]

                ics = [0] if dbg3 else list(range(ICH))
                b1_load(ics[0])
                for ii, ic in enumerate(ics):
                    if inject is not None and ic % 3 == 1:
                        inject()
                    if not dbg3 and q + 1 < 4 and ic in (8, 15):
                        B0(q + 1, 0 if ic == 8 else 1)
                    if ii + 1 < len(ics):
                        b1_load(ics[ii + 1])
                    g.w("tensor", "dm%d" % (ic % 2), wtok[ic])
                    for g2 in range(2):
                        for wi in range(2):
                            ws = (ic % 2) * 2 + wi
                            bank = (P0, P1, P2, P3)[2 * g2 + wi]
                            g.need("tensor", ("PB", 2 * g2 + wi))
                            for k in range(CCH):
                                last_mm = g.im(
                                    "tensor", "matmul", bank[:],
                                    wmb[:, C * ws + 128 * k:
                                        C * ws + 128 * (k + 1)],
                                    xqT[:, base + 1024 * k + 512 * g2:
                                        base + 1024 * k + 512 * g2 + 512],
                                    start=(k == 0), stop=(k == CCH - 1),
                                    sem="pe")
                        g.done(("wmb", (ic % 2) * 2), "pe", last_mm)
                        g.done(("wmb", (ic % 2) * 2 + 1), "pe", last_mm)
                        gbank, ubank = (P0, P1) if g2 == 0 else (P2, P3)
                        g.need("scalar", "silu_s")
                        g.w("scalar", "pe", last_mm)
                        tok_si = g.im("scalar", "activation", out=silu_s[:],
                                      in_=gbank[:], func=AF.Silu,
                                      scale=eff_g[:, ic:ic + 1], sem="ac")
                        g.done(("PB", 2 * g2 + 0), "ac", tok_si)
                        hdst = hstg[:, 1024 * (ic % 2) + 512 * g2:
                                    1024 * (ic % 2) + 512 * g2 + 512]
                        g.need("vector", ("hstg", ic % 2))
                        g.w("vector", "ac", tok_si)
                        tok_h = g.im("vector", "scalar_tensor_tensor",
                                     out=hdst, in0=ubank[:],
                                     scalar=eff_u[:, ic:ic + 1], in1=silu_s[:],
                                     op0=ALU.mult, op1=ALU.mult, sem="dv")
                        g.done(("PB", 2 * g2 + 1), "dv", tok_h)
                        g.done("silu_s", "dv", tok_h)
                        g.i("vector", "tensor_reduce",
                            out=habs[:, 4 * (2 * ic + g2) + q:
                                     4 * (2 * ic + g2) + q + 1],
                            in_=hdst, axis=AX.X, op=ALU.max,
                            apply_absolute_value=True)
                    tok_hv = g.mark("vector", "dv")
                    g.w("sync", "dv", tok_hv)
                    sh = "dh%d" % (ic % 2)
                    g.dma("sync", hsp[ic][:, 1024 * q:1024 * (q + 1)],
                          hstg[:, 1024 * (ic % 2):1024 * (ic % 2) + 1024],
                          sem=sh)
                    g.done(("hstg", ic % 2), sh, g.semv[sh])
                xq_used[q] = last_mm

            if dbg3:
                B0(0, 0)
                B0(0, 1)
                B1(0)
                sDBG = es.enter_context(contextlib.ExitStack())
                dbf = g.sbuf("dbf", [128, 1024], F32, sDBG)
                g.barrier()
                g.i("vector", "tensor_copy", out=dbf[:], in_=xqT[:, 0:1024])
                tok_d = g.mark("vector", "dv")
                g.w("sync", "dv", tok_d)
                g.dma("sync", out_d[0:128, :], dbf[:])
                g.w("vector", "ds")
                g.i("vector", "tensor_copy", out=dbf[:, 0:512], in_=P0[:])
                g.i("vector", "tensor_copy", out=dbf[:, 512:1024], in_=P1[:])
                tok_d = g.mark("vector", "dv")
                g.w("sync", "dv", tok_d)
                g.dma("sync", out_d[128:256, :], dbf[:])
                g.w("vector", "ds")
                g.i("vector", "tensor_copy", out=dbf[:, 0:512], in_=silu_s[:])
                g.i("vector", "tensor_copy", out=dbf[:, 512:512 + ICH],
                    in_=eff_g[:])
                g.i("vector", "tensor_copy", out=dbf[:, 544:544 + ICH],
                    in_=eff_u[:])
                tok_d = g.mark("vector", "dv")
                g.w("sync", "dv", tok_d)
                g.dma("sync", out_d[256:384, :], dbf[:])
                g.w("vector", "ds")
                g.i("vector", "tensor_copy", out=dbf[:], in_=hstg[:, 0:1024])
                tok_d = g.mark("vector", "dv")
                g.w("sync", "dv", tok_d)
                g.dma("sync", out_d[384:512, :], dbf[:])
                g.w("sync", "ds")
                g.dma("sync", out_d[512:640, :], xres[:, 0:1024])
                g.w("vector", "ds")
                g.i("vector", "memset", dbf[:], 0.0)
                g.i("vector", "tensor_copy", out=dbf[:, 0:1], in_=scb[:, 1:2])
                g.i("vector", "tensor_copy", out=dbf[:, 1:2], in_=scb[:, 0:1])
                g.i("vector", "tensor_copy", out=dbf[:, 2:3], in_=magic_c[:])
                g.i("vector", "tensor_copy", out=dbf[:, 16:48], in_=m_a[:])
                g.i("vector", "tensor_copy", out=dbf[:, 48:80], in_=r_a[:])
                g.i("vector", "tensor_copy", out=dbf[:, 80:102], in_=s_wg[:])
                g.i("vector", "tensor_copy", out=dbf[:, 112:112 + 1024 - 112]
                    if False else dbf[:, 102:103], in_=red_c[:])
                tok_d = g.mark("vector", "dv")
                g.w("sync", "dv", tok_d)
                g.dma("sync", out_d[640:768, :], dbf[:])
                g.w("vector", "ds")
                g.i("vector", "tensor_copy", out=dbf[:], in_=xq_tm[:, 0:1024])
                tok_d = g.mark("vector", "dv")
                g.w("sync", "dv", tok_d)
                g.dma("sync", out_d[768:896, :], dbf[:])
                g.w("sync", "ds")
                return None
            B0(0, 0)
            B0(0, 1)
            for q in range(4):
                B1(q)
            g.w("vector", "dv")
            g.i("vector", "reduce_max", out=red_c[:], in_=habs[:, 0:176],
                axis=AX.X)
            tok_habs = g.mark("vector", "dv")
            sB.close()
            if post_b is not None:
                post_b()

            sC = es.enter_context(contextlib.ExitStack())
            wdw = g.sbuf("wdw" + tg, [128, ICH * C], F16, sC)
            hqb = g.sbuf("hqb" + tg, [128, 2 * ICH * 256], F16, sC)
            qc = g.sbuf("qc" + tg, [128, 4 * 256], F32, sC)
            dsc = g.sbuf("dsc" + tg, [128, 2 * 1024], BF16, sC)
            g.w("sync", "pe", xq_used[3])
            g.w("sync", "dv", tok_habs)
            g.w("sync", "dh0")
            g.w("sync", "dh1")
            if wstore_t is not None:
                g.w("sync", "dt", g.semv["dt"])
            for ic in range(ICH):
                g.dma("sync", wdw[:, C * ic:C * (ic + 1)], wdb[ic], sem="dm0")
            tok_sh = global_max_scale(tok_habs, 1)
            g.w("vector", "ac", tok_sh)
            g.w("scalar", "ac", tok_sh)
            g.w("tensor", "dm0")
            hq_used = {}
            hq_tok = {}

            def c_load(gi):
                slot = gi % 2
                hb = slot * ICH * 256
                if gi >= 2:
                    g.w("sync", "pe", hq_used[gi - 2])
                sc = "dc%d" % (gi % 2)
                g.dma("sync", hqb[:, hb:hb + 11 * 256],
                      hsp.rearrange("ic p j -> p ic j")
                      [:, 0:11, 256 * gi:256 * (gi + 1)], sem=sc)
                g.dma("sync", hqb[:, hb + 11 * 256:hb + ICH * 256],
                      hsp.rearrange("ic p j -> p ic j")
                      [:, 11:ICH, 256 * gi:256 * (gi + 1)], sem=sc)

            def c_quant(gi):
                slot = gi % 2
                hb = slot * ICH * 256
                sc = "dc%d" % (gi % 2)
                g.w("scalar", sc)
                for ic in range(ICH):
                    hsl = hqb[:, hb + 256 * ic:hb + 256 * (ic + 1)]
                    qs = 256 * (2 * (gi % 2) + ic % 2)
                    g.need("scalar", ("qc", gi % 2, ic % 2))
                    tok_a = g.im("scalar", "activation",
                                 out=qc[:, qs:qs + 256],
                                 in_=hsl, func=AF.Identity,
                                 scale=scb[:, 1:2], bias=magic_c[:, 0:1],
                                 sem="ac")
                    g.w("vector", "ac", tok_a)
                    g.i("vector", "tensor_scalar", out=qc[:, qs:qs + 256],
                        in0=qc[:, qs:qs + 256], scalar1=MAGIC, scalar2=127.0,
                        op0=ALU.subtract, op1=ALU.min)
                    tok_qq = g.im("vector", "tensor_scalar", out=hsl,
                                  in0=qc[:, qs:qs + 256], scalar1=-128.0,
                                  scalar2=None, op0=ALU.max, sem="dv")
                    g.done(("qc", gi % 2, ic % 2), "dv", tok_qq)
                hq_tok[gi] = g.semv["dv"]

            c_load(0)
            c_load(1)
            c_quant(0)
            for gi in range(16):
                slot = gi % 2
                hb = slot * ICH * 256
                if gi + 1 < 16:
                    c_quant(gi + 1)
                g.w("tensor", "dv", hq_tok[gi])
                last_mm = None
                for tc in range(2):
                    t = 2 * gi + tc
                    bA, bB = ((P0, P1), (P2, P3))[tc % 2]
                    g.need("tensor", ("PC", tc % 2))
                    for ic in range(ICH):
                        stat = hqb[:, hb + 256 * ic + 128 * tc:
                                   hb + 256 * ic + 128 * tc + 128]
                        g.im("tensor", "matmul", bA[:], stat,
                             wdw[:, C * ic:C * ic + 512],
                             start=(ic == 0), stop=(ic == ICH - 1), sem="pe")
                        last_mm = g.im("tensor", "matmul", bB[:], stat,
                                       wdw[:, C * ic + 512:C * ic + 1024],
                                       start=(ic == 0), stop=(ic == ICH - 1),
                                       sem="pe")
                    g.need("scalar", ("dsc", tc % 2))
                    g.w("scalar", "pe", last_mm)
                    g.i("scalar", "activation", out=dsc[:, 1024 * (tc % 2):
                                                       1024 * (tc % 2) + 512],
                        in_=bA[:], func=AF.Identity, scale=scb[:, 2:3])
                    tok_ev = g.im("scalar", "activation",
                                  out=dsc[:, 1024 * (tc % 2) + 512:
                                          1024 * (tc % 2) + 1024],
                                  in_=bB[:], func=AF.Identity,
                                  scale=scb[:, 2:3], sem="ac")
                    g.done(("PC", tc % 2), "ac", tok_ev)
                    g.w("vector", "ac", tok_ev)
                    xsl = xres[:, C * t:C * (t + 1)]
                    tok_add = g.im("vector", "tensor_tensor", out=xsl, in0=xsl,
                                   in1=dsc[:, 1024 * (tc % 2):
                                           1024 * (tc % 2) + 1024],
                                   op=ALU.add, sem="dv")
                    g.done(("dsc", tc % 2), "dv", tok_add)
                    if not final:
                        g.i("vector", "reduce_sum", out=st_sum[:, t:t + 1],
                            in_=xsl, axis=AX.X)
                        g.w("scalar", "dv", tok_add)
                        tok_sq2 = g.im("scalar", "activation",
                                       out=dsc[:, 1024 * (tc % 2):
                                               1024 * (tc % 2) + 1024],
                                       in_=xsl, func=AF.Square,
                                       accum_out=st_sq[:, t:t + 1], sem="ac")
                        g.done(("dsc", tc % 2), "ac", tok_sq2)
                hq_used[gi] = last_mm
                if gi + 2 < 16:
                    c_load(gi + 2)
                if final:
                    g.w("sync", "dv", tok_add)
                    for tc in range(2):
                        t = 2 * gi + tc
                        g.dma("sync", out_d[128 * t:128 * (t + 1), :],
                              xres[:, C * t:C * (t + 1)], sem="dt")
            g.barrier()
            sC.close()

        if stage == "dbg3":
            g.barrier()
            ffn_run(m_a, r_a, False, dbg3=True)
            blk = es.enter_context(nc.Block())
            g.finish(blk)
            return nc
        sVRO = es.enter_context(contextlib.ExitStack())
        vld = [g.sbuf("vld%d" % j2, [128, C], F32, sVRO) for j2 in range(2)]
        vqo = [g.sbuf("vqo%d" % j2, [128, C], BF16, sVRO) for j2 in range(2)]
        vst = [g.sbuf("vst%d" % j2, [128, C], BF16, sVRO) for j2 in range(2)]
        VCH = [(wv_in, wvb, "cb"), (wr_in, wrb, "cb"), (wo_in, wob, "cc")]
        VSEM = ("dx", "dcr")
        vtok = {}

        def vload(j):
            g.need("scalar", ("vld", j % 2))
            g.dma("scalar", vld[j % 2][:],
                  VCH[j // 8][0][128 * (j % 8):128 * (j % 8 + 1), :],
                  sem=VSEM[j % 2])

        def vro_item(it):
            if it < 24:
                g.need("scalar", ("vqo", it % 2))
                g.w("scalar", VSEM[it % 2])
                tok_c = g.im("scalar", "copy", out=vqo[it % 2][:],
                             in_=vld[it % 2][:], sem="ac")
                g.done(("vld", it % 2), "ac", tok_c)
                vtok[it] = tok_c
                if it + 2 < 24:
                    vload(it + 2)
            if it >= 1:
                j = it - 1
                g.w("tensor", "ac", vtok[j])
                prep_tail(j, VCH[j // 8][2], C, CCH, VCH[j // 8][1],
                          sr=j % 8, bufs=(vqo[j % 2], vst[j % 2]),
                          gk=("vqo", "vst"))

        it_ctr = [0]

        def vro_inject():
            if it_ctr[0] <= 24:
                vro_item(it_ctr[0])
                it_ctr[0] += 1

        g.w("scalar", "dt", wstore)
        vload(0)
        vload(1)
        def vro_close():
            assert it_ctr[0] > 24, it_ctr
            sVRO.close()

        ffn_run(m_a, r_a, False, wgate=wpair, wstore_t=wstore,
                eff_t=tok_eff, inject=vro_inject, post_b=vro_close)
        tok_2v = g.mark("vector", "dv")
        tok_2s = g.mark("scalar", "ac")
        g.w("vector", "dv", tok_2v)
        g.w("vector", "ac", tok_2s)
        ln_finalize(m_b, r_b, False)

        if stage in ("ffn1", "dbg2"):
            g.barrier()
            for t in range(NT):
                g.dma("sync", out_d[128 * t:128 * (t + 1), :],
                      xres[:, C * t:C * (t + 1)])
            if stage == "dbg2":
                g.w("sync", "ds")
                sDBG = es.enter_context(contextlib.ExitStack())
                dbf = g.sbuf("dbf", [128, 1024], F32, sDBG)
                dbh = g.sbuf("dbh", [128, 1024], BF16, sDBG)
                g.dma("sync", dbh.bitcast(F16)[:], hsp[0][:, 0:1024])
                g.w("vector", "ds")
                g.i("vector", "tensor_copy", out=dbf[:], in_=dbh.bitcast(F16)[:])
                tok_d = g.mark("vector", "dv")
                g.w("sync", "dv", tok_d)
                g.dma("sync", out_d[3840:3968, :], dbf[:])
                g.w("vector", "ds")
                g.i("vector", "memset", dbf[:], 0.0)
                g.i("vector", "tensor_copy", out=dbf[:, 0:1], in_=scb[:, 0:1])
                g.i("vector", "tensor_copy", out=dbf[:, 1:2], in_=scb[:, 1:2])
                g.i("vector", "tensor_copy", out=dbf[:, 2:3], in_=scb[:, 2:3])
                g.i("vector", "tensor_copy", out=dbf[:, 3:4], in_=red_c[:])
                g.i("vector", "tensor_copy", out=dbf[:, 16:16 + NT], in_=m_b[:])
                g.i("vector", "tensor_copy", out=dbf[:, 48:48 + NT], in_=r_b[:])
                tok_d = g.mark("vector", "dv")
                g.w("sync", "dv", tok_d)
                g.dma("sync", out_d[3968:4096, :], dbf[:])
            g.w("sync", "ds")
            blk = es.enter_context(nc.Block())
            g.finish(blk)
            return nc

        g.barrier()
        sD = es.enter_context(contextlib.ExitStack())
        ln2Tb = [g.sbuf("ln2T%d" % j2, [128, CCH * 512], BF16, sD)
                 for j2 in range(2)]
        wvTs = g.sbuf("wvTs", [128, CCH * C], BF16, sD)
        wrTs = g.sbuf("wrTs", [128, CCH * C], BF16, sD)
        lnb = g.sbuf("lnb", [128, 4 * 1024], BF16, sD)
        vgrp = g.sbuf("vgrp", [128, 2 * 512], BF16, sD)
        rgrp = g.sbuf("rgrp", [128, 2 * 512], BF16, sD)
        bvsD = g.sbuf("bvsD", [128, 512], F32, sD)
        zjunk = g.sbuf("zjunk", [128, 512], F32, sD)
        bbD = g.sbuf("bbD", [128, 520], F32, sD)
        lnD = g.sbuf("lnD", [128, 512], F32, sD)
        rdD = g.sbuf("rdD", [128, 2 * 512], BF16, sD)
        e_t8 = g.sbuf("e_t8", [8, 128], F32, sD)
        e_a8 = g.sbuf("e_a8", [8, 128], F32, sD)
        e_p0 = g.sbuf("e_p0", [8, 128], F32, sD)
        e_p1 = g.sbuf("e_p1", [8, 128], F32, sD)
        e_p2 = g.sbuf("e_p2", [8, 128], F32, sD)
        e_c2 = g.sbuf("e_c2", [8, 128], F32, sD)
        e_c3 = g.sbuf("e_c3", [8, 128], F32, sD)
        e_s1 = g.sbuf("e_s1", [8, 128], F32, sD)
        e_s2 = g.sbuf("e_s2", [8, 128], F32, sD)

        for dc in range(CCH):
            g.dma("sync", wvTs[:, C * dc:C * (dc + 1)], wvb[dc], sem="dm0")
            g.dma("sync", wrTs[:, C * dc:C * (dc + 1)], wrb[dc], sem="dm0")
        g.w("tensor", "dm0")

        def bb_batch(bgi):
            bk1c, bk1n = (kap1a, kap1b) if bgi % 2 == 0 else (kap1b, kap1a)
            for dc in range(CCH):
                g.need("vector", "bbD")
                g.i("vector", "tensor_copy", out=bbD[:, 0:1],
                    in_=bk1c[:, dc:dc + 1])
                tok_tb = g.im("vector", "tensor_tensor_scan",
                              out=bbD[:, 1:513],
                              data0=alph[:, dc:dc + 1].broadcast_to([128, 512]),
                              data1=gbt[:, dc:dc + 1].broadcast_to([128, 512]),
                              initial=bk1c[:, dc:dc + 1],
                              op0=ALU.mult, op1=ALU.add, sem="dv")
                g.w("vector", "dv", tok_tb)
                g.i("vector", "tensor_copy", out=bk1n[:, dc:dc + 1],
                    in_=bbD[:, 512:513])
                tok_kk = g.mark("vector", "dv")
                g.w("scalar", "dv", tok_kk)
                tok_ln = g.im("scalar", "activation", out=lnD[:],
                              in_=bbD[:, 0:512], func=AF.Ln,
                              bias=dplus[:, dc:dc + 1], sem="ac")
                g.done("bbD", "ac", tok_ln)
                g.need("scalar", ("rdD", dc % 2))
                tok_rdd = g.im("scalar", "activation",
                               out=rdD[:, 512 * (dc % 2):512 * (dc % 2 + 1)],
                               in_=lnD[:], func=AF.Exp,
                               scale=neg1_c[:, 0:1], sem="ac")
                g.w("scalar", "ac", tok_rdd)
                g.dma("scalar", rdsp[dc][:, 512 * bgi:512 * (bgi + 1)],
                      rdD[:, 512 * (dc % 2):512 * (dc % 2 + 1)],
                      sem="dc%d" % (dc % 2))
                g.done(("rdD", dc % 2), "dc%d" % (dc % 2),
                       g.semv["dc%d" % (dc % 2)])

        def ln_write(gi2, tl):
            t = 4 * gi2 + tl
            xt = xres[:, C * t:C * (t + 1)]
            g.need("vector", "lnb")
            return g.im("vector", "tensor_scalar",
                        out=lnb[:, 1024 * tl:1024 * (tl + 1)], in0=xt,
                        scalar1=m_b[:, t:t + 1], scalar2=r_b[:, t:t + 1],
                        op0=ALU.subtract, op1=ALU.mult, sem="dv")

        lnb4tok = {}
        ev_tok = {}

        def ln2t_build(gi2, ccs):
            ln2Tx = ln2Tb[gi2 % 2]
            for cc in ccs:
                PTx = (PT0, PT1, PT2)[cc % 3]
                g.need("tensor", ("PT", cc % 3))
                tok_tr = None
                for tj in range(4):
                    tok_tr = g.im("tensor", "transpose",
                                  PTx[:, 128 * tj:128 * tj + 128],
                                  lnb[:, 1024 * tj + 128 * cc:
                                      1024 * tj + 128 * cc + 128],
                                  eyeb[:, :], sem="pe")
                dst = ln2Tx[:, 512 * cc:512 * (cc + 1)]
                g.need("vector" if cc % 2 == 0 else "scalar",
                       ("ln2T", gi2 % 2))
                if cc % 2 == 0:
                    g.w("vector", "pe", tok_tr)
                    tok_e = g.im("vector", "tensor_copy", out=dst,
                                 in_=PTx[:, 0:512], sem="dv")
                    g.done(("PT", cc % 3), "dv", tok_e)
                else:
                    g.w("scalar", "pe", tok_tr)
                    tok_e = g.im("scalar", "copy", out=dst,
                                 in_=PTx[:, 0:512], sem="ac")
                    g.done(("PT", cc % 3), "ac", tok_e)
            if ccs[-1] == CCH - 1:
                g.done("lnb", "pe", g.semv["pe"])
                ev_tok[gi2] = (g.mark("vector", "dv"), g.mark("scalar", "ac"))

        for tl in range(4):
            lnb4tok[0] = ln_write(0, tl)
        g.w("tensor", "dv", lnb4tok[0])
        ln2t_build(0, list(range(CCH)))
        for gi in range(8):
            tok_lv, tok_ls = ev_tok[gi]
            g.w("tensor", "dv", tok_lv)
            g.w("tensor", "ac", tok_ls)
            ln2T = ln2Tb[gi % 2]
            last_r = None
            for dc in range(CCH):
                if gi + 1 < 8 and 1 <= dc <= 4:
                    lnb4tok[gi + 1] = ln_write(gi + 1, dc - 1)
                    if dc == 4:
                        g.w("tensor", "dv", lnb4tok[gi + 1])
                if gi + 1 < 8 and 5 <= dc <= 7:
                    ln2t_build(gi + 1,
                               [[0, 1, 2], [3, 4, 5], [6, 7]][dc - 5])
                bank = (P0, P1)[dc % 2]
                g.need("tensor", ("PVa", dc % 2))
                g.need("tensor", ("PVv", dc % 2))
                last = None
                for k in range(CCH):
                    last = g.im("tensor", "matmul", bank[:],
                                wvTs[:, C * dc + 128 * k:C * dc + 128 * (k + 1)],
                                ln2T[:, 512 * k:512 * (k + 1)],
                                start=(k == 0), stop=(k == CCH - 1), sem="pe")
                g.need("scalar", ("vgrp", dc % 2))
                g.w("scalar", "pe", last)
                tok_ev = g.im("scalar", "copy",
                              out=vgrp[:, 512 * (dc % 2):512 * (dc % 2 + 1)],
                              in_=bank[:], sem="ac")
                g.done(("PVa", dc % 2), "ac", tok_ev)
                g.need("vector", "bvsD")
                g.w("vector", "pe", last)
                g.w("vector", "ac", tok_ev)
                tok_bv = g.im("vector", "tensor_scalar", out=bvsD[:],
                              in0=bank[:], scalar1=gbt[:, dc:dc + 1],
                              scalar2=None, op0=ALU.mult, sem="dv")
                g.done(("PVv", dc % 2), "dv", tok_bv)
                tok_ts1 = g.im("vector", "tensor_tensor_scan", out=zjunk[:],
                               data0=alph[:, dc:dc + 1].broadcast_to([128, 512]),
                               data1=bvsD[:], initial=Epart[:, dc:dc + 1],
                               op0=ALU.mult, op1=ALU.add, sem="dv")
                g.w("vector", "dv", tok_ts1)
                tok_sc = g.im("vector", "tensor_copy",
                              out=Epart[:, dc:dc + 1],
                              in_=zjunk[:, 511:512], sem="dv")
                g.done("bvsD", "dv", tok_sc)
                bank2 = (P2, P3)[dc % 2]
                g.need("tensor", ("PR", dc % 2))
                last_r = None
                for k in range(CCH):
                    last_r = g.im("tensor", "matmul", bank2[:],
                                  wrTs[:, C * dc + 128 * k:C * dc + 128 * (k + 1)],
                                  ln2T[:, 512 * k:512 * (k + 1)],
                                  start=(k == 0), stop=(k == CCH - 1), sem="pe")
                g.need("scalar", ("rgrp", dc % 2))
                g.w("scalar", "pe", last_r)
                tok_r = g.im("scalar", "activation",
                             out=rgrp[:, 512 * (dc % 2):512 * (dc % 2 + 1)],
                             in_=bank2[:], func=AF.Sigmoid, sem="ac")
                g.done(("PR", dc % 2), "ac", tok_r)
                g.w("sync", "ac", tok_r)
                sh = "dh%d" % (dc % 2)
                g.dma("sync", vsp[dc][:, 512 * gi:512 * (gi + 1)],
                      vgrp[:, 512 * (dc % 2):512 * (dc % 2 + 1)], sem=sh)
                g.dma("sync", rsp[dc][:, 512 * gi:512 * (gi + 1)],
                      rgrp[:, 512 * (dc % 2):512 * (dc % 2 + 1)], sem=sh)
                g.done(("vgrp", dc % 2), sh, g.semv[sh])
                g.done(("rgrp", dc % 2), sh, g.semv[sh])
            g.done(("ln2T", gi % 2), "pe", last_r)
            if gi >= 1:
                bb_batch(gi - 1)

        if stage == "dloc":
            g.barrier()
            for t in range(NT):
                g.dma("sync", out_d[128 * t:128 * (t + 1), :],
                      xres[:, C * t:C * (t + 1)])
            g.w("sync", "ds")
            blk = es.enter_context(nc.Block())
            g.finish(blk)
            return nc

        bb_batch(7)

        tok_ep = g.mark("vector", "dv")
        g.w("tensor", "dv", tok_ep)
        g.need("tensor", "PS")
        g.i("tensor", "transpose", PS[0:8, 0:128], Epart[:, :], eye32[:, :])
        tok_tr = g.im("tensor", "transpose", PS[0:8, 128:256], a4096[:, :],
                      eye32[:, :], sem="pe")
        g.w("scalar", "pe", tok_tr)
        g.i("scalar", "copy", out=e_t8[:, :], in_=PS[0:8, 0:128])
        tok_e8 = g.im("scalar", "copy", out=e_a8[:, :], in_=PS[0:8, 128:256],
                      sem="ac")
        g.done("PS", "ac", tok_e8)
        g.w("gpsimd", "ac", tok_e8)
        g.dma("gpsimd", ag_i, e_t8[:, :])
        g.w("gpsimd", "dg")
        tok_ag = g.im("gpsimd", "collective_compute", "AllGather", ALU.bypass,
                      replica_groups=[[0, 1, 2, 3], [4, 5, 6, 7]],
                      ins=[ag_i], outs=[ag_o], sem="gp")
        g.w("gpsimd", "gp", tok_ag)
        g.dma("gpsimd", e_p0[:, :], ag_o[0:8, :])
        g.dma("gpsimd", e_p1[:, :], ag_o[8:16, :])
        g.dma("gpsimd", e_p2[:, :], ag_o[16:24, :])
        g.w("vector", "dg", g.semv["dg"])
        g.i("vector", "tensor_mul", out=e_c2[:], in0=e_a8[:], in1=e_p0[:])
        g.i("vector", "tensor_add", out=e_c2[:], in0=e_c2[:], in1=e_p1[:])
        g.i("vector", "tensor_mul", out=e_c3[:], in0=e_a8[:], in1=e_c2[:])
        g.i("vector", "tensor_add", out=e_c3[:], in0=e_c3[:], in1=e_p2[:])
        g.i("vector", "tensor_scalar", out=e_s1[:], in0=e_p0[:],
            scalar1=oh_c[0:8, 1:2], scalar2=None, op0=ALU.mult)
        g.i("vector", "scalar_tensor_tensor", out=e_s2[:], in0=e_c2[:],
            scalar=oh_c[0:8, 2:3], in1=e_s1[:], op0=ALU.mult, op1=ALU.add)
        tok_sel = g.im("vector", "scalar_tensor_tensor", out=e_s1[:],
                       in0=e_c3[:], scalar=oh_c[0:8, 3:4], in1=e_s2[:],
                       op0=ALU.mult, op1=ALU.add, sem="dv")
        g.w("tensor", "dv", tok_sel)
        tok_trb = g.im("tensor", "transpose", PS[:, 0:8], e_s1[:, :],
                       eye32[0:8, 0:8], sem="pe")
        g.w("scalar", "pe", tok_trb)
        tok_kap = g.im("scalar", "copy", out=kap[:, :], in_=PS[:, 0:8],
                       sem="ac")
        g.done("PS", "ac", tok_kap)
        g.barrier()
        sD.close()

        if stage == "dag":
            for t in range(NT):
                g.dma("sync", out_d[128 * t:128 * (t + 1), :],
                      xres[:, C * t:C * (t + 1)])
            g.w("sync", "ds")
            blk = es.enter_context(nc.Block())
            g.finish(blk)
            return nc

        sE = es.enter_context(contextlib.ExitStack())
        woTs = g.sbuf("woTs", [128, CCH * C], BF16, sE)
        vld = g.sbuf("vld", [128, 2 * CCH * 512], BF16, sE)
        rld = g.sbuf("rld", [128, CCH * 512], BF16, sE)
        rdld = g.sbuf("rdld", [128, CCH * 512], BF16, sE)
        rwg = g.sbuf("rwg", [128, CCH * 512], BF16, sE)
        aab = g.sbuf("aab", [128, 2 * 520], F32, sE)
        bvsE = g.sbuf("bvsE", [128, 2 * 512], F32, sE)
        numsE = g.sbuf("numsE", [128, 512], F32, sE)
        lnsE = g.sbuf("lnsE", [128, 512], F32, sE)
        wkvE = g.sbuf("wkvE", [128, 512], F32, sE)

        for dc in range(CCH):
            g.dma("sync", woTs[:, C * dc:C * (dc + 1)], wob[dc], sem="dm0")
        for gi in range(2):
            g.dma("sync", vld[:, CCH * 512 * gi:CCH * 512 * (gi + 1)],
                  vsp.rearrange("dc p j -> p dc j")
                  [:, :, 512 * gi:512 * (gi + 1)], sem="dc%d" % (gi % 2))
        g.dma("sync", rld[:, :],
              rsp.rearrange("dc p j -> p dc j")[:, :, 0:512], sem="dcr")
        g.dma("sync", rdld[:, :],
              rdsp.rearrange("dc p j -> p dc j")[:, :, 0:512], sem="dcr")
        g.w("tensor", "dm0")
        g.w("tensor", "dc0")
        g.w("tensor", "dc1")
        g.w("tensor", "dcr")

        rec_h1 = {}
        rec_h2 = {}
        for gi in range(8):
            slot = gi % 2
            vb = slot * CCH * 512
            g.w("vector", "dc%d" % (gi % 2))
            g.w("scalar", "dc%d" % (gi % 2))
            g.w("vector", "dcr", rec_h1.get(gi) or g.semv["dcr"])
            g.w("scalar", "dcr", rec_h1.get(gi) or g.semv["dcr"])
            for dc in range(CCH):
                dpar = dc % 2
                if dc == 4:
                    v2 = rec_h2.get(gi) or g.semv["dm1"]
                    g.w("vector", "dm1", v2)
                    g.w("scalar", "dm1", v2)
                if dc == 4 and gi + 1 < 8:
                    g.w("sync", "dv")
                    g.dma("sync", rld[:, 0:4 * 512],
                          rsp.rearrange("dc p j -> p dc j")
                          [:, 0:4, 512 * (gi + 1):512 * (gi + 2)], sem="dcr")
                    g.dma("sync", rdld[:, 0:4 * 512],
                          rdsp.rearrange("dc p j -> p dc j")
                          [:, 0:4, 512 * (gi + 1):512 * (gi + 2)], sem="dcr")
                    rec_h1[gi + 1] = g.semv["dcr"]
                vsl = vld[:, vb + 512 * dc:vb + 512 * (dc + 1)]
                rsl = rld[:, 512 * dc:512 * (dc + 1)]
                g.need("scalar", ("bvsE", dpar))
                tok_bv = g.im("scalar", "activation",
                              out=bvsE[:, 512 * dpar:512 * (dpar + 1)],
                              in_=vsl, func=AF.Identity,
                              scale=gbt[:, dc:dc + 1], sem="ac")
                ab = aab[:, 520 * dpar:520 * dpar + 520]
                g.need("vector", ("aab", dpar))
                g.i("vector", "tensor_copy", out=ab[:, 0:1],
                    in_=kap[:, dc:dc + 1])
                g.w("vector", "ac", tok_bv)
                tok_ta = g.im("vector", "tensor_tensor_scan", out=ab[:, 1:513],
                              data0=alph[:, dc:dc + 1].broadcast_to([128, 512]),
                              data1=bvsE[:, 512 * dpar:512 * (dpar + 1)],
                              initial=kap[:, dc:dc + 1], op0=ALU.mult,
                              op1=ALU.add, sem="dv")
                g.done(("bvsE", dpar), "dv", tok_ta)
                g.w("vector", "dv", tok_ta)
                g.i("vector", "tensor_copy", out=kap[:, dc:dc + 1],
                    in_=ab[:, 512:513])
                g.need("vector", "numsE")
                tok_nm = g.im("vector", "scalar_tensor_tensor",
                              out=numsE[:],
                              in0=vsl, scalar=delt[:, dc:dc + 1],
                              in1=ab[:, 0:512], op0=ALU.mult, op1=ALU.add,
                              sem="dv")
                g.done(("aab", dpar), "dv", tok_nm)
                g.need("vector", "wkvE")
                g.i("vector", "tensor_mul", out=wkvE[:],
                    in0=numsE[:],
                    in1=rdld[:, 512 * dc:512 * (dc + 1)])
                g.done("numsE", "dv", g.semv["dv"])
                g.need("vector", ("rwg", dc))
                tok_rw = g.im("vector", "tensor_mul",
                              out=rwg[:, 512 * dc:512 * (dc + 1)],
                              in0=rsl, in1=wkvE[:], sem="dv")
                g.done("wkvE", "dv", tok_rw)
            tok_rwall = g.semv["dv"]
            g.done(("vld", slot), "dv", tok_rwall)
            if gi + 2 < 8:
                g.need("sync", ("vld", slot))
                g.dma("sync", vld[:, vb:vb + CCH * 512],
                      vsp.rearrange("dc p j -> p dc j")
                      [:, :, 512 * (gi + 2):512 * (gi + 3)],
                      sem="dc%d" % (gi % 2))
            if gi + 1 < 8:
                g.w("sync", "dv", tok_rwall)
                g.dma("sync", rld[:, 4 * 512:CCH * 512],
                      rsp.rearrange("dc p j -> p dc j")
                      [:, 4:CCH, 512 * (gi + 1):512 * (gi + 2)], sem="dm1")
                g.dma("sync", rdld[:, 4 * 512:CCH * 512],
                      rdsp.rearrange("dc p j -> p dc j")
                      [:, 4:CCH, 512 * (gi + 1):512 * (gi + 2)], sem="dm1")
                rec_h2[gi + 1] = g.semv["dm1"]
            g.w("tensor", "dv", tok_rwall)
            for tc in range(4):
                t = 4 * gi + tc
                bA, bB = ((P0, P1), (P2, P3))[tc % 2]
                g.need("tensor", ("PO", tc % 2))
                last = None
                for dc in range(CCH):
                    stat = rwg[:, 512 * dc + 128 * tc:512 * dc + 128 * tc + 128]
                    g.im("tensor", "matmul", bA[:], stat,
                         woTs[:, C * dc:C * dc + 512],
                         start=(dc == 0), stop=(dc == CCH - 1), sem="pe")
                    last = g.im("tensor", "matmul", bB[:], stat,
                                woTs[:, C * dc + 512:C * dc + 1024],
                                start=(dc == 0), stop=(dc == CCH - 1),
                                sem="pe")
                xsl = xres[:, C * t:C * (t + 1)]
                g.w("vector", "pe", last)
                g.i("vector", "tensor_tensor", out=xsl[:, 0:512],
                    in0=xsl[:, 0:512], in1=bA[:], op=ALU.add)
                tok_add = g.im("vector", "tensor_tensor", out=xsl[:, 512:1024],
                               in0=xsl[:, 512:1024], in1=bB[:], op=ALU.add,
                               sem="dv")
                g.done(("PO", tc % 2), "dv", tok_add)
                g.i("vector", "reduce_max", out=st_max[:, t:t + 1], in_=xsl,
                    axis=AX.X)
                g.i("vector", "tensor_reduce", out=st_min[:, t:t + 1], in_=xsl,
                    axis=AX.X, op=ALU.min)
                g.w("scalar", "dv", tok_add)
                g.i("scalar", "activation", out=lnsE[:], in_=xsl[:, 0:512],
                    func=AF.Identity, accum_out=st_sum[:, t:t + 1])
                g.i("scalar", "activation", out=lnsE[:], in_=xsl[:, 512:1024],
                    func=AF.Identity, accum_out=m_b[:, t:t + 1])
                g.i("scalar", "activation", out=lnsE[:], in_=xsl[:, 0:512],
                    func=AF.Square, accum_out=st_sq[:, t:t + 1])
                g.im("scalar", "activation", out=lnsE[:],
                     in_=xsl[:, 512:1024], func=AF.Square,
                     accum_out=stt_sc[:, t:t + 1], sem="ac")
            for dc in range(CCH):
                g.done(("rwg", dc), "pe", g.semv["pe"])

        tok_3v = g.mark("vector", "dv")
        tok_3s = g.mark("scalar", "ac")
        g.w("vector", "dv", tok_3v)
        g.w("vector", "ac", tok_3s)
        g.i("vector", "tensor_add", out=st_sum[:], in0=st_sum[:], in1=m_b[:])
        g.i("vector", "tensor_add", out=st_sq[:], in0=st_sq[:], in1=stt_sc[:])
        t_r3, t_am3 = ln_finalize(m_a, r_a, True)

        if stage == "attn":
            g.barrier()
            for t in range(NT):
                g.dma("sync", out_d[128 * t:128 * (t + 1), :],
                      xres[:, C * t:C * (t + 1)])
            g.w("sync", "ds")
            blk = es.enter_context(nc.Block())
            g.finish(blk)
            return nc

        tok_sx2 = global_max_scale(t_am3, 0)
        g.w("vector", "ac", tok_sx2)
        g.i("vector", "tensor_scalar", out=eff_g[:], in0=s_wg[:],
            scalar1=scb[:, 0:1], scalar2=None, op0=ALU.mult)
        g.im("vector", "tensor_scalar", out=eff_u[:], in0=s_wu[:],
             scalar1=scb[:, 0:1], scalar2=None, op0=ALU.mult, sem="dv")
        g.barrier()
        sE.close()
        ffn_run(m_a, r_a, True)
        g.w("sync", "ds")
        g.w("gpsimd", "dg")
        blk = es.enter_context(nc.Block())
        g.finish(blk)
        return nc


_NC_CACHE = {}


def kernel(L0_attn_x, L0_attn_time_decay, L0_attn_time_first, L0_attn_key_w,
           L0_attn_value_w, L0_attn_recep_w, L0_attn_out_w, L0_ffn_clip_x,
           L0_ffn_clip_down, L0_ffn_w_gate, L0_ffn_w_up, L0_ffn_w_down):
    x = np.ascontiguousarray(np.asarray(L0_attn_x, dtype=np.float32))
    td = np.ascontiguousarray(
        np.asarray(L0_attn_time_decay, np.float32).reshape(CCH, 128))
    tf = np.ascontiguousarray(
        np.asarray(L0_attn_time_first, np.float32).reshape(CCH, 128))
    wv = np.ascontiguousarray(np.asarray(L0_attn_value_w, np.float32))
    wr = np.ascontiguousarray(np.asarray(L0_attn_recep_w, np.float32))
    wo = np.ascontiguousarray(np.asarray(L0_attn_out_w, np.float32))
    wg = np.ascontiguousarray(np.asarray(L0_ffn_w_gate, np.float32))
    wu = np.ascontiguousarray(np.asarray(L0_ffn_w_up, np.float32))
    wd = np.ascontiguousarray(np.asarray(L0_ffn_w_down, np.float32))
    clips = np.ascontiguousarray(
        np.stack([np.asarray(L0_ffn_clip_x, np.float32).reshape(1),
                  np.asarray(L0_ffn_clip_down, np.float32).reshape(1)],
                 axis=1))
    eye = np.eye(128, dtype=np.float32)

    if STAGE not in _NC_CACHE:
        _NC_CACHE[STAGE] = build_program(STAGE)
    nc = _NC_CACHE[STAGE]

    in_maps = []
    for r in range(8):
        b, q = r // 4, r % 4
        oh = np.zeros((4, 1), np.float32)
        oh[q, 0] = 1.0
        in_maps.append({
            "x": np.ascontiguousarray(x[b, TOK * q:TOK * (q + 1), :]),
            "w_gate": wg, "w_up": wu, "w_down": wd,
            "w_val": wv, "w_rec": wr, "w_out": wo,
            "tdec": td, "tfir": tf, "clips": clips,
            "eye": eye, "onehot": oh,
        })
    res = run_bass_kernel_spmd(nc, in_maps, core_ids=list(range(8)))
    out = np.empty((B, T, C), np.float32)
    for r in range(8):
        b, q = r // 4, r % 4
        out[b, TOK * q:TOK * (q + 1), :] = res.results[r]["out"]
    return out
